# version 05
# baseline (speedup 1.0000x reference)
import numpy as np
from contextlib import ExitStack

import concourse.bass as bass
import concourse.mybir as mybir
from concourse.bass_utils import run_bass_kernel_spmd

F32 = mybir.dt.float32
BF16 = mybir.dt.bfloat16
I32 = mybir.dt.int32
U8C = mybir.dt.uint8
AF = mybir.ActivationFunctionType
ALU = mybir.AluOpType
AX = mybir.AxisListType

D = 2048
L = 2048
NCORES = 8


class _Sem:
    __slots__ = ("name", "h", "count")
    def __init__(self, name):
        self.name = name
        self.h = None
        self.count = 0


class _Op:
    __slots__ = ("idx", "stream", "emit", "deps", "is_dma", "sem", "inc", "cum", "tag")


def _ap_interval(ap):
    t = ap.tensor
    lo = ap.offset
    hi = ap.offset
    for step, count in ap.ap:
        hi += step * (count - 1)
    return t.name, min(lo, hi), max(lo, hi) + 1


class Sched:
    STREAMS = ("pe", "ve", "sc", "gp", "sy")

    def __init__(self, nc, psum_names=()):
        self.nc = nc
        self.ops = []
        self.sems = {}
        self.psum_names = set(psum_names)
        self.mem = {}

    def dsem(self, name):
        if name not in self.sems:
            self.sems[name] = _Sem(name)
        return name

    def _access(self, ap, op, is_write, deps):
        name, lo, hi = _ap_interval(ap)
        if name in self.psum_names:
            lo, hi = 0, 1 << 40
        rec = self.mem.setdefault(name, {"w": [], "r": []})
        if is_write:
            for (l2, h2, o2) in rec["w"]:
                if l2 < hi and lo < h2:
                    deps.append(o2)
            for (l2, h2, o2) in rec["r"]:
                if l2 < hi and lo < h2:
                    deps.append(o2)
            rec["w"] = [(l2, h2, o2) for (l2, h2, o2) in rec["w"]
                        if not (lo <= l2 and h2 <= hi)]
            rec["r"] = [(l2, h2, o2) for (l2, h2, o2) in rec["r"]
                        if not (l2 < hi and lo < h2)]
            rec["w"].append((lo, hi, op))
        else:
            for (l2, h2, o2) in rec["w"]:
                if l2 < hi and lo < h2:
                    deps.append(o2)
            rec["r"].append((lo, hi, op))

    def add(self, stream, emit, reads=(), writes=(), deps=(), dma=None, tag=""):
        op = _Op()
        op.idx = len(self.ops)
        op.stream = stream
        op.emit = emit
        op.is_dma = dma is not None
        op.sem = self.dsem(dma) if dma is not None else None
        op.inc = False
        op.cum = None
        op.tag = tag
        dd = [d for d in deps if d is not None]
        for ap in reads:
            self._access(ap, op, False, dd)
        for ap in writes:
            self._access(ap, op, True, dd)
        seen = set()
        op.deps = []
        for d in dd:
            if d is not op and id(d) not in seen:
                seen.add(id(d))
                assert d.idx < op.idx, f"forward dep at op {op.idx} ({tag})"
                op.deps.append(d)
        self.ops.append(op)
        return op

    def finalize(self, ctx):
        nc = self.nc
        for op in self.ops:
            for d in op.deps:
                if d.is_dma:
                    continue
                if d.stream == "pe" and op.stream == "pe" and not op.is_dma:
                    continue
                d.inc = True
        for op in self.ops:
            if op.is_dma:
                s = self.sems[op.sem]
                s.count += 16
                op.cum = s.count
            elif op.inc:
                sname = "e_" + op.stream
                op.sem = self.dsem(sname)
                s = self.sems[sname]
                s.count += 1
                op.cum = s.count
        for s in self.sems.values():
            s.h = ctx.enter_context(nc.semaphore(s.name))
        per = {st: [op for op in self.ops if op.stream == st] for st in self.STREAMS}
        block = ctx.enter_context(nc.Block())
        waited = {st: {} for st in self.STREAMS}

        def run_stream(st, eng):
            w = waited[st]
            for op in per[st]:
                for d in op.deps:
                    if d.cum is None:
                        continue
                    if (not d.is_dma) and d.stream == "pe" and st == "pe" and not op.is_dma:
                        continue
                    if w.get(d.sem, 0) < d.cum:
                        eng.wait_ge(self.sems[d.sem].h, d.cum)
                        w[d.sem] = d.cum
                inst = op.emit(eng)
                if op.is_dma:
                    inst.then_inc(self.sems[op.sem].h, 16)
                elif op.inc:
                    inst.then_inc(self.sems[op.sem].h, 1)
            if st == "sy":
                for s in self.sems.values():
                    if not s.name.startswith("e_") and s.count > 0:
                        if w.get(s.name, 0) < s.count:
                            eng.wait_ge(s.h, s.count)
                            w[s.name] = s.count

        for st, attr in (("pe", "tensor"), ("ve", "vector"), ("sc", "scalar"),
                         ("gp", "gpsimd"), ("sy", "sync")):
            if per[st] or st == "sy":
                getattr(block, attr)(lambda eng, _st=st: run_stream(_st, eng))
        return len(self.ops)


class Arena:
    def __init__(self, nc, ctx):
        self.nc = nc
        self.ctx = ctx
        self.t = {}

    def sb(self, name, shape, dtype=F32):
        h = self.ctx.enter_context(self.nc.sbuf_tensor(name, shape, dtype))
        self.t[name] = h
        return h

    def ps(self, name, shape, dtype=F32):
        h = self.ctx.enter_context(self.nc.psum_tensor(name, shape, dtype))
        self.t[name] = h
        return h



FFN_TG = 256
FFN_NG = L // FFN_TG
FFN_NI = 64


def alloc_shared(A):
    A.sb("conesb", [128, 128], BF16)
    A.sb("xg", [128, 16 * 512])
    A.sb("xn", [128, 16 * 512])
    A.sb("slab", [128, 64 * 256])
    for b in range(2):
        A.sb(f"wgb{b}", [128, 2048])
        A.sb(f"wub{b}", [128, 2048])
        A.sb(f"wdb{b}", [128, 2048])
    A.sb("rowsA", [1, 512])
    A.sb("rowsB", [1, 512])
    A.sb("rowsC", [1, 512])
    A.sb("rowsD", [1, 512])
    A.sb("epsb", [1, 1])
    A.sb("tscr", [128, 2 * 512])
    for b in range(2):
        A.sb(f"outr{b}", [128, 512])


def alloc_ffn0(A):
    A.sb("sgt", [128, 64])
    A.sb("sut", [128, 64])
    A.sb("sdt", [128, 16])


def emit_consts(S, A):
    eps = A.t["epsb"].ap()
    S.add("ve", lambda e: e.memset(eps, 1e-5), writes=[eps], tag="eps")


_ROWMAP = {0: "rowsA", 1: "rowsD", 2: "rowsC", 3: "rowsD", 4: "rowsC",
           5: "rowsB", 6: "rowsC"}


def row_slicer(A, TG):
    def R(i):
        return A.t[_ROWMAP[i]].ap()[0:1, 0:TG]
    return R


def emit_ln(S, A, P, xg, xn, TG, tag, split_hi=None, split_lo=None):
    t = A.t
    cones = t["cones"].ap()
    eps = t["epsb"].ap()
    sq = t["tscr"].ap()
    tscr = t["tscr"].ap()
    R = row_slicer(A, TG)

    p_sx = P[6].ap()[0:1, 0:TG]
    p_sq = P[7].ap()[0:1, 0:TG]
    p_mb = P[4].ap()[:, 0:TG]
    p_rb = P[5].ap()[:, 0:TG]

    for c in range(16):
        xgc = xg[:, c * TG:(c + 1) * TG]
        sqs = sq[:, (c % 2) * TG:(c % 2 + 1) * TG]
        S.add("pe", lambda e, xgc=xgc, c=c: e.matmul(
            p_sx, cones[:, 0:1], xgc, start=(c == 0), stop=(c == 15)),
            reads=[cones[:, 0:1], xgc], writes=[p_sx], tag=f"{tag}sx{c}")
        S.add("sc", lambda e, xgc=xgc, sqs=sqs: e.activation(
            out=sqs, in_=xgc, func=AF.Square),
            reads=[xgc], writes=[sqs], tag=f"{tag}sq{c}")
        S.add("pe", lambda e, sqs=sqs, c=c: e.matmul(
            p_sq, cones[:, 0:1], sqs, start=(c == 0), stop=(c == 15)),
            reads=[cones[:, 0:1], sqs], writes=[p_sq], tag=f"{tag}sqm{c}")
    S.add("ve", lambda e: e.tensor_scalar(R(0), p_sx, -1.0 / D, None, op0=ALU.mult),
          reads=[p_sx], writes=[R(0)], tag=f"{tag}negm")
    S.add("ve", lambda e: e.tensor_scalar(R(1), p_sq, 1.0 / D, None, op0=ALU.mult),
          reads=[p_sq], writes=[R(1)], tag=f"{tag}ex2")
    S.add("ve", lambda e: e.tensor_tensor(R(2), R(0), R(0), op=ALU.mult),
          reads=[R(0)], writes=[R(2)], tag=f"{tag}m2")
    S.add("ve", lambda e: e.tensor_tensor(R(3), R(1), R(2), op=ALU.subtract),
          reads=[R(1), R(2)], writes=[R(3)], tag=f"{tag}var")
    S.add("sc", lambda e: e.activation(out=R(4), in_=R(3), func=AF.Sqrt, bias=eps),
          reads=[R(3), eps], writes=[R(4)], tag=f"{tag}std")
    S.add("ve", lambda e: e.reciprocal(R(5), R(4)),
          reads=[R(4)], writes=[R(5)], tag=f"{tag}r")
    S.add("pe", lambda e: e.matmul(p_mb, cones[0:1, :], R(0), start=True, stop=True),
          reads=[cones[0:1, :], R(0)], writes=[p_mb], tag=f"{tag}mb")
    S.add("pe", lambda e: e.matmul(p_rb, cones[0:1, :], R(5), start=True, stop=True),
          reads=[cones[0:1, :], R(5)], writes=[p_rb], tag=f"{tag}rb")
    for c in range(16):
        xgc = xg[:, c * TG:(c + 1) * TG]
        tsl = tscr[:, (c % 2) * TG:(c % 2 + 1) * TG]
        if split_hi is None:
            xnc = xn[:, c * TG:(c + 1) * TG]
        else:
            xnc = tscr[:, (2 + c % 2) * TG:(3 + c % 2) * TG]
        S.add("ve", lambda e, xgc=xgc, tsl=tsl: e.tensor_tensor(
            tsl, xgc, p_mb, op=ALU.add),
            reads=[xgc, p_mb], writes=[tsl], tag=f"{tag}xna{c}")
        S.add("ve", lambda e, xnc=xnc, tsl=tsl: e.tensor_tensor(
            xnc, tsl, p_rb, op=ALU.mult),
            reads=[tsl, p_rb], writes=[xnc], tag=f"{tag}xnb{c}")
        if split_hi is not None:
            hic = split_hi[:, c * TG:(c + 1) * TG]
            loc = split_lo[:, c * TG:(c + 1) * TG]
            S.add("sc", lambda e, xnc=xnc, hic=hic: e.copy(out=hic, in_=xnc),
                  reads=[xnc], writes=[hic], tag=f"{tag}xh{c}")
            S.add("ve", lambda e, xnc=xnc, hic=hic, loc=loc: e.tensor_tensor(
                loc, xnc, hic, op=ALU.subtract),
                reads=[xnc, hic], writes=[loc], tag=f"{tag}xl{c}")


def emit_ffn0(S, A, P, W, x_in, x_out, uid):
    t = A.t
    TG, NG, NI = FFN_TG, FFN_NG, FFN_NI
    xg, xn = t["xg"].ap(), t["xn"].ap()
    xnb = xn.bitcast(BF16)
    xnh = xnb[:, 0:16 * TG]
    xnl = xnb[:, 16 * TG:32 * TG]
    slabb = t["slab"].ap().bitcast(BF16)
    hhh = slabb[:, 0:NI * TG]
    hhl = slabb[:, NI * TG:2 * NI * TG]
    conesb = t["conesb"].ap()
    cones = t["cones"].ap()
    sgt, sut, sdt = t["sgt"].ap(), t["sut"].ap(), t["sdt"].ap()
    gs = xn[:, 4096:4096 + TG]
    gsil = xn[:, 4352:4352 + TG]
    us = xn[:, 4608:4608 + TG]
    tscr = t["tscr"].ap()
    R = row_slicer(A, TG)

    x_in3 = x_in.rearrange("(c p) t -> p c t", p=128)
    x_out3 = x_out.rearrange("(c p) t -> p c t", p=128)

    S.add("sy", lambda e: e.dma_start(out=sgt, in_=W["sg"]), writes=[sgt], dma=f"{uid}p0")
    S.add("sy", lambda e: e.dma_start(out=sut, in_=W["su"]), writes=[sut], dma=f"{uid}p1")
    S.add("sy", lambda e: e.dma_start(out=sdt, in_=W["sd"]), writes=[sdt], dma=f"{uid}p2")

    p_g = [P[0].ap()[:, 0:TG], P[1].ap()[:, 0:TG]]
    p_u = [P[2].ap()[:, 0:TG], P[3].ap()[:, 0:TG]]
    p_mb = P[4].ap()[:, 0:TG]
    p_rb = P[5].ap()[:, 0:TG]
    p_d = [P[4].ap()[:, 0:TG], P[5].ap()[:, 0:TG]]
    p_sx = P[6].ap()[0:1, 0:TG]
    p_cs = P[6].ap()[0:1, 256:256 + TG]
    p_sq = P[7].ap()[0:1, 0:TG]

    for g in range(NG):
        S.add("sy", lambda e, g=g: e.dma_start(
            out=xg[:, 0:16 * TG].rearrange("p (c t) -> p c t", t=TG),
            in_=x_in3[:, :, g * TG:(g + 1) * TG]),
            reads=[x_in3[:, :, g * TG:(g + 1) * TG]], writes=[xg[:, 0:16 * TG]],
            dma=f"{uid}xg", tag=f"ldx{g}")
        emit_ln(S, A, P, xg, xn, TG, tag=f"{uid}ln{g}", split_hi=xnh, split_lo=xnl)
        def ldcast(i):
            b = i % 2
            wgb = t[f"wgb{b}"].ap().bitcast(BF16)[:, 0:2048]
            wub = t[f"wub{b}"].ap().bitcast(BF16)[:, 0:2048]
            stg = t[f"wgb{b}"].ap().bitcast(U8C)[:, 4096:8192]
            S.add("sy", lambda e, stg=stg, i=i: e.dma_start(out=stg, in_=W["wgm"][i]),
                  reads=[W["wgm"][i]], writes=[stg], dma=f"{uid}wg{b}", tag=f"ldwg{g}.{i}")
            S.add("ve", lambda e, wgb=wgb, stg=stg: e.tensor_copy(wgb, stg[:, 0:2048]),
                  reads=[stg[:, 0:2048]], writes=[wgb], tag=f"cvg{g}.{i}")
            S.add("sc", lambda e, wub=wub, stg=stg: e.copy(out=wub, in_=stg[:, 2048:4096]),
                  reads=[stg[:, 2048:4096]], writes=[wub], tag=f"cvu{g}.{i}")

        def gumms(i):
            b = i % 2
            wgb = t[f"wgb{b}"].ap().bitcast(BF16)[:, 0:2048]
            wub = t[f"wub{b}"].ap().bitcast(BF16)[:, 0:2048]
            for hl, xsrc in ((0, xnh), (1, xnl)):
                for c in range(16):
                    xnc = xsrc[:, c * TG:(c + 1) * TG]
                    ws = wgb[:, c * 128:(c + 1) * 128]
                    S.add("pe", lambda e, ws=ws, xnc=xnc, c=c, b=b, hl=hl: e.matmul(
                        p_g[b], ws, xnc, start=(hl == 0 and c == 0), stop=(hl == 1 and c == 15)),
                        reads=[ws, xnc], writes=[p_g[b]], tag=f"mmg{g}.{i}.{hl}.{c}")
            for hl, xsrc in ((0, xnh), (1, xnl)):
                for c in range(16):
                    xnc = xsrc[:, c * TG:(c + 1) * TG]
                    ws = wub[:, c * 128:(c + 1) * 128]
                    S.add("pe", lambda e, ws=ws, xnc=xnc, c=c, b=b, hl=hl: e.matmul(
                        p_u[b], ws, xnc, start=(hl == 0 and c == 0), stop=(hl == 1 and c == 15)),
                        reads=[ws, xnc], writes=[p_u[b]], tag=f"mmu{g}.{i}.{hl}.{c}")

        def guevac(i):
            b = i % 2
            S.add("ve", lambda e, i=i, b=b: e.tensor_scalar(
                gs, p_g[b], sgt[:, i:i + 1], None, op0=ALU.mult),
                reads=[p_g[b], sgt[:, i:i + 1]], writes=[gs], tag=f"gs{g}.{i}")
            S.add("sc", lambda e: e.activation(out=gsil, in_=gs, func=AF.Silu),
                  reads=[gs], writes=[gsil], tag=f"sil{g}.{i}")
            S.add("ve", lambda e, i=i, b=b: e.tensor_scalar(
                us, p_u[b], sut[:, i:i + 1], None, op0=ALU.mult),
                reads=[p_u[b], sut[:, i:i + 1]], writes=[us], tag=f"us{g}.{i}")
            S.add("ve", lambda e: e.tensor_tensor(gs, gsil, us, op=ALU.mult),
                  reads=[gsil, us], writes=[gs], tag=f"h{g}.{i}")
            hih = hhh[:, i * TG:(i + 1) * TG]
            hil = hhl[:, i * TG:(i + 1) * TG]
            S.add("sc", lambda e, hih=hih: e.copy(out=hih, in_=gs),
                  reads=[gs], writes=[hih], tag=f"hh{g}.{i}")
            S.add("ve", lambda e, hih=hih, hil=hil: e.tensor_tensor(
                hil, gs, hih, op=ALU.subtract),
                reads=[gs, hih], writes=[hil], tag=f"hl{g}.{i}")
            S.add("pe", lambda e, hih=hih, i=i: e.matmul(
                p_cs, conesb[:, 0:1], hih, start=(i == 0), stop=False),
                reads=[conesb[:, 0:1], hih], writes=[p_cs], tag=f"cs{g}.{i}")
            S.add("pe", lambda e, hil=hil, i=i: e.matmul(
                p_cs, conesb[:, 0:1], hil, start=False, stop=(i == NI - 1)),
                reads=[conesb[:, 0:1], hil], writes=[p_cs], tag=f"cs2{g}.{i}")

        for i in range(NI):
            ldcast(i)
            gumms(i)
            guevac(i)
        S.add("ve", lambda e: e.tensor_scalar(R(6), p_cs, -128.0, None, op0=ALU.mult),
              reads=[p_cs], writes=[R(6)], tag=f"csr{g}")
        NQ = (NI + 15) // 16
        for o in range(16):
            b = o % 2
            for qp in range(max(1, NQ // 2)):
                stg = t[f"wdb{qp % 2}"].ap().bitcast(U8C)[:, 4096:8192]
                S.add("sy", lambda e, stg=stg, o=o, qp=qp: e.dma_start(
                    out=stg, in_=W["wdm"][o, qp]),
                    reads=[W["wdm"][o, qp]], writes=[stg], dma=f"{uid}wd{qp % 2}",
                    tag=f"ldwd{g}.{o}.{qp}")
            for q in range(NQ):
                bb = q % 2
                wdb = t[f"wdb{bb}"].ap().bitcast(BF16)[:, 0:2048]
                srcq = t[f"wdb{q // 2}"].ap().bitcast(U8C)[:, 4096 + (q % 2) * 2048: 4096 + (q % 2 + 1) * 2048]
                eng = "ve" if q % 2 == 0 else "sc"
                if eng == "ve":
                    S.add("ve", lambda e, wdb=wdb, srcq=srcq: e.tensor_copy(wdb, srcq),
                          reads=[srcq], writes=[wdb], tag=f"cvd{g}.{o}.{q}")
                else:
                    S.add("sc", lambda e, wdb=wdb, srcq=srcq: e.copy(out=wdb, in_=srcq),
                          reads=[srcq], writes=[wdb], tag=f"cvd{g}.{o}.{q}")
                for ii in range(min(16, NI - q * 16)):
                    iglob = q * 16 + ii
                    ws = wdb[:, ii * 128:(ii + 1) * 128]
                    for hl, hsrc in ((0, hhh), (1, hhl)):
                        hsl = hsrc[:, iglob * TG:(iglob + 1) * TG]
                        S.add("pe", lambda e, ws=ws, hsl=hsl, q=q, ii=ii, b=b, hl=hl: e.matmul(
                            p_d[b], ws, hsl, start=(q == 0 and ii == 0 and hl == 0), stop=False),
                            reads=[ws, hsl], writes=[p_d[b]], tag=f"mmd{g}.{o}.{q}.{ii}.{hl}")
            S.add("pe", lambda e, b=b: e.matmul(
                p_d[b], cones[0:1, :], R(6), start=False, stop=True),
                reads=[cones[0:1, :], R(6)], writes=[p_d[b]], tag=f"rk{g}.{o}")
            oslot = o % 2
            outr = t[f"outr{oslot}"].ap()[:, 0:TG]
            xres = t[f"outr{oslot}"].ap()[:, 256:256 + TG]
            S.add("sc", lambda e, o=o, g=g, xres=xres: e.dma_start(
                out=xres, in_=x_in3[:, o, g * TG:(g + 1) * TG]),
                reads=[x_in3[:, o, g * TG:(g + 1) * TG]], writes=[xres],
                dma=f"{uid}xr{oslot}", tag=f"ldxr{g}.{o}")
            S.add("ve", lambda e, o=o, b=b: e.tensor_scalar(
                tscr[:, 0:TG], p_d[b], sdt[:, o:o + 1], 0.5, op0=ALU.mult, op1=ALU.mult),
                reads=[p_d[b], sdt[:, o:o + 1]], writes=[tscr[:, 0:TG]], tag=f"dts{g}.{o}")
            S.add("ve", lambda e, o=o, outr=outr, xres=xres: e.tensor_tensor(
                outr, tscr[:, 0:TG], xres, op=ALU.add),
                reads=[tscr[:, 0:TG], xres], writes=[outr],
                tag=f"res{g}.{o}")
            S.add("sc", lambda e, o=o, g=g, outr=outr: e.dma_start(
                out=x_out3[:, o, g * TG:(g + 1) * TG], in_=outr),
                reads=[outr], writes=[x_out3[:, o, g * TG:(g + 1) * TG]],
                dma=f"{uid}st{oslot}", tag=f"st{g}.{o}")



def pack_ffn0_weights(gate_w, gate_scale, up_w, up_scale, down_w, down_scale):
    g = np.ascontiguousarray(
        np.asarray(gate_w).reshape(64, 128, 16, 128).transpose(0, 3, 2, 1)).reshape(64, 128, 2048).astype(np.uint8)
    u = np.ascontiguousarray(
        np.asarray(up_w).reshape(64, 128, 16, 128).transpose(0, 3, 2, 1)).reshape(64, 128, 2048).astype(np.uint8)
    g = np.ascontiguousarray(np.concatenate([g, u], axis=2))
    dnw = np.asarray(down_w).reshape(16, 128, 64, 128)
    dn = np.ascontiguousarray(
        dnw.transpose(0, 2, 3, 1).reshape(16, 4, 16, 128, 128).transpose(0, 1, 3, 2, 4)
    ).reshape(16, 4, 128, 2048).astype(np.uint8)
    dn = np.ascontiguousarray(dn.reshape(16, 2, 128 * 0 + 2, 128, 2048).reshape(16, 2, 2, 128, 2048)
                              .transpose(0, 1, 3, 2, 4)).reshape(16, 2, 128, 4096)
    sg = np.ascontiguousarray(np.asarray(gate_scale).reshape(64, 128).T)
    su = np.ascontiguousarray(np.asarray(up_scale).reshape(64, 128).T)
    sd = np.ascontiguousarray(np.asarray(down_scale).reshape(16, 128).T)
    return dict(wgm=g, wdm=dn, sg=sg, su=su, sd=sd)


def declare_ffn0_weights(nc, prefix=""):
    d = {}
    U8 = mybir.dt.uint8
    d["wgm"] = nc.dram_tensor(prefix + "wgm", [64, 128, 4096], U8, kind="ExternalInput").ap()
    d["wdm"] = nc.dram_tensor(prefix + "wdm", [16, 2, 128, 4096], U8, kind="ExternalInput").ap()
    d["sg"] = nc.dram_tensor(prefix + "sg", [128, 64], F32, kind="ExternalInput").ap()
    d["su"] = nc.dram_tensor(prefix + "su", [128, 64], F32, kind="ExternalInput").ap()
    d["sd"] = nc.dram_tensor(prefix + "sd", [128, 16], F32, kind="ExternalInput").ap()
    return d


def build_ffn0_program():
    nc = bass.Bass("TRN2", target_bir_lowering=False)
    Lx = FFN_NG * FFN_TG
    x_in = nc.dram_tensor("x_in", [D, Lx], F32, kind="ExternalInput").ap()
    x_out = nc.dram_tensor("x_out", [D, Lx], F32, kind="ExternalOutput").ap()
    W = declare_ffn0_weights(nc)
    cones_d = nc.dram_tensor("cones_d", [128, 128], F32, kind="ExternalInput").ap()
    conesb_d = nc.dram_tensor("conesb_d", [128, 128], BF16, kind="ExternalInput").ap()
    ctx = ExitStack()
    with ctx:
        A = Arena(nc, ctx)
        A.sb("cones", [128, 128])
        alloc_shared(A)
        alloc_ffn0(A)
        P = [A.ps(f"pb{i}", [128, 512]) for i in range(8)]
        S = Sched(nc, psum_names=[f"pb{i}" for i in range(8)])
        S.add("sy", lambda e: e.dma_start(out=A.t["cones"].ap(), in_=cones_d),
              writes=[A.t["cones"].ap()], dma="cones")
        S.add("sy", lambda e: e.dma_start(out=A.t["conesb"].ap(), in_=conesb_d),
              writes=[A.t["conesb"].ap()], dma="conesb")
        emit_consts(S, A)
        emit_ffn0(S, A, P, W, x_in, x_out, uid="f")
        n = S.finalize(ctx)
    return nc, n





MOE_TG = 512
MOE_NG = L // MOE_TG


def alloc_moe1(A):
    A.sb("wrt", [128, 16 * 8])
    A.sb("gate", [128, 32])
    A.sb("gsc", [128, 32])
    A.sb("lsb", [128, 8])
    A.sb("g8", [8, 512])
    A.sb("sel8", [8, 1024])
    A.sb("beC", [128, 128])
    A.sb("gbsb", [128, 512])


def emit_moe1(S, A, P, W, x_in, x_out, uid):
    t = A.t
    TG, NG = MOE_TG, MOE_NG
    xg, xn = t["xg"].ap(), t["xn"].ap()
    slab = t["slab"].ap()
    acc = [slab[:, o * TG:(o + 1) * TG] for o in range(16)]
    cones = t["cones"].ap()
    ceye = t["ceye"].ap()
    tscr = t["tscr"].ap()
    wrt = t["wrt"].ap()
    gate = t["gate"].ap()
    gsc = t["gsc"].ap()
    lsb = t["lsb"].ap()
    g8 = t["g8"].ap()
    sel8 = t["sel8"].ap()
    beC = t["beC"].ap()
    gbsb = t["gbsb"].ap()

    x_in3 = x_in.rearrange("(c p) t -> p c t", p=128)
    x_out3 = x_out.rearrange("(c p) t -> p c t", p=128)

    S.add("sy", lambda e: e.dma_start(
        out=wrt.rearrange("p (c r) -> p c r", r=8),
        in_=W["wr"].rearrange("(c p) r -> p c r", p=128)),
        writes=[wrt], dma=f"{uid}p0")
    S.add("sy", lambda e: e.dma_start(out=sel8, in_=W["sel8"]),
          writes=[sel8], dma=f"{uid}p1")
    S.add("sy", lambda e: e.dma_start(out=beC, in_=W["beC"]),
          writes=[beC], dma=f"{uid}p2")

    p_y = [P[0].ap()[:, 0:TG], P[1].ap()[:, 0:TG]]
    p_gb = P[2].ap()[:, 0:TG]
    p_tr = P[3].ap()[0:8, 0:128]
    p_l = P[3].ap()[:, 256:264]

    def GS(i):
        return gsc[:, i * 8:(i + 1) * 8]

    def GC(i):
        return gsc[:, 24 + i:25 + i]

    for g in range(NG):
        S.add("sy", lambda e, g=g: e.dma_start(
            out=xg[:, 0:16 * TG].rearrange("p (c t) -> p c t", t=TG),
            in_=x_in3[:, :, g * TG:(g + 1) * TG]),
            reads=[x_in3[:, :, g * TG:(g + 1) * TG]], writes=[xg[:, 0:16 * TG]],
            dma=f"{uid}xg", tag=f"{uid}ldx{g}")
        emit_ln(S, A, P, xg, xn, TG, tag=f"{uid}ln{g}")
        for tc in range(4):
            for c in range(16):
                xsl = xn[:, c * TG + tc * 128: c * TG + tc * 128 + 128]
                S.add("pe", lambda e, xsl=xsl, c=c: e.matmul(
                    p_l, xsl, wrt[:, c * 8:(c + 1) * 8], start=(c == 0), stop=(c == 15)),
                    reads=[xsl, wrt[:, c * 8:(c + 1) * 8]], writes=[p_l],
                    tag=f"{uid}lg{g}.{tc}.{c}")
            S.add("ve", lambda e: e.tensor_copy(lsb, p_l), reads=[p_l], writes=[lsb])
            S.add("ve", lambda e: e.reduce_max(GC(0), lsb, axis=AX.X),
                  reads=[lsb], writes=[GC(0)], tag=f"{uid}m1")
            S.add("ve", lambda e: e.tensor_scalar(GS(0), lsb, GC(0), None, op0=ALU.is_equal),
                  reads=[lsb, GC(0)], writes=[GS(0)], tag=f"{uid}eq1")
            S.add("ve", lambda e: e.tensor_scalar(GS(1), GS(0), -1e30, None, op0=ALU.mult),
                  reads=[GS(0)], writes=[GS(1)])
            S.add("ve", lambda e: e.tensor_tensor(GS(1), lsb, GS(1), op=ALU.add),
                  reads=[lsb, GS(1)], writes=[GS(1)])
            S.add("ve", lambda e: e.reduce_max(GC(1), GS(1), axis=AX.X),
                  reads=[GS(1)], writes=[GC(1)], tag=f"{uid}m2")
            S.add("ve", lambda e: e.tensor_scalar(GS(1), lsb, GC(1), None, op0=ALU.is_equal),
                  reads=[lsb, GC(1)], writes=[GS(1)], tag=f"{uid}eq2")
            S.add("ve", lambda e: e.tensor_tensor(GS(0), GS(0), GS(1), op=ALU.add),
                  reads=[GS(0), GS(1)], writes=[GS(0)], tag=f"{uid}sel")
            S.add("ve", lambda e: e.tensor_scalar(GC(2), GC(0), -1.0, None, op0=ALU.mult),
                  reads=[GC(0)], writes=[GC(2)], tag=f"{uid}nm1")
            S.add("sc", lambda e: e.activation(out=GS(2), in_=lsb, func=AF.Exp, bias=GC(2)),
                  reads=[lsb, GC(2)], writes=[GS(2)], tag=f"{uid}expP")
            S.add("ve", lambda e: e.tensor_tensor(GC(3), GC(1), GC(0), op=ALU.subtract),
                  reads=[GC(1), GC(0)], writes=[GC(3)], tag=f"{uid}d21")
            S.add("sc", lambda e: e.activation(out=GC(4), in_=GC(3), func=AF.Exp),
                  reads=[GC(3)], writes=[GC(4)])
            S.add("ve", lambda e: e.tensor_scalar(GC(5), GC(4), 1.0, None, op0=ALU.add),
                  reads=[GC(4)], writes=[GC(5)])
            S.add("ve", lambda e: e.reciprocal(GC(6), GC(5)),
                  reads=[GC(5)], writes=[GC(6)])
            S.add("ve", lambda e: e.tensor_tensor(GS(2), GS(2), GS(0), op=ALU.mult),
                  reads=[GS(2), GS(0)], writes=[GS(2)])
            gsl = gate[:, tc * 8:(tc + 1) * 8]
            S.add("ve", lambda e, gsl=gsl: e.tensor_scalar(
                gsl, GS(2), GC(6), 0.5, op0=ALU.mult, op1=ALU.mult),
                reads=[GS(2), GC(6)], writes=[gsl], tag=f"{uid}gate{g}.{tc}")
            S.add("pe", lambda e, gsl=gsl: e.transpose(p_tr, gsl, ceye),
                  reads=[gsl, ceye], writes=[p_tr], tag=f"{uid}gt{g}.{tc}")
            S.add("ve", lambda e, tc=tc: e.tensor_copy(g8[:, tc * 128:(tc + 1) * 128], p_tr),
                  reads=[p_tr], writes=[g8[:, tc * 128:(tc + 1) * 128]],
                  tag=f"{uid}gtc{g}.{tc}")
        for ei in range(8):
            S.add("pe", lambda e, ei=ei: e.matmul(
                p_gb, sel8[:, ei * 128:(ei + 1) * 128], g8, start=True, stop=True),
                reads=[sel8[:, ei * 128:(ei + 1) * 128], g8], writes=[p_gb],
                tag=f"{uid}gb{g}.{ei}")
            S.add("ve", lambda e: e.tensor_copy(gbsb[:, 0:TG], p_gb),
                  reads=[p_gb], writes=[gbsb[:, 0:TG]], tag=f"{uid}gbe{g}.{ei}")
            for o in range(16):
                b = o % 2
                wb = t[f"wgb{b}"].ap()
                S.add("sy", lambda e, wb=wb, ei=ei, o=o: e.dma_start(
                    out=wb, in_=W["wem"][ei, o]),
                    reads=[W["wem"][ei, o]], writes=[wb], dma=f"{uid}we{b}",
                    tag=f"{uid}ldw{g}.{ei}.{o}")
                for c in range(16):
                    xnc = xn[:, c * TG:(c + 1) * TG]
                    ws = wb[:, c * 128:(c + 1) * 128]
                    S.add("pe", lambda e, ws=ws, xnc=xnc, c=c, b=b: e.matmul(
                        p_y[b], ws, xnc, start=(c == 0), stop=(c == 15)),
                        reads=[ws, xnc], writes=[p_y[b]], tag=f"{uid}mm{g}.{ei}.{o}.{c}")
                becol = beC[:, o * 8 + ei: o * 8 + ei + 1]
                t1 = tscr[:, 0:TG]
                S.add("ve", lambda e, b=b, becol=becol, t1=t1: e.tensor_scalar(
                    t1, p_y[b], becol, None, op0=ALU.add),
                    reads=[p_y[b], becol], writes=[t1], tag=f"{uid}bea{g}.{ei}.{o}")
                if ei == 0:
                    S.add("ve", lambda e, o=o, t1=t1: e.tensor_tensor(
                        acc[o], t1, gbsb[:, 0:TG], op=ALU.mult),
                        reads=[t1, gbsb[:, 0:TG]], writes=[acc[o]], tag=f"{uid}a0{g}.{o}")
                else:
                    t2 = tscr[:, TG:2 * TG]
                    S.add("ve", lambda e, t1=t1, t2=t2: e.tensor_tensor(
                        t2, t1, gbsb[:, 0:TG], op=ALU.mult),
                        reads=[t1, gbsb[:, 0:TG]], writes=[t2], tag=f"{uid}gm{g}.{ei}.{o}")
                    if ei < 7:
                        S.add("ve", lambda e, o=o, t2=t2: e.tensor_tensor(
                            acc[o], acc[o], t2, op=ALU.add),
                            reads=[acc[o], t2], writes=[acc[o]], tag=f"{uid}ac{g}.{ei}.{o}")
                    else:
                        oslot = o % 2
                        outr = t[f"outr{oslot}"].ap()[:, 0:TG]
                        S.add("ve", lambda e, o=o, t2=t2, t1=t1: e.tensor_tensor(
                            t1, acc[o], t2, op=ALU.add),
                            reads=[acc[o], t2], writes=[t1], tag=f"{uid}af{g}.{o}")
                        S.add("ve", lambda e, o=o, outr=outr, t1=t1: e.tensor_tensor(
                            outr, t1, xg[:, o * TG:(o + 1) * TG], op=ALU.add),
                            reads=[t1, xg[:, o * TG:(o + 1) * TG]], writes=[outr],
                            tag=f"{uid}res{g}.{o}")
                        S.add("sc", lambda e, o=o, g=g, outr=outr, oslot=oslot: e.dma_start(
                            out=x_out3[:, o, g * TG:(g + 1) * TG], in_=outr),
                            reads=[outr], writes=[x_out3[:, o, g * TG:(g + 1) * TG]],
                            dma=f"{uid}st{oslot}", tag=f"{uid}st{g}.{o}")


def pack_moe1_weights(Wr, We, be):
    wem = np.ascontiguousarray(
        np.asarray(We).reshape(8, 16, 128, 16, 128).transpose(0, 3, 2, 1, 4)
    ).reshape(8, 16, 128, 2048)
    beC = np.ascontiguousarray(np.asarray(be).reshape(8, 16, 128).transpose(2, 1, 0)).reshape(128, 128)
    sel8 = np.kron(np.eye(8, dtype=np.float32), np.ones((1, 128), np.float32))
    return dict(wem=wem, wr=np.ascontiguousarray(np.asarray(Wr)), beC_d=beC,
                sel8_d=np.ascontiguousarray(sel8))


def declare_moe1_weights(nc, prefix=""):
    d = {}
    d["wem"] = nc.dram_tensor(prefix + "wem", [8, 16, 128, 2048], F32, kind="ExternalInput").ap()
    d["wr"] = nc.dram_tensor(prefix + "wr", [2048, 8], F32, kind="ExternalInput").ap()
    d["beC"] = nc.dram_tensor(prefix + "beC_d", [128, 128], F32, kind="ExternalInput").ap()
    d["sel8"] = nc.dram_tensor(prefix + "sel8_d", [8, 1024], F32, kind="ExternalInput").ap()
    return d


def build_moe1_program():
    nc = bass.Bass("TRN2", target_bir_lowering=False)
    Lx = MOE_NG * MOE_TG
    x_in = nc.dram_tensor("x_in", [D, Lx], F32, kind="ExternalInput").ap()
    x_out = nc.dram_tensor("x_out", [D, Lx], F32, kind="ExternalOutput").ap()
    W = declare_moe1_weights(nc)
    cones_d = nc.dram_tensor("cones_d", [128, 128], F32, kind="ExternalInput").ap()
    ceye_d = nc.dram_tensor("ceye_d", [128, 128], F32, kind="ExternalInput").ap()
    ctx = ExitStack()
    with ctx:
        A = Arena(nc, ctx)
        A.sb("cones", [128, 128])
        A.sb("ceye", [128, 128])
        alloc_shared(A)
        alloc_moe1(A)
        P = [A.ps(f"pb{i}", [128, 512]) for i in range(8)]
        S = Sched(nc, psum_names=[f"pb{i}" for i in range(8)])
        S.add("sy", lambda e: e.dma_start(out=A.t["cones"].ap(), in_=cones_d),
              writes=[A.t["cones"].ap()], dma="cones")
        S.add("sy", lambda e: e.dma_start(out=A.t["ceye"].ap(), in_=ceye_d),
              writes=[A.t["ceye"].ap()], dma="ceye")
        emit_consts(S, A)
        emit_moe1(S, A, P, W, x_in, x_out, uid="m")
        n = S.finalize(ctx)
    return nc, n




ATT_TG = 512
ATT_NG = L // ATT_TG
ATT_SCALE = float(128 ** -0.5)

_AS = dict(qh0=0, qh1=2048, vt=4096, ex0=6144, ex1=6656, at0=7168, at1=7680,
           la=8192, nfr=10240, zq=12288, mcol=12800, scol=12816, fsc=12832,
           asc=12848, lb4=12864, bb=13888)


def emit_attn0(S, A, P, W, x_in, x_out, qs, ks, vs, os_, uid):
    t = A.t
    TG, NG = ATT_TG, ATT_NG
    Lx = NG * TG
    NQC = Lx // 128
    NQT = Lx // 512
    xg, xn = t["xg"].ap(), t["xn"].ap()
    slab = t["slab"].ap()
    cones = t["cones"].ap()
    ceye = t["ceye"].ap()

    qh = [slab[:, _AS["qh0"]:_AS["qh0"] + Lx], slab[:, _AS["qh1"]:_AS["qh1"] + Lx]]
    vt = slab[:, _AS["vt"]:_AS["vt"] + Lx]
    ex = [slab[:, _AS["ex0"]:_AS["ex0"] + 512], slab[:, _AS["ex1"]:_AS["ex1"] + 512]]
    at = [slab[:, _AS["at0"]:_AS["at0"] + 512], slab[:, _AS["at1"]:_AS["at1"] + 512]]
    la = slab[0:16, _AS["la"]:_AS["la"] + 2048]
    nfr = slab[0:1, _AS["nfr"]:_AS["nfr"] + Lx]
    zq = slab[0:16, _AS["zq"]:_AS["zq"] + TG]
    mcol = slab[:, _AS["mcol"]:_AS["mcol"] + NQC]
    scol = slab[:, _AS["scol"]:_AS["scol"] + NQC]
    fsc = slab[:, _AS["fsc"]:_AS["fsc"] + 16]
    asc = slab[:, _AS["asc"]:_AS["asc"] + 16]
    lb4 = slab[:, _AS["lb4"]:_AS["lb4"] + 1024]
    bb = slab[:, _AS["bb"]:_AS["bb"] + 64]

    x_in3 = x_in.rearrange("(c p) t -> p c t", p=128)
    x_out3 = x_out.rearrange("(c p) t -> p c t", p=128)
    sp3 = {n: s.rearrange("(c p) t -> p c t", p=128) for n, s in
           (("q", qs), ("k", ks), ("v", vs), ("o", os_))}

    S.add("sy", lambda e: e.dma_start(out=lb4, in_=W["lb4"]), writes=[lb4], dma=f"{uid}p0")
    S.add("sy", lambda e: e.dma_start(out=bb, in_=W["bb"]), writes=[bb], dma=f"{uid}p1")

    p_pr = [P[0].ap()[:, 0:TG], P[1].ap()[:, 0:TG]]
    p_z = P[2].ap()[0:16, 0:TG]
    p_vt = P[2].ap()[:, 0:128]
    p_tr2 = P[2].ap()[0:1, 0:128]

    def proj(g, wi, wm, dst3, src_tile, resid):
        S.add("sy", lambda e: e.dma_start(out=la, in_=W["aqT4"][wi]),
              writes=[la], dma=f"{uid}la", tag=f"{uid}la{g}.{wi}")
        for c in range(16):
            xnc = src_tile[:, c * TG:(c + 1) * TG]
            lbs = lb4[:, wi * 256 + c * 16: wi * 256 + (c + 1) * 16]
            S.add("pe", lambda e, xnc=xnc, lbs=lbs, c=c: e.matmul(
                p_z, lbs, xnc, start=(c == 0), stop=(c == 15)),
                reads=[lbs, xnc], writes=[p_z], tag=f"{uid}z{g}.{wi}.{c}")
        S.add("ve", lambda e: e.tensor_scalar(zq, p_z, 0.5, None, op0=ALU.mult),
              reads=[p_z], writes=[zq], tag=f"{uid}zev{g}.{wi}")
        for o in range(16):
            b = o % 2
            wb = t[f"wgb{b}"].ap()
            S.add("sy", lambda e, wb=wb, o=o, wm=wm: e.dma_start(out=wb, in_=wm[o]),
                  reads=[wm[o]], writes=[wb], dma=f"{uid}w{b}", tag=f"{uid}ldw{g}.{wi}.{o}")
            for c in range(16):
                xnc = src_tile[:, c * TG:(c + 1) * TG]
                ws = wb[:, c * 128:(c + 1) * 128]
                S.add("pe", lambda e, ws=ws, xnc=xnc, c=c, b=b: e.matmul(
                    p_pr[b], ws, xnc, start=(c == 0), stop=False),
                    reads=[ws, xnc], writes=[p_pr[b]], tag=f"{uid}pm{g}.{wi}.{o}.{c}")
            las = la[:, o * 128:(o + 1) * 128]
            S.add("pe", lambda e, las=las, b=b: e.matmul(
                p_pr[b], las, zq, start=False, stop=True),
                reads=[las, zq], writes=[p_pr[b]], tag=f"{uid}aug{g}.{wi}.{o}")
            oslot = o % 2
            outr = t[f"outr{oslot}"].ap()[:, 0:TG]
            bcol = bb[:, wi * 16 + o: wi * 16 + o + 1]
            if resid is None:
                S.add("ve", lambda e, b=b, outr=outr, bcol=bcol: e.tensor_scalar(
                    outr, p_pr[b], bcol, None, op0=ALU.add),
                    reads=[p_pr[b], bcol], writes=[outr], tag=f"{uid}pev{g}.{wi}.{o}")
                S.add("sc", lambda e, o=o, g=g, outr=outr, dst3=dst3, oslot=oslot: e.dma_start(
                    out=dst3[:, o, g * TG:(g + 1) * TG], in_=outr),
                    reads=[outr], writes=[dst3[:, o, g * TG:(g + 1) * TG]],
                    dma=f"{uid}st{oslot}", tag=f"{uid}sts{g}.{wi}.{o}")
            else:
                tsc = t["tscr"].ap()[:, 0:TG]
                S.add("ve", lambda e, b=b, bcol=bcol, tsc=tsc: e.tensor_scalar(
                    tsc, p_pr[b], bcol, None, op0=ALU.add),
                    reads=[p_pr[b], bcol], writes=[tsc], tag=f"{uid}pev{g}.{wi}.{o}")
                S.add("ve", lambda e, o=o, outr=outr, tsc=tsc: e.tensor_tensor(
                    outr, tsc, xg[:, o * TG:(o + 1) * TG], op=ALU.add),
                    reads=[tsc, xg[:, o * TG:(o + 1) * TG]], writes=[outr],
                    tag=f"{uid}res{g}.{o}")
                S.add("sc", lambda e, o=o, g=g, outr=outr, oslot=oslot: e.dma_start(
                    out=x_out3[:, o, g * TG:(g + 1) * TG], in_=outr),
                    reads=[outr], writes=[x_out3[:, o, g * TG:(g + 1) * TG]],
                    dma=f"{uid}st{oslot}", tag=f"{uid}stx{g}.{o}")

    for g in range(NG):
        S.add("sy", lambda e, g=g: e.dma_start(
            out=xg[:, 0:16 * TG].rearrange("p (c t) -> p c t", t=TG),
            in_=x_in3[:, :, g * TG:(g + 1) * TG]),
            reads=[x_in3[:, :, g * TG:(g + 1) * TG]], writes=[xg[:, 0:16 * TG]],
            dma=f"{uid}xg", tag=f"{uid}ldx{g}")
        emit_ln(S, A, P, xg, xn, TG, tag=f"{uid}ln{g}")
        for wi, (wm, dst) in enumerate(((W["wqm"], sp3["q"]), (W["wkm"], sp3["k"]),
                                        (W["wvm"], sp3["v"]))):
            proj(g, wi, wm, dst, xn, None)

    for h in range(16):
        b = h % 2
        khb = t[f"wub{b}"].ap()[:, 0:Lx]
        vhb = t[f"wdb{b}"].ap()[:, 0:Lx]
        qhb = qh[b]
        for (nm, buf, s3) in (("k", khb, sp3["k"]), ("v", vhb, sp3["v"]), ("q", qhb, sp3["q"])):
            S.add("sy", lambda e, buf=buf, s3=s3, h=h: e.dma_start(
                out=buf, in_=s3[:, h, :]),
                reads=[s3[:, h, :]], writes=[buf], dma=f"{uid}h{nm}{b}",
                tag=f"{uid}ld{nm}{h}")
        for kc in range(NQC):
            S.add("pe", lambda e, kc=kc, vhb=vhb: e.transpose(p_vt, vhb[:, kc * 128:(kc + 1) * 128], ceye),
                  reads=[vhb[:, kc * 128:(kc + 1) * 128], ceye], writes=[p_vt],
                  tag=f"{uid}vt{h}.{kc}")
            S.add("ve", lambda e, kc=kc: e.tensor_copy(vt[:, kc * 128:(kc + 1) * 128], p_vt),
                  reads=[p_vt], writes=[vt[:, kc * 128:(kc + 1) * 128]],
                  tag=f"{uid}vtc{h}.{kc}")
        for qc in range(NQC):
            PB = [P[(qc % 2) * 4 + kt] for kt in range(NQT)]
            for kt in range(NQT):
                S.add("pe", lambda e, qc=qc, kt=kt, qhb=qhb, khb=khb, PB=PB: e.matmul(
                    PB[kt].ap(), qhb[:, qc * 128:(qc + 1) * 128],
                    khb[:, kt * 512:(kt + 1) * 512], start=True, stop=True),
                    reads=[qhb[:, qc * 128:(qc + 1) * 128], khb[:, kt * 512:(kt + 1) * 512]],
                    writes=[PB[kt].ap()], tag=f"{uid}sc{h}.{qc}.{kt}")
                S.add("ve", lambda e, qc=qc, kt=kt, PB=PB: e.reduce_max(
                    asc[:, kt:kt + 1], PB[kt].ap(), axis=AX.X),
                    reads=[PB[kt].ap()], writes=[asc[:, kt:kt + 1]],
                    tag=f"{uid}mx{h}.{qc}.{kt}")
            mc = mcol[:, qc:qc + 1]
            if NQT == 4:
                S.add("ve", lambda e: e.tensor_tensor(asc[:, 4:5], asc[:, 0:1], asc[:, 1:2], op=ALU.max),
                      reads=[asc[:, 0:2]], writes=[asc[:, 4:5]])
                S.add("ve", lambda e: e.tensor_tensor(asc[:, 5:6], asc[:, 2:3], asc[:, 3:4], op=ALU.max),
                      reads=[asc[:, 2:4]], writes=[asc[:, 5:6]])
                S.add("ve", lambda e, mc=mc: e.tensor_tensor(mc, asc[:, 4:5], asc[:, 5:6], op=ALU.max),
                      reads=[asc[:, 4:6]], writes=[mc], tag=f"{uid}m{h}.{qc}")
            else:
                S.add("ve", lambda e, mc=mc: e.tensor_copy(mc, asc[:, 0:1]),
                      reads=[asc[:, 0:1]], writes=[mc], tag=f"{uid}m{h}.{qc}")
            S.add("ve", lambda e, mc=mc: e.tensor_scalar(
                asc[:, 6:7], mc, -ATT_SCALE, None, op0=ALU.mult),
                reads=[mc], writes=[asc[:, 6:7]], tag=f"{uid}nm{h}.{qc}")
            for kt in range(NQT):
                S.add("sc", lambda e, kt=kt, PB=PB: e.activation(
                    out=ex[kt % 2], in_=PB[kt].ap(), func=AF.Exp, bias=asc[:, 6:7],
                    scale=ATT_SCALE, accum_out=asc[:, 8 + kt:9 + kt]),
                    reads=[PB[kt].ap(), asc[:, 6:7]], writes=[ex[kt % 2], asc[:, 8 + kt:9 + kt]],
                    tag=f"{uid}ex{h}.{qc}.{kt}")
            sc_ = scol[:, qc:qc + 1]
            if NQT == 4:
                S.add("ve", lambda e: e.tensor_tensor(asc[:, 12:13], asc[:, 8:9], asc[:, 9:10], op=ALU.add),
                      reads=[asc[:, 8:10]], writes=[asc[:, 12:13]])
                S.add("ve", lambda e: e.tensor_tensor(asc[:, 13:14], asc[:, 10:11], asc[:, 11:12], op=ALU.add),
                      reads=[asc[:, 10:12]], writes=[asc[:, 13:14]])
                S.add("ve", lambda e, sc_=sc_: e.tensor_tensor(sc_, asc[:, 12:13], asc[:, 13:14], op=ALU.add),
                      reads=[asc[:, 12:14]], writes=[sc_], tag=f"{uid}s{h}.{qc}")
            else:
                S.add("ve", lambda e, sc_=sc_: e.tensor_copy(sc_, asc[:, 8:9]),
                      reads=[asc[:, 8:9]], writes=[sc_], tag=f"{uid}s{h}.{qc}")
        S.add("sc", lambda e: e.activation(out=fsc[:, 0:NQC], in_=scol[:, 0:NQC], func=AF.Ln),
              reads=[scol[:, 0:NQC]], writes=[fsc[:, 0:NQC]], tag=f"{uid}lns{h}")
        S.add("ve", lambda e: e.tensor_scalar(asc[:, 0:NQC], mcol[:, 0:NQC], ATT_SCALE, None, op0=ALU.mult),
              reads=[mcol[:, 0:NQC]], writes=[asc[:, 0:NQC]], tag=f"{uid}msc{h}")
        S.add("ve", lambda e: e.tensor_tensor(fsc[:, 0:NQC], asc[:, 0:NQC], fsc[:, 0:NQC], op=ALU.add),
              reads=[asc[:, 0:NQC], fsc[:, 0:NQC]], writes=[fsc[:, 0:NQC]], tag=f"{uid}f{h}")
        S.add("ve", lambda e: e.tensor_scalar(fsc[:, 0:NQC], fsc[:, 0:NQC], -1.0 / ATT_SCALE, None, op0=ALU.mult),
              reads=[fsc[:, 0:NQC]], writes=[fsc[:, 0:NQC]], tag=f"{uid}nf{h}")
        for qc in range(NQC):
            S.add("pe", lambda e, qc=qc: e.transpose(p_tr2, fsc[:, qc:qc + 1], ceye),
                  reads=[fsc[:, qc:qc + 1], ceye], writes=[p_tr2], tag=f"{uid}ft{h}.{qc}")
            S.add("ve", lambda e, qc=qc: e.tensor_copy(nfr[0:1, qc * 128:(qc + 1) * 128], p_tr2),
                  reads=[p_tr2], writes=[nfr[0:1, qc * 128:(qc + 1) * 128]],
                  tag=f"{uid}ftc{h}.{qc}")
        steps = [(kc, qt) for kc in range(NQC) for qt in range(NQT)]

        def emit_sT(i):
            kc, qt = steps[i]
            pb = P[4 + i % 2].ap()
            S.add("pe", lambda e, kc=kc, qt=qt, pb=pb, khb=khb, qhb=qhb: e.matmul(
                pb, khb[:, kc * 128:(kc + 1) * 128], qhb[:, qt * 512:(qt + 1) * 512],
                start=True, stop=False),
                reads=[khb[:, kc * 128:(kc + 1) * 128], qhb[:, qt * 512:(qt + 1) * 512]],
                writes=[pb], tag=f"{uid}sT{h}.{kc}.{qt}")
            S.add("pe", lambda e, qt=qt, pb=pb: e.matmul(
                pb, cones[0:1, :], nfr[0:1, qt * 512:(qt + 1) * 512],
                start=False, stop=True),
                reads=[cones[0:1, :], nfr[0:1, qt * 512:(qt + 1) * 512]],
                writes=[pb], tag=f"{uid}rk{h}.{kc}.{qt}")
            ats = at[i % 2]
            S.add("sc", lambda e, pb=pb, ats=ats: e.activation(
                out=ats, in_=pb, func=AF.Exp, scale=ATT_SCALE),
                reads=[pb], writes=[ats], tag=f"{uid}ea{h}.{kc}.{qt}")
            return ats

        def emit_av(i, ats):
            kc, qt = steps[i]
            S.add("pe", lambda e, kc=kc, qt=qt, ats=ats: e.matmul(
                P[qt].ap(), vt[:, kc * 128:(kc + 1) * 128], ats,
                start=(kc == 0), stop=(kc == NQC - 1)),
                reads=[vt[:, kc * 128:(kc + 1) * 128], ats], writes=[P[qt].ap()],
                tag=f"{uid}av{h}.{kc}.{qt}")

        prev = emit_sT(0)
        for i in range(1, len(steps)):
            cur = emit_sT(i)
            emit_av(i - 1, prev)
            prev = cur
        emit_av(len(steps) - 1, prev)
        for qt in range(NQT):
            oslot = qt % 2
            outr = t[f"outr{oslot}"].ap()
            S.add("ve", lambda e, qt=qt, outr=outr: e.tensor_copy(outr, P[qt].ap()),
                  reads=[P[qt].ap()], writes=[outr], tag=f"{uid}oev{h}.{qt}")
            S.add("sc", lambda e, h=h, qt=qt, outr=outr, oslot=oslot: e.dma_start(
                out=sp3["o"][:, h, qt * 512:(qt + 1) * 512], in_=outr),
                reads=[outr], writes=[sp3["o"][:, h, qt * 512:(qt + 1) * 512]],
                dma=f"{uid}so{oslot}", tag=f"{uid}sto{h}.{qt}")

    for g in range(NG):
        S.add("sy", lambda e, g=g: e.dma_start(
            out=xg[:, 0:16 * TG].rearrange("p (c t) -> p c t", t=TG),
            in_=x_in3[:, :, g * TG:(g + 1) * TG]),
            reads=[x_in3[:, :, g * TG:(g + 1) * TG]], writes=[xg[:, 0:16 * TG]],
            dma=f"{uid}xg", tag=f"{uid}ldx2{g}")
        S.add("sy", lambda e, g=g: e.dma_start(
            out=xn[:, 0:16 * TG].rearrange("p (c t) -> p c t", t=TG),
            in_=sp3["o"][:, :, g * TG:(g + 1) * TG]),
            reads=[sp3["o"][:, :, g * TG:(g + 1) * TG]], writes=[xn[:, 0:16 * TG]],
            dma=f"{uid}xo", tag=f"{uid}ldo{g}")
        proj(g, 3, W["wom"], None, xn, resid=True)


def pack_proj(Wn):
    return np.ascontiguousarray(
        np.asarray(Wn).reshape(16, 128, 16, 128).transpose(0, 3, 2, 1)).reshape(16, 128, 2048)


def pack_attn0_weights(Wq, bq, Bq, Aq, Wk, bk, Bk, Ak, Wv, bv, Bv, Av, Wo, bo, Bo, Ao):
    d = {}
    d["wqm"], d["wkm"], d["wvm"], d["wom"] = (pack_proj(w) for w in (Wq, Wk, Wv, Wo))
    lb = []
    for B in (Bq, Bk, Bv, Bo):
        lb.append(np.ascontiguousarray(np.asarray(B).T).reshape(16, 128, 16).transpose(1, 0, 2).reshape(128, 256))
    d["lb4"] = np.ascontiguousarray(np.concatenate(lb, axis=1))
    d["aqT4"] = np.ascontiguousarray(np.stack([np.asarray(A_).T for A_ in (Aq, Ak, Av, Ao)]))
    bbs = [np.asarray(b).reshape(16, 128).T for b in (bq, bk, bv, bo)]
    d["bb"] = np.ascontiguousarray(np.concatenate(bbs, axis=1))
    return d


def declare_attn0_weights(nc, prefix=""):
    d = {}
    for n in ("wqm", "wkm", "wvm", "wom"):
        d[n] = nc.dram_tensor(prefix + n, [16, 128, 2048], F32, kind="ExternalInput").ap()
    d["lb4"] = nc.dram_tensor(prefix + "lb4_d", [128, 1024], F32, kind="ExternalInput").ap()
    d["aqT4"] = nc.dram_tensor(prefix + "aqT4_d", [4, 16, 2048], F32, kind="ExternalInput").ap()
    d["bb"] = nc.dram_tensor(prefix + "bb_d", [128, 64], F32, kind="ExternalInput").ap()
    return d


def build_attn0_program():
    nc = bass.Bass("TRN2", target_bir_lowering=False)
    Lx = ATT_NG * ATT_TG
    x_in = nc.dram_tensor("x_in", [D, Lx], F32, kind="ExternalInput").ap()
    x_out = nc.dram_tensor("x_out", [D, Lx], F32, kind="ExternalOutput").ap()
    qs = nc.dram_tensor("q_s", [D, Lx], F32, kind="Internal").ap()
    ks = nc.dram_tensor("k_s", [D, Lx], F32, kind="Internal").ap()
    vs = nc.dram_tensor("v_s", [D, Lx], F32, kind="Internal").ap()
    os_ = nc.dram_tensor("o_s", [D, Lx], F32, kind="Internal").ap()
    W = declare_attn0_weights(nc)
    cones_d = nc.dram_tensor("cones_d", [128, 128], F32, kind="ExternalInput").ap()
    ceye_d = nc.dram_tensor("ceye_d", [128, 128], F32, kind="ExternalInput").ap()
    ctx = ExitStack()
    with ctx:
        A = Arena(nc, ctx)
        A.sb("cones", [128, 128])
        A.sb("ceye", [128, 128])
        alloc_shared(A)
        P = [A.ps(f"pb{i}", [128, 512]) for i in range(8)]
        S = Sched(nc, psum_names=[f"pb{i}" for i in range(8)])
        S.add("sy", lambda e: e.dma_start(out=A.t["cones"].ap(), in_=cones_d),
              writes=[A.t["cones"].ap()], dma="cones")
        S.add("sy", lambda e: e.dma_start(out=A.t["ceye"].ap(), in_=ceye_d),
              writes=[A.t["ceye"].ap()], dma="ceye")
        emit_consts(S, A)
        emit_attn0(S, A, P, W, x_in, x_out, qs, ks, vs, os_, uid="a")
        n = S.finalize(ctx)
    return nc, n




AT1_TG = 512
AT1_NG = L // AT1_TG
AT1_EPS = 1e-6
AT1_DBG = False
AT1_LEVEL = 0
AT1_SECTS = 'zlcaetsbfw'

_A1 = dict(qb0=0, qb1=2048, xcwb=4096, dfm=6144, w1s=8192, TMg=10240,
           cxb=12288, ATsb=14336, Etsb=14848, AEtsb=15360, sTsb=15872,
           sTTsb=15888, s2sb=15920, coTM=16176, sfm=15872)
_X1 = dict(z1=0, z1e=256, sqs=512, xh=768, g0=1024, grad=1280, cog=1536,
           zsb=1792, zbe=2048, t1=2304, t2=2560, b1s=2816, yo=3072, outf=3328,
           gx=3584, stat=3840, stat2=4352, red=4864)


def alloc_attn1(A):
    A.sb("lwt", [128, 512])
    A.sb("hind", [2, 128])
    A.sb("o2bd", [128, 2])
    A.sb("a2c", [2, 256])
    A.sb("aTMc", [16, 1])
    A.sb("lbT", [1, 32])
    A.sb("eps2", [2, 1])


def emit_attn1(S, A, P, W, x_in, x_out, qs, ks, vs, xcws, sd, sdT, uid):
    t = A.t
    TG, NG = AT1_TG, AT1_NG
    Lx = NG * TG
    NCH = Lx // 16
    NMG = NCH // 8
    xg, xn = t["xg"].ap(), t["xn"].ap()
    slab = t["slab"].ap()
    cones = t["cones"].ap()
    ceye = t["ceye"].ap()

    qb = [slab[0:128, _A1["qb0"]:_A1["qb0"] + 2048],
          slab[0:128, _A1["qb1"]:_A1["qb1"] + 2048]]
    xcwb = slab[0:128, _A1["xcwb"]:_A1["xcwb"] + 2048]
    dfm = slab[0:128, _A1["dfm"]:_A1["dfm"] + 2048]
    w1s = slab[0:128, _A1["w1s"]:_A1["w1s"] + 2048]
    TMg = slab[0:16, _A1["TMg"]:_A1["TMg"] + 2048]
    cxb = slab[0:16, _A1["cxb"]:_A1["cxb"] + 2048]
    ATsb = slab[0:16, _A1["ATsb"]:_A1["ATsb"] + 512]
    Etsb = slab[0:16, _A1["Etsb"]:_A1["Etsb"] + 512]
    AEtsb = slab[0:16, _A1["AEtsb"]:_A1["AEtsb"] + 512]
    sTsb = slab[0:32, _A1["sTsb"]:_A1["sTsb"] + 16]
    sTTsb = slab[0:16, _A1["sTTsb"]:_A1["sTTsb"] + 32]
    s2sb = slab[0:2, _A1["s2sb"]:_A1["s2sb"] + 256]
    coTM = slab[0:16, _A1["coTM"]:_A1["coTM"] + 32]
    lwt = t["lwt"].ap()
    mkTnA = xn[0:16, 5392:5904]
    csel = xn[0:32, 5904:6416]
    gfm = xn[0:128, 4880:5136]
    bfm = xn[0:128, 5136:5392]
    hind = t["hind"].ap()
    o2bd = t["o2bd"].ap()
    a2c = t["a2c"].ap()
    aTMc = t["aTMc"].ap()
    lbT = t["lbT"].ap()
    eps2 = t["eps2"].ap()

    def XV(nm, cols):
        return xn[0:128, _X1[nm]:_X1[nm] + cols]

    x_in3 = x_in.rearrange("(c p) t -> p c t", p=128)
    x_out3 = x_out.rearrange("(c p) t -> p c t", p=128)
    sp3 = {n: s.rearrange("(c p) t -> p c t", p=128) for n, s in
           (("q", qs), ("k", ks), ("v", vs), ("x", xcws))}

    for nm in ("lwt", "hind", "o2bd", "a2c", "aTMc", "lbT"):
        ap = t[nm].ap()
        S.add("sy", lambda e, ap=ap, nm=nm: e.dma_start(out=ap, in_=W[nm]),
              writes=[ap], dma=f"{uid}c{nm}")
    S.add("sy", lambda e: e.dma_start(
        out=w1s.rearrange("p (pr o) -> p pr o", o=128),
        in_=W["w1bd"].rearrange("pr p o -> p pr o")),
        reads=[W["w1bd"]], writes=[w1s], dma=f"{uid}w1i")
    b1s = XV("b1s", 256)
    S.add("ve", lambda e: e.memset(eps2, AT1_EPS), writes=[eps2], tag=f"{uid}eps2")

    p_pr = [P[0].ap()[:, 0:TG], P[1].ap()[:, 0:TG]]

    for g in range(NG):
        S.add("sy", lambda e, g=g: e.dma_start(
            out=xg[:, 0:16 * TG].rearrange("p (c t) -> p c t", t=TG),
            in_=x_in3[:, :, g * TG:(g + 1) * TG]),
            reads=[x_in3[:, :, g * TG:(g + 1) * TG]], writes=[xg[:, 0:16 * TG]],
            dma=f"{uid}xg", tag=f"{uid}ldx{g}")
        emit_ln(S, A, P, xg, xn, TG, tag=f"{uid}ln{g}")
        p_il = P[2].ap()[:, 0:32]
        p_tr0 = P[3].ap()[0:32, 0:128]
        sfm = slab[0:32, _A1["sfm"]:_A1["sfm"] + 512]
        for tc in range(4):
            for c in range(16):
                xsl = xn[:, c * TG + tc * 128: c * TG + tc * 128 + 128]
                S.add("pe", lambda e, xsl=xsl, c=c: e.matmul(
                    p_il, xsl, lwt[:, c * 32:(c + 1) * 32], start=(c == 0), stop=False),
                    reads=[xsl, lwt[:, c * 32:(c + 1) * 32]], writes=[p_il],
                    tag=f"{uid}il{g}.{tc}.{c}")
            S.add("pe", lambda e: e.matmul(p_il, cones[0:1, :], lbT, start=False, stop=True),
                  reads=[cones[0:1, :], lbT], writes=[p_il], tag=f"{uid}ilb{g}.{tc}")
            stm = t["tscr"].ap()[:, 0:32]
            S.add("sc", lambda e, stm=stm: e.activation(out=stm, in_=p_il, func=AF.Sigmoid),
                  reads=[p_il], writes=[stm], tag=f"{uid}sig{g}.{tc}")
            row0 = g * 512 + tc * 128
            S.add("sc", lambda e, row0=row0, stm=stm: e.dma_start(
                out=sdT[row0:row0 + 128, :], in_=stm),
                reads=[stm], writes=[sdT[row0:row0 + 128, :]],
                dma=f"{uid}sdt", tag=f"{uid}ssdT{g}.{tc}")
            S.add("pe", lambda e, stm=stm, p_tr0=p_tr0: e.transpose(p_tr0, stm, ceye),
                  reads=[stm, ceye], writes=[p_tr0], tag=f"{uid}strp{g}.{tc}")
            S.add("ve", lambda e, tc=tc, sfm=sfm, p_tr0=p_tr0: e.tensor_copy(
                sfm[:, tc * 128:(tc + 1) * 128], p_tr0),
                reads=[p_tr0], writes=[sfm[:, tc * 128:(tc + 1) * 128]],
                tag=f"{uid}sTc{g}.{tc}")
        S.add("sc", lambda e, g=g, sfm=sfm: e.dma_start(
            out=sd[:, g * 512:(g + 1) * 512], in_=sfm),
            reads=[sfm], writes=[sd[:, g * 512:(g + 1) * 512]],
            dma=f"{uid}sds", tag=f"{uid}ssd{g}")
        for wi, (wmk, dst) in enumerate((("wqm1", "q"), ("wkm1", "k"), ("wvm1", "v"))):
            wm = W[wmk]
            for o in range(16):
                b = o % 2
                wb = t[f"wgb{b}"].ap()
                S.add("sy", lambda e, wb=wb, o=o, wm=wm: e.dma_start(out=wb, in_=wm[o]),
                      reads=[wm[o]], writes=[wb], dma=f"{uid}w{b}",
                      tag=f"{uid}ldw{g}.{wi}.{o}")
                for c in range(16):
                    xnc = xn[:, c * TG:(c + 1) * TG]
                    ws = wb[:, c * 128:(c + 1) * 128]
                    S.add("pe", lambda e, ws=ws, xnc=xnc, c=c, b=b: e.matmul(
                        p_pr[b], ws, xnc, start=(c == 0), stop=(c == 15)),
                        reads=[ws, xnc], writes=[p_pr[b]], tag=f"{uid}pm{g}.{wi}.{o}.{c}")
                oslot = o % 2
                outr = t[f"outr{oslot}"].ap()[:, 0:TG]
                S.add("ve", lambda e, b=b, outr=outr: e.tensor_copy(outr, p_pr[b]),
                      reads=[p_pr[b]], writes=[outr], tag=f"{uid}pev{g}.{wi}.{o}")
                d3 = sp3[dst]
                S.add("sc", lambda e, o=o, g=g, outr=outr, d3=d3, oslot=oslot: e.dma_start(
                    out=d3[:, o, g * TG:(g + 1) * TG], in_=outr),
                    reads=[outr], writes=[d3[:, o, g * TG:(g + 1) * TG]],
                    dma=f"{uid}st{oslot}", tag=f"{uid}sts{g}.{wi}.{o}")

    S.add("sy", lambda e: e.dma_start(out=b1s, in_=W["b1fm"]),
          reads=[W["b1fm"]], writes=[b1s], dma=f"{uid}b1i")
    for nm, ap in (("mkTnA", mkTnA), ("csel", csel), ("gfm", gfm), ("bfm", bfm)):
        S.add("sy", lambda e, ap=ap, nm=nm: e.dma_start(out=ap, in_=W[nm]),
              reads=[W[nm]], writes=[ap], dma=f"{uid}c{nm}")
    z1 = XV("z1", 256)
    z1e = XV("z1e", 256)
    sqs = XV("sqs", 256)
    xh = XV("xh", 256)
    g0 = XV("g0", 256)
    grad = XV("grad", 256)
    cog = XV("cog", 256)
    zsb = XV("zsb", 256)
    zbe = XV("zbe", 256)
    t1s = XV("t1", 256)
    t2s = XV("t2", 256)
    outf = XV("outf", 256)
    gxs = XV("gx", 256)
    red3 = XV("red", 16)
    stat = xn[0:2, _X1["stat"]:_X1["stat"] + 512]
    stat2 = xn[0:2, _X1["stat2"]:_X1["stat2"] + 512]

    pZ = P[0].ap()[:, 0:256]
    pR = P[1].ap()[0:2, 0:512]
    pB = [P[2].ap()[:, 0:256], P[3].ap()[:, 0:256]]
    pCO = P[4].ap()[:, 0:256]
    pCT = P[1].ap()[0:16, 0:512]
    pAT = P[5].ap()[0:16, 0:256]
    pATb = P[2].ap()[0:16, 256:512]
    pTR = [P[6].ap()[0:16, 0:512], P[7].ap()[0:16, 0:512]]
    pW = [P[5].ap()[:, 64:128], P[6].ap()[:, 64:128]]
    bci = [0]

    def bcast(vals_ap, tag):
        b = bci[0] % 2
        bci[0] += 1
        pb = pB[b]
        S.add("pe", lambda e, vals_ap=vals_ap, pb=pb: e.matmul(
            pb, hind, vals_ap, start=True, stop=True),
            reads=[hind, vals_ap], writes=[pb], tag=tag)
        return pb

    def bn_stats(src, tag):
        S.add("sc", lambda e, src=src: e.activation(out=sqs, in_=src, func=AF.Square),
              reads=[src], writes=[sqs], tag=f"{tag}sq")
        S.add("pe", lambda e, src=src: e.matmul(pR[:, 0:256], o2bd, src, start=True, stop=True),
              reads=[o2bd, src], writes=[pR], tag=f"{tag}rs")
        S.add("pe", lambda e: e.matmul(pR[:, 256:512], o2bd, sqs, start=True, stop=True),
              reads=[o2bd, sqs], writes=[pR], tag=f"{tag}rq")
        S.add("ve", lambda e: e.tensor_copy(stat, pR), reads=[pR], writes=[stat],
              tag=f"{tag}ev")
        S.add("ve", lambda e: e.tensor_scalar(stat2[:, 0:256], stat[:, 0:256], 1.0 / 64, None, op0=ALU.mult),
              reads=[stat[:, 0:256]], writes=[stat2[:, 0:256]], tag=f"{tag}mu")
        S.add("ve", lambda e: e.tensor_scalar(stat2[:, 256:512], stat[:, 256:512], 1.0 / 64, None, op0=ALU.mult),
              reads=[stat[:, 256:512]], writes=[stat2[:, 256:512]], tag=f"{tag}e2")
        S.add("ve", lambda e: e.tensor_tensor(stat[:, 0:256], stat2[:, 0:256], stat2[:, 0:256], op=ALU.mult),
              reads=[stat2[:, 0:256]], writes=[stat[:, 0:256]], tag=f"{tag}m2")
        S.add("ve", lambda e: e.tensor_tensor(stat[:, 256:512], stat2[:, 256:512], stat[:, 0:256], op=ALU.subtract),
              reads=[stat2[:, 256:512], stat[:, 0:256]], writes=[stat[:, 256:512]], tag=f"{tag}var")
        S.add("sc", lambda e: e.activation(out=stat[:, 0:256], in_=stat[:, 256:512],
                                           func=AF.Sqrt, bias=eps2),
              reads=[stat[:, 256:512], eps2], writes=[stat[:, 0:256]], tag=f"{tag}std")
        S.add("ve", lambda e: e.reciprocal(stat[:, 256:512], stat[:, 0:256]),
              reads=[stat[:, 0:256]], writes=[stat[:, 256:512]], tag=f"{tag}rst")

    for mg in range(NMG):
        mb = mg % 2
        kb = t[f"wub{mb}"].ap()[:, 0:2048]
        vb = t[f"wdb{mb}"].ap()[:, 0:2048]
        qbb = qb[mb]
        for (nm, buf, s3) in (("k", kb, sp3["k"]), ("v", vb, sp3["v"]), ("q", qbb, sp3["q"])):
            S.add("sy", lambda e, buf=buf, s3=s3, mg=mg: e.dma_start(
                out=buf.rearrange("p (pr t) -> p pr t", t=128),
                in_=s3[:, :, mg * 128:(mg + 1) * 128]),
                reads=[s3[:, :, mg * 128:(mg + 1) * 128]], writes=[buf],
                dma=f"{uid}b{nm}{mb}", tag=f"{uid}mg{nm}{mg}")
        S.add("ve", lambda e, kb=kb, vb=vb: e.tensor_tensor(dfm, vb, kb, op=ALU.subtract),
              reads=[vb, kb], writes=[dfm], tag=f"{uid}dfm{mg}")
        for cc in range(8):
            ch = mg * 8 + cc
            kb3 = kb.rearrange("p (pr t) -> p pr t", t=128)
            qb3 = qbb.rearrange("p (pr t) -> p pr t", t=128)
            db3 = dfm.rearrange("p (pr t) -> p pr t", t=128)
            XB = kb3[:, :, cc * 16:(cc + 1) * 16]
            XC = qb3[:, :, cc * 16:(cc + 1) * 16]
            DD = db3[:, :, cc * 16:(cc + 1) * 16]
            S.add("sy", lambda e, ch=ch: e.dma_start(out=sTsb, in_=W["sd_ap"][:, ch * 16:(ch + 1) * 16]),
                  reads=[W["sd_ap"][:, ch * 16:(ch + 1) * 16]], writes=[sTsb],
                  dma=f"{uid}lsT", tag=f"{uid}lsT{ch}")
            S.add("sy", lambda e, ch=ch: e.dma_start(out=sTTsb, in_=W["sdT_ap"][ch * 16:(ch + 1) * 16, :]),
                  reads=[W["sdT_ap"][ch * 16:(ch + 1) * 16, :]], writes=[sTTsb],
                  dma=f"{uid}lsTT", tag=f"{uid}lsTT{ch}")
            S.add("sy", lambda e, ch=ch: e.dma_start(
                out=s2sb.rearrange("r (pr k) -> r pr k", k=16),
                in_=W["sd_ap"].rearrange("(pr r) (cg k) -> cg r pr k", r=2, k=16)[ch]),
                reads=[W["sd_ap"]], writes=[s2sb], dma=f"{uid}ls2", tag=f"{uid}ls2{ch}")
            if AT1_LEVEL >= 2:
                continue
            for pr in range(16 if "z" in AT1_SECTS else 0):
                S.add("pe", lambda e, pr=pr, XB=XB: e.matmul(
                    pZ[:, pr * 16:(pr + 1) * 16], w1s[:, pr * 128:(pr + 1) * 128],
                    XB[:, pr, :], start=True, stop=True),
                    reads=[w1s[:, pr * 128:(pr + 1) * 128], XB[:, pr, :]],
                    writes=[pZ], tag=f"{uid}z1.{ch}.{pr}")
            S.add("ve", lambda e: e.tensor_copy(z1, pZ), reads=[pZ], writes=[z1],
                  tag=f"{uid}z1v{ch}")
            S.add("ve", lambda e: e.tensor_tensor(z1e, z1, b1s, op=ALU.add),
                  reads=[z1, b1s], writes=[z1e], tag=f"{uid}z1b{ch}")
            S.add("ve", lambda e: e.tensor_tensor(s2sb, s2sb, a2c, op=ALU.mult),
                  reads=[s2sb, a2c], writes=[s2sb], tag=f"{uid}s2a{ch}")
            S.add("pe", lambda e: e.matmul(pCO, hind, s2sb, start=True, stop=True),
                  reads=[hind, s2sb], writes=[pCO], tag=f"{uid}cof{ch}")
            for h in range(32):
                pr, par = h // 2, h % 2
                xbh = XB[par * 64:(par + 1) * 64, pr, :]
                xch = XC[par * 64:(par + 1) * 64, pr, :]
                pdst = pAT if par == 0 else pATb
                S.add("pe", lambda e, h=h, xbh=xbh, xch=xch, pdst=pdst: e.matmul(
                    pdst[:, (h // 2) * 16:(h // 2 + 1) * 16], xbh, xch, start=True, stop=True),
                    reads=[xbh, xch], writes=[pdst], tag=f"{uid}at{ch}.{h}")
            ATv = ATsb.rearrange("j (hp par k) -> j hp par k", par=2, k=16)
            S.add("ve", lambda e, ATv=ATv: e.tensor_copy(
                ATv[:, :, 0, :], pAT.rearrange("j (hp k) -> j hp k", k=16)),
                reads=[pAT], writes=[ATsb], tag=f"{uid}atv{ch}")
            S.add("ve", lambda e, ATv=ATv: e.tensor_copy(
                ATv[:, :, 1, :], pATb.rearrange("j (hp k) -> j hp k", k=16)),
                reads=[pATb], writes=[ATsb], tag=f"{uid}atv2{ch}")
            S.add("pe", lambda e: e.matmul(pCT, sTsb, csel, start=True, stop=True),
                  reads=[sTsb, csel], writes=[pCT], tag=f"{uid}ct{ch}")
            S.add("ve", lambda e: e.tensor_tensor(Etsb, mkTnA, pCT, op=ALU.mult),
                  reads=[mkTnA, pCT], writes=[Etsb], tag=f"{uid}et{ch}")
            S.add("ve", lambda e: e.tensor_tensor(AEtsb, ATsb, Etsb, op=ALU.mult),
                  reads=[ATsb, Etsb], writes=[AEtsb], tag=f"{uid}aet{ch}")
            S.add("ve", lambda e: e.tensor_scalar(coTM, sTTsb, aTMc, None, op0=ALU.mult),
                  reads=[sTTsb, aTMc], writes=[coTM], tag=f"{uid}coT{ch}")
            for quad in range(4):
                ptr = pTR[quad % 2]
                for q4 in range(4):
                    pr = quad * 4 + q4
                    S.add("pe", lambda e, pr=pr, q4=q4, ptr=ptr, XB=XB: e.transpose(
                        ptr[:, q4 * 128:(q4 + 1) * 128], XB[:, pr, :], ceye),
                        reads=[XB[:, pr, :], ceye], writes=[ptr],
                        tag=f"{uid}tx{ch}.{pr}")
                for q4 in range(4):
                    pr = quad * 4 + q4
                    for par in range(2):
                        h = pr * 2 + par
                        dst = cxb[:, pr * 128 + par * 64: pr * 128 + (par + 1) * 64]
                        srcp = ptr[:, q4 * 128 + par * 64: q4 * 128 + (par + 1) * 64]
                        S.add("ve", lambda e, dst=dst, srcp=srcp, h=h: e.tensor_scalar(
                            dst, srcp, coTM[:, h:h + 1], None, op0=ALU.mult),
                            reads=[srcp, coTM[:, h:h + 1]], writes=[dst],
                            tag=f"{uid}cxb{ch}.{h}")
            bn_stats(z1e, f"{uid}bn{ch}")
            pmu = bcast(stat2[:, 0:256], f"{uid}bmu{ch}")
            prs = bcast(stat[:, 256:512], f"{uid}brs{ch}")
            S.add("ve", lambda e, pmu=pmu: e.tensor_tensor(t1s, z1e, pmu, op=ALU.subtract),
                  reads=[z1e, pmu], writes=[t1s], tag=f"{uid}xh1{ch}")
            S.add("ve", lambda e, prs=prs: e.tensor_tensor(xh, t1s, prs, op=ALU.mult),
                  reads=[t1s, prs], writes=[xh], tag=f"{uid}xh2{ch}")
            S.add("ve", lambda e: e.tensor_scalar(stat[:, 0:256], stat[:, 256:512], 1.0 / 64, None, op0=ALU.mult),
                  reads=[stat[:, 256:512]], writes=[stat[:, 0:256]], tag=f"{uid}r64{ch}")
            S.add("ve", lambda e: e.tensor_tensor(t1s, xh, gfm, op=ALU.mult),
                  reads=[xh, gfm], writes=[t1s], tag=f"{uid}y1{ch}")
            S.add("ve", lambda e: e.tensor_tensor(t2s, t1s, bfm, op=ALU.add),
                  reads=[t1s, bfm], writes=[t2s], tag=f"{uid}y2{ch}")
            S.add("ve", lambda e, DD=DD: e.tensor_tensor(
                t1s.rearrange("p (pr k) -> p pr k", k=16), t2s.rearrange("p (pr k) -> p pr k", k=16),
                DD, op=ALU.subtract),
                reads=[t2s, DD], writes=[t1s], tag=f"{uid}ymt{ch}")
            S.add("ve", lambda e: e.tensor_tensor(g0, t1s, gfm, op=ALU.mult),
                  reads=[t1s, gfm], writes=[g0], tag=f"{uid}g0{ch}")
            S.add("pe", lambda e: e.matmul(pR[:, 0:256], o2bd, g0, start=True, stop=True),
                  reads=[o2bd, g0], writes=[pR], tag=f"{uid}rsg{ch}")
            S.add("ve", lambda e: e.tensor_tensor(gxs, g0, xh, op=ALU.mult),
                  reads=[g0, xh], writes=[gxs], tag=f"{uid}gx{ch}")
            S.add("pe", lambda e: e.matmul(pR[:, 256:512], o2bd, gxs, start=True, stop=True),
                  reads=[o2bd, gxs], writes=[pR], tag=f"{uid}rgx{ch}")
            S.add("ve", lambda e: e.tensor_copy(stat2, pR), reads=[pR], writes=[stat2],
                  tag=f"{uid}sev2{ch}")
            psg = bcast(stat2[:, 0:256], f"{uid}bsg{ch}")
            psgx = bcast(stat2[:, 256:512], f"{uid}bsgx{ch}")
            S.add("ve", lambda e: e.tensor_scalar(t1s, g0, 64.0, None, op0=ALU.mult),
                  reads=[g0], writes=[t1s], tag=f"{uid}g64{ch}")
            S.add("ve", lambda e, psg=psg: e.tensor_tensor(t2s, t1s, psg, op=ALU.subtract),
                  reads=[t1s, psg], writes=[t2s], tag=f"{uid}gm1{ch}")
            S.add("ve", lambda e, psgx=psgx: e.tensor_tensor(t1s, xh, psgx, op=ALU.mult),
                  reads=[xh, psgx], writes=[t1s], tag=f"{uid}gm2{ch}")
            S.add("ve", lambda e: e.tensor_tensor(t2s, t2s, t1s, op=ALU.subtract),
                  reads=[t2s, t1s], writes=[t2s], tag=f"{uid}gm3{ch}")
            prs64 = bcast(stat[:, 0:256], f"{uid}brs64{ch}")
            S.add("ve", lambda e, prs64=prs64: e.tensor_tensor(grad, t2s, prs64, op=ALU.mult),
                  reads=[t2s, prs64], writes=[grad], tag=f"{uid}grad{ch}")
            S.add("ve", lambda e: e.tensor_tensor(cog, grad, pCO, op=ALU.mult),
                  reads=[grad, pCO], writes=[cog], tag=f"{uid}cgr{ch}")
            for half in range(2):
                ptr = pTR[half]
                for q4 in range(4):
                    pr = half * 4 + q4
                    S.add("pe", lambda e, pr=pr, q4=q4, ptr=ptr: e.transpose(
                        ptr[:, q4 * 128:(q4 + 1) * 128], grad[:, pr * 16:(pr + 1) * 16], ceye),
                        reads=[grad[:, pr * 16:(pr + 1) * 16], ceye], writes=[ptr],
                        tag=f"{uid}tg{ch}.{pr}")
                S.add("ve", lambda e, half=half, ptr=ptr: e.tensor_copy(
                    TMg[:, half * 512:(half + 1) * 512], ptr),
                    reads=[ptr], writes=[TMg[:, half * 512:(half + 1) * 512]],
                    tag=f"{uid}tgv{ch}.{half}")
            for half in range(2):
                ptr = pTR[half]
                for q4 in range(4):
                    pr = 8 + half * 4 + q4
                    S.add("pe", lambda e, pr=pr, q4=q4, ptr=ptr: e.transpose(
                        ptr[:, q4 * 128:(q4 + 1) * 128], grad[:, pr * 16:(pr + 1) * 16], ceye),
                        reads=[grad[:, pr * 16:(pr + 1) * 16], ceye], writes=[ptr],
                        tag=f"{uid}tg2{ch}.{pr}")
                S.add("ve", lambda e, half=half, ptr=ptr: e.tensor_copy(
                    TMg[:, 1024 + half * 512:1024 + (half + 1) * 512], ptr),
                    reads=[ptr], writes=[TMg[:, 1024 + half * 512:1024 + (half + 1) * 512]],
                    tag=f"{uid}tgv2{ch}.{half}")
            if "s" not in AT1_SECTS:
                continue
            for pr in range(16):
                S.add("pe", lambda e, pr=pr, XC=XC, stp=(AT1_LEVEL >= 1): e.matmul(
                    pZ[:, pr * 16:(pr + 1) * 16], w1s[:, pr * 128:(pr + 1) * 128],
                    XC[:, pr, :], start=True, stop=stp, skip_group_check=True),
                    reads=[w1s[:, pr * 128:(pr + 1) * 128], XC[:, pr, :]],
                    writes=[pZ], tag=f"{uid}s1.{ch}.{pr}")
                if AT1_LEVEL >= 1:
                    continue
                for par in range(2):
                    h = pr * 2 + par
                    tmslc = TMg[:, pr * 128 + par * 64: pr * 128 + (par + 1) * 64]
                    for ei, esrc in ((0, Etsb), (1, AEtsb)):
                        S.add("pe", lambda e, pr=pr, par=par, h=h, tmslc=tmslc,
                              esrc=esrc, ei=ei: e.matmul(
                            pZ[par * 64:(par + 1) * 64, pr * 16:(pr + 1) * 16],
                            tmslc, esrc[:, h * 16:(h + 1) * 16],
                            start=False, stop=(par == 1 and ei == 1),
                            skip_group_check=True),
                            reads=[tmslc, esrc[:, h * 16:(h + 1) * 16]],
                            writes=[pZ], tag=f"{uid}eg{ch}.{h}.{ei}")
            S.add("ve", lambda e: e.tensor_copy(zsb, pZ), reads=[pZ], writes=[zsb],
                  tag=f"{uid}zbv{ch}")
            S.add("ve", lambda e: e.tensor_tensor(zbe, zsb, b1s, op=ALU.add),
                  reads=[zsb, b1s], writes=[zbe], tag=f"{uid}zbb{ch}")
            for pr in range(16 if AT1_LEVEL < 1 else 0):
                b = pr % 2
                for par in range(2):
                    h = pr * 2 + par
                    cxslc = cxb[:, pr * 128 + par * 64: pr * 128 + (par + 1) * 64]
                    tmslc = TMg[:, pr * 128 + par * 64: pr * 128 + (par + 1) * 64]
                    S.add("pe", lambda e, cxslc=cxslc, tmslc=tmslc, par=par, b=b: e.matmul(
                        pW[b][par * 64:(par + 1) * 64, 0:64], cxslc, tmslc,
                        start=True, stop=True),
                        reads=[cxslc, tmslc], writes=[pW[b]], tag=f"{uid}wu{ch}.{h}")
                for par in range(2):
                    wslc = w1s[par * 64:(par + 1) * 64,
                               pr * 128 + par * 64: pr * 128 + (par + 1) * 64]
                    S.add("ve", lambda e, wslc=wslc, par=par, b=b: e.tensor_tensor(
                        wslc, wslc, pW[b][par * 64:(par + 1) * 64, 0:64], op=ALU.subtract),
                        reads=[wslc, pW[b]], writes=[wslc], tag=f"{uid}wus{ch}.{pr}.{par}")
            if "b" not in AT1_SECTS:
                continue
            S.add("ve", lambda e: e.reduce_sum(
                red3, cog.rearrange("p (pr k) -> p pr k", k=16), axis=AX.X),
                reads=[cog], writes=[red3], tag=f"{uid}b1r{ch}")
            for k in range(16):
                b1v = b1s.rearrange("p (pr k) -> p k pr", k=16)[:, k, :]
                S.add("ve", lambda e, b1v=b1v: e.tensor_tensor(b1v, b1v, red3, op=ALU.subtract),
                      reads=[b1v, red3], writes=[b1v], tag=f"{uid}b1u{ch}.{k}")
            if "f" not in AT1_SECTS:
                continue
            bn_stats(zbe, f"{uid}bf{ch}")
            pmu2 = bcast(stat2[:, 0:256], f"{uid}bmu2{ch}")
            prs2 = bcast(stat[:, 256:512], f"{uid}brs2{ch}")
            S.add("ve", lambda e, pmu2=pmu2: e.tensor_tensor(t1s, zbe, pmu2, op=ALU.subtract),
                  reads=[zbe, pmu2], writes=[t1s], tag=f"{uid}f1{ch}")
            S.add("ve", lambda e, prs2=prs2: e.tensor_tensor(t2s, t1s, prs2, op=ALU.mult),
                  reads=[t1s, prs2], writes=[t2s], tag=f"{uid}f2{ch}")
            S.add("ve", lambda e: e.tensor_tensor(t1s, t2s, gfm, op=ALU.mult),
                  reads=[t2s, gfm], writes=[t1s], tag=f"{uid}f3{ch}")
            S.add("ve", lambda e: e.tensor_tensor(t2s, t1s, bfm, op=ALU.add),
                  reads=[t1s, bfm], writes=[t2s], tag=f"{uid}f4{ch}")
            S.add("ve", lambda e, XC=XC: e.tensor_tensor(
                outf.rearrange("p (pr k) -> p pr k", k=16),
                t2s.rearrange("p (pr k) -> p pr k", k=16), XC, op=ALU.add),
                reads=[t2s, XC], writes=[outf], tag=f"{uid}fo{ch}")
            if AT1_DBG and ch == 0:
                for dnm, dap in (("dz1e", z1e), ("dgrad", grad), ("dcog", cog),
                                 ("dzsb", zsb), ("dzbe", zbe), ("dxh", xh),
                                 ("dg0", g0), ("dEt", Etsb), ("dAEt", AEtsb),
                                 ("dAT", ATsb), ("dTMg", TMg), ("dcxb", cxb)):
                    dd = W["dbg"][dnm]
                    S.add("sc", lambda e, dap=dap, dd=dd: e.dma_start(out=dd, in_=dap),
                          reads=[dap], writes=[dd], dma=f"{uid}dbg{dnm}", tag=f"{uid}dbg{dnm}")
            xw3 = xcwb.rearrange("p (pr t) -> p pr t", t=128)[:, :, cc * 16:(cc + 1) * 16]
            S.add("ve", lambda e, xw3=xw3: e.tensor_copy(
                xw3, outf.rearrange("p (pr k) -> p pr k", k=16)),
                reads=[outf], writes=[xw3], tag=f"{uid}xw{ch}")
        S.add("sc", lambda e, mg=mg: e.dma_start(
            out=sp3["x"][:, :, mg * 128:(mg + 1) * 128],
            in_=xcwb.rearrange("p (pr t) -> p pr t", t=128)),
            reads=[xcwb], writes=[sp3["x"][:, :, mg * 128:(mg + 1) * 128]],
            dma=f"{uid}xws", tag=f"{uid}xws{mg}")

    for g in range(NG):
        S.add("sy", lambda e, g=g: e.dma_start(
            out=xg[:, 0:16 * TG].rearrange("p (c t) -> p c t", t=TG),
            in_=x_in3[:, :, g * TG:(g + 1) * TG]),
            reads=[x_in3[:, :, g * TG:(g + 1) * TG]], writes=[xg[:, 0:16 * TG]],
            dma=f"{uid}xg", tag=f"{uid}ldx2{g}")
        S.add("sy", lambda e, g=g: e.dma_start(
            out=xn[:, 0:16 * TG].rearrange("p (c t) -> p c t", t=TG),
            in_=sp3["x"][:, :, g * TG:(g + 1) * TG]),
            reads=[sp3["x"][:, :, g * TG:(g + 1) * TG]], writes=[xn[:, 0:16 * TG]],
            dma=f"{uid}xcwl", tag=f"{uid}ldxc{g}")
        wm = W["wom1"]
        for o in range(16):
            b = o % 2
            wb = t[f"wgb{b}"].ap()
            S.add("sy", lambda e, wb=wb, o=o, wm=wm: e.dma_start(out=wb, in_=wm[o]),
                  reads=[wm[o]], writes=[wb], dma=f"{uid}w{b}", tag=f"{uid}ldwo{g}.{o}")
            for c in range(16):
                xnc = xn[:, c * TG:(c + 1) * TG]
                ws = wb[:, c * 128:(c + 1) * 128]
                S.add("pe", lambda e, ws=ws, xnc=xnc, c=c, b=b: e.matmul(
                    p_pr[b], ws, xnc, start=(c == 0), stop=(c == 15)),
                    reads=[ws, xnc], writes=[p_pr[b]], tag=f"{uid}om{g}.{o}.{c}")
            oslot = o % 2
            outr = t[f"outr{oslot}"].ap()[:, 0:TG]
            tsc = t["tscr"].ap()[:, 0:TG]
            S.add("ve", lambda e, b=b, tsc=tsc: e.tensor_copy(tsc, p_pr[b]),
                  reads=[p_pr[b]], writes=[tsc], tag=f"{uid}oev{g}.{o}")
            S.add("ve", lambda e, o=o, outr=outr, tsc=tsc: e.tensor_tensor(
                outr, tsc, xg[:, o * TG:(o + 1) * TG], op=ALU.add),
                reads=[tsc, xg[:, o * TG:(o + 1) * TG]], writes=[outr],
                tag=f"{uid}ores{g}.{o}")
            S.add("sc", lambda e, o=o, g=g, outr=outr, oslot=oslot: e.dma_start(
                out=x_out3[:, o, g * TG:(g + 1) * TG], in_=outr),
                reads=[outr], writes=[x_out3[:, o, g * TG:(g + 1) * TG]],
                dma=f"{uid}st{oslot}", tag=f"{uid}ostx{g}.{o}")


def pack_attn1_weights(Wq, Wk, Wv, Wo, lw, lb, ln_w, ln_b, W1i, b1i):
    d = {}
    d["wqm1"], d["wkm1"], d["wvm1"], d["wom1"] = (pack_proj(w) for w in (Wq, Wk, Wv, Wo))
    lwm = np.asarray(lw)[:, 0, :]
    d["lwt"] = np.ascontiguousarray(lwm.T.reshape(16, 128, 32).transpose(1, 0, 2)).reshape(128, 512)
    d["lbT"] = np.ascontiguousarray(np.asarray(lb).reshape(1, 32))
    gfm = np.zeros((128, 256), np.float32)
    bfm = np.zeros((128, 256), np.float32)
    b1fm = np.zeros((128, 256), np.float32)
    for par in range(2):
        for pr in range(16):
            h = 2 * pr + par
            gfm[par * 64:(par + 1) * 64, pr * 16:(pr + 1) * 16] = np.asarray(ln_w)[h][:, None]
            bfm[par * 64:(par + 1) * 64, pr * 16:(pr + 1) * 16] = np.asarray(ln_b)[h][:, None]
            b1fm[par * 64:(par + 1) * 64, pr * 16:(pr + 1) * 16] = np.asarray(b1i)[h, 0][:, None]
    d["gfm"], d["bfm"], d["b1fm"] = gfm, bfm, b1fm
    w1bd = np.zeros((16, 128, 128), np.float32)
    for pr in range(16):
        for par in range(2):
            w1bd[pr, par * 64:(par + 1) * 64, par * 64:(par + 1) * 64] = np.asarray(W1i)[2 * pr + par]
    d["w1bd"] = w1bd
    a = (1.0 / (64.0 * np.arange(1, 17, dtype=np.float64))).astype(np.float32)
    maskT = np.triu(np.ones((16, 16), np.float32))
    d["mkTnA"] = np.ascontiguousarray(np.tile(-(a[:, None]) * maskT, (1, 32)))
    d["csel"] = np.ascontiguousarray(np.kron(np.eye(32, dtype=np.float32), np.ones((1, 16), np.float32)))
    hind = np.zeros((2, 128), np.float32)
    hind[0, 0:64] = 1.0
    hind[1, 64:128] = 1.0
    d["hind"] = hind
    d["o2bd"] = np.ascontiguousarray(hind.T)
    d["a2c"] = np.ascontiguousarray(np.tile(a, (2, 16)).reshape(2, 256))
    d["aTMc"] = np.ascontiguousarray(a.reshape(16, 1))
    return d


def declare_attn1_weights(nc, prefix=""):
    d = {}
    shapes = dict(wqm1=[16, 128, 2048], wkm1=[16, 128, 2048], wvm1=[16, 128, 2048],
                  wom1=[16, 128, 2048], lwt=[128, 512], lbT=[1, 32],
                  gfm=[128, 256], bfm=[128, 256], b1fm=[128, 256],
                  w1bd=[16, 128, 128], mkTnA=[16, 512], csel=[32, 512],
                  hind=[2, 128], o2bd=[128, 2], a2c=[2, 256], aTMc=[16, 1])
    for n, s in shapes.items():
        d[n] = nc.dram_tensor(prefix + n + "_d", s, F32, kind="ExternalInput").ap()
    return d


def build_attn1_program():
    nc = bass.Bass("TRN2", target_bir_lowering=False)
    Lx = AT1_NG * AT1_TG
    x_in = nc.dram_tensor("x_in", [D, Lx], F32, kind="ExternalInput").ap()
    x_out = nc.dram_tensor("x_out", [D, Lx], F32, kind="ExternalOutput").ap()
    qs = nc.dram_tensor("q_s", [D, Lx], F32, kind="Internal").ap()
    ks = nc.dram_tensor("k_s", [D, Lx], F32, kind="Internal").ap()
    vs = nc.dram_tensor("v_s", [D, Lx], F32, kind="Internal").ap()
    xcws = nc.dram_tensor("xcw_s", [D, Lx], F32, kind="Internal").ap()
    sd = nc.dram_tensor("sd_s", [32, Lx], F32, kind="Internal").ap()
    sdT = nc.dram_tensor("sdT_s", [Lx, 32], F32, kind="Internal").ap()
    W = declare_attn1_weights(nc)
    W["sd_ap"] = sd
    W["sdT_ap"] = sdT
    if AT1_DBG:
        W["dbg"] = {}
        for dnm, shp in (("dz1e", [128, 256]), ("dgrad", [128, 256]), ("dcog", [128, 256]),
                         ("dzsb", [128, 256]), ("dzbe", [128, 256]), ("dxh", [128, 256]),
                         ("dg0", [128, 256]), ("dEt", [16, 512]), ("dAEt", [16, 512]),
                         ("dAT", [16, 512]), ("dTMg", [16, 2048]), ("dcxb", [16, 2048])):
            W["dbg"][dnm] = nc.dram_tensor(dnm, shp, F32, kind="Internal").ap()
    cones_d = nc.dram_tensor("cones_d", [128, 128], F32, kind="ExternalInput").ap()
    ceye_d = nc.dram_tensor("ceye_d", [128, 128], F32, kind="ExternalInput").ap()
    ctx = ExitStack()
    with ctx:
        A = Arena(nc, ctx)
        A.sb("cones", [128, 128])
        A.sb("ceye", [128, 128])
        alloc_shared(A)
        alloc_attn1(A)
        P = [A.ps(f"pb{i}", [128, 512]) for i in range(8)]
        S = Sched(nc, psum_names=[f"pb{i}" for i in range(8)])
        S.add("sy", lambda e: e.dma_start(out=A.t["cones"].ap(), in_=cones_d),
              writes=[A.t["cones"].ap()], dma="cones")
        S.add("sy", lambda e: e.dma_start(out=A.t["ceye"].ap(), in_=ceye_d),
              writes=[A.t["ceye"].ap()], dma="ceye")
        emit_consts(S, A)
        emit_attn1(S, A, P, W, x_in, x_out, qs, ks, vs, xcws, sd, sdT, uid="t")
        n = S.finalize(ctx)
    return nc, n


def host_pack_x(x_td):
    return np.ascontiguousarray(np.asarray(x_td).T)


def build_full_program():
    nc = bass.Bass("TRN2", target_bir_lowering=False)
    x_in = nc.dram_tensor("x_in", [D, L], F32, kind="ExternalInput").ap()
    x_out = nc.dram_tensor("x_out", [D, L], F32, kind="ExternalOutput").ap()
    xa = nc.dram_tensor("xa_s", [D, L], F32, kind="Internal").ap()
    xb = nc.dram_tensor("xb_s", [D, L], F32, kind="Internal").ap()
    qs = nc.dram_tensor("q_s", [D, L], F32, kind="Internal").ap()
    ks = nc.dram_tensor("k_s", [D, L], F32, kind="Internal").ap()
    vs = nc.dram_tensor("v_s", [D, L], F32, kind="Internal").ap()
    os_ = nc.dram_tensor("o_s", [D, L], F32, kind="Internal").ap()
    sd = nc.dram_tensor("sd_s", [32, L], F32, kind="Internal").ap()
    sdT = nc.dram_tensor("sdT_s", [L, 32], F32, kind="Internal").ap()
    Wf = declare_ffn0_weights(nc)
    Wa = declare_attn0_weights(nc)
    Wm = declare_moe1_weights(nc)
    Wt = declare_attn1_weights(nc)
    Wt["sd_ap"] = sd
    Wt["sdT_ap"] = sdT
    cones_d = nc.dram_tensor("cones_d", [128, 128], F32, kind="ExternalInput").ap()
    conesb_d = nc.dram_tensor("conesb_d", [128, 128], BF16, kind="ExternalInput").ap()
    ceye_d = nc.dram_tensor("ceye_d", [128, 128], F32, kind="ExternalInput").ap()
    ctx = ExitStack()
    with ctx:
        A = Arena(nc, ctx)
        A.sb("cones", [128, 128])
        A.sb("ceye", [128, 128])
        alloc_shared(A)
        alloc_ffn0(A)
        alloc_moe1(A)
        alloc_attn1(A)
        P = [A.ps(f"pb{i}", [128, 512]) for i in range(8)]
        S = Sched(nc, psum_names=[f"pb{i}" for i in range(8)])
        S.add("sy", lambda e: e.dma_start(out=A.t["cones"].ap(), in_=cones_d),
              writes=[A.t["cones"].ap()], dma="cones")
        S.add("sy", lambda e: e.dma_start(out=A.t["conesb"].ap(), in_=conesb_d),
              writes=[A.t["conesb"].ap()], dma="conesb")
        S.add("sy", lambda e: e.dma_start(out=A.t["ceye"].ap(), in_=ceye_d),
              writes=[A.t["ceye"].ap()], dma="ceye")
        emit_consts(S, A)
        emit_ffn0(S, A, P, Wf, x_in, xa, uid="f")
        emit_attn0(S, A, P, Wa, xa, xb, qs, ks, vs, os_, uid="a")
        emit_ffn0(S, A, P, Wf, xb, xa, uid="f")
        emit_moe1(S, A, P, Wm, xa, xb, uid="m")
        emit_attn1(S, A, P, Wt, xb, xa, qs, ks, vs, os_, sd, sdT, uid="t")
        emit_moe1(S, A, P, Wm, xa, x_out, uid="m")
        n = S.finalize(ctx)
    return nc, n


_PROGS = {}


def _get_progs():
    if not _PROGS:
        _PROGS["ffn0"] = build_ffn0_program()
        _PROGS["attn0"] = build_attn0_program()
        _PROGS["moe1"] = build_moe1_program()
        _PROGS["attn1"] = build_attn1_program()
    return _PROGS


def _pack_all(z):
    import ml_dtypes
    cones = np.ones((128, 128), np.float32)
    conesb = np.ones((128, 128), dtype=ml_dtypes.bfloat16)
    ceye = np.eye(128, dtype=np.float32)
    Wf = pack_ffn0_weights(z["L0_ffn_gate_w"], z["L0_ffn_gate_scale"],
                           z["L0_ffn_up_w"], z["L0_ffn_up_scale"],
                           z["L0_ffn_down_w"], z["L0_ffn_down_scale"])
    Wa = pack_attn0_weights(z["L0_attn_Wq"], z["L0_attn_bq"], z["L0_attn_Bq"], z["L0_attn_Aq"],
                            z["L0_attn_Wk"], z["L0_attn_bk"], z["L0_attn_Bk"], z["L0_attn_Ak"],
                            z["L0_attn_Wv"], z["L0_attn_bv"], z["L0_attn_Bv"], z["L0_attn_Av"],
                            z["L0_attn_Wo"], z["L0_attn_bo"], z["L0_attn_Bo"], z["L0_attn_Ao"])
    Wm = pack_moe1_weights(z["L1_ffn_Wr"], z["L1_ffn_We"], z["L1_ffn_be"])
    Wt = pack_attn1_weights(z["L1_attn_Wq"], z["L1_attn_Wk"], z["L1_attn_Wv"], z["L1_attn_Wo"],
                            z["L1_attn_linear_weight"], z["L1_attn_linear_bias"],
                            z["L1_attn_ln_weight"], z["L1_attn_ln_bias"],
                            z["L1_attn_W1_init"], z["L1_attn_b1_init"])
    base = dict(cones_d=cones, conesb_d=conesb, ceye_d=ceye, **Wf)
    base.update(wqm=Wa["wqm"], wkm=Wa["wkm"], wvm=Wa["wvm"], wom=Wa["wom"],
                lb4_d=Wa["lb4"], aqT4_d=Wa["aqT4"], bb_d=Wa["bb"])
    base.update(**Wm)
    base.update({k + "_d": v for k, v in Wt.items()})
    return base


def kernel(**inputs):
    if "full" not in _PROGS:
        _PROGS["full"] = build_full_program()
    nc, _ = _PROGS["full"]
    base = _pack_all(inputs)
    x = [host_pack_x(np.asarray(inputs["L0_attn_x"][b], np.float32)) for b in range(NCORES)]
    maps = []
    for b in range(NCORES):
        m = dict(base)
        m["x_in"] = x[b]
        maps.append(m)
    res = run_bass_kernel_spmd(nc, maps, core_ids=list(range(NCORES)))
    out = np.stack([np.ascontiguousarray(np.asarray(r["x_out"]).T) for r in res.results])
    return out.astype(np.float32)


def kernel_unfused(**inputs):
    progs = _get_progs()
    z = inputs
    cones = np.ones((128, 128), np.float32)
    ceye = np.eye(128, dtype=np.float32)
    Wf = pack_ffn0_weights(z["L0_ffn_gate_w"], z["L0_ffn_gate_scale"],
                           z["L0_ffn_up_w"], z["L0_ffn_up_scale"],
                           z["L0_ffn_down_w"], z["L0_ffn_down_scale"])
    Wa = pack_attn0_weights(z["L0_attn_Wq"], z["L0_attn_bq"], z["L0_attn_Bq"], z["L0_attn_Aq"],
                            z["L0_attn_Wk"], z["L0_attn_bk"], z["L0_attn_Bk"], z["L0_attn_Ak"],
                            z["L0_attn_Wv"], z["L0_attn_bv"], z["L0_attn_Bv"], z["L0_attn_Av"],
                            z["L0_attn_Wo"], z["L0_attn_bo"], z["L0_attn_Bo"], z["L0_attn_Ao"])
    Wm = pack_moe1_weights(z["L1_ffn_Wr"], z["L1_ffn_We"], z["L1_ffn_be"])
    Wt = pack_attn1_weights(z["L1_attn_Wq"], z["L1_attn_Wk"], z["L1_attn_Wv"], z["L1_attn_Wo"],
                            z["L1_attn_linear_weight"], z["L1_attn_linear_bias"],
                            z["L1_attn_ln_weight"], z["L1_attn_ln_bias"],
                            z["L1_attn_W1_init"], z["L1_attn_b1_init"])
    x = [host_pack_x(np.asarray(z["L0_attn_x"][b], np.float32)) for b in range(NCORES)]

    def launch(key, extra, xs):
        nc, _ = progs[key]
        maps = []
        for b in range(NCORES):
            m = dict(extra)
            m["x_in"] = xs[b]
            maps.append(m)
        res = run_bass_kernel_spmd(nc, maps, core_ids=list(range(NCORES)))
        return [np.asarray(r["x_out"]) for r in res.results]

    ef = dict(cones_d=cones, **Wf)
    ea = dict(cones_d=cones, ceye_d=ceye,
              wqm=Wa["wqm"], wkm=Wa["wkm"], wvm=Wa["wvm"], wom=Wa["wom"],
              lb4_d=Wa["lb4"], aqT4_d=Wa["aqT4"], bb_d=Wa["bb"])
    em = dict(cones_d=cones, ceye_d=ceye, **Wm)
    et = {k + "_d": v for k, v in Wt.items()}
    et.update(cones_d=cones, ceye_d=ceye)

    x = launch("ffn0", ef, x)
    x = launch("attn0", ea, x)
    x = launch("ffn0", ef, x)
    x = launch("moe1", em, x)
    x = launch("attn1", et, x)
    x = launch("moe1", em, x)
    out = np.stack([np.ascontiguousarray(xb.T) for xb in x])
    return out.astype(np.float32)


# version 06
# speedup vs baseline: 1.2371x; 1.2371x over previous
import numpy as np
from contextlib import ExitStack

import concourse.bass as bass
import concourse.mybir as mybir
from concourse.bass_utils import run_bass_kernel_spmd

F32 = mybir.dt.float32
BF16 = mybir.dt.bfloat16
I32 = mybir.dt.int32
U8C = mybir.dt.uint8
AF = mybir.ActivationFunctionType
ALU = mybir.AluOpType
AX = mybir.AxisListType

D = 2048
L = 2048
NCORES = 8


class _Sem:
    __slots__ = ("name", "h", "count")
    def __init__(self, name):
        self.name = name
        self.h = None
        self.count = 0


class _Op:
    __slots__ = ("idx", "stream", "emit", "deps", "is_dma", "sem", "inc", "cum", "tag")


def _ap_interval(ap):
    t = ap.tensor
    lo = ap.offset
    hi = ap.offset
    for step, count in ap.ap:
        hi += step * (count - 1)
    return t.name, min(lo, hi), max(lo, hi) + 1


class Sched:
    STREAMS = ("pe", "ve", "sc", "gp", "sy")

    def __init__(self, nc, psum_names=()):
        self.nc = nc
        self.ops = []
        self.sems = {}
        self.psum_names = set(psum_names)
        self.mem = {}

    def dsem(self, name):
        if name not in self.sems:
            self.sems[name] = _Sem(name)
        return name

    def _access(self, ap, op, is_write, deps):
        name, lo, hi = _ap_interval(ap)
        if name in self.psum_names:
            lo, hi = 0, 1 << 40
        rec = self.mem.setdefault(name, {"w": [], "r": []})
        if is_write:
            for (l2, h2, o2) in rec["w"]:
                if l2 < hi and lo < h2:
                    deps.append(o2)
            for (l2, h2, o2) in rec["r"]:
                if l2 < hi and lo < h2:
                    deps.append(o2)
            rec["w"] = [(l2, h2, o2) for (l2, h2, o2) in rec["w"]
                        if not (lo <= l2 and h2 <= hi)]
            rec["r"] = [(l2, h2, o2) for (l2, h2, o2) in rec["r"]
                        if not (l2 < hi and lo < h2)]
            rec["w"].append((lo, hi, op))
        else:
            for (l2, h2, o2) in rec["w"]:
                if l2 < hi and lo < h2:
                    deps.append(o2)
            rec["r"].append((lo, hi, op))

    def add(self, stream, emit, reads=(), writes=(), deps=(), dma=None, tag=""):
        op = _Op()
        op.idx = len(self.ops)
        op.stream = stream
        op.emit = emit
        op.is_dma = dma is not None
        op.sem = self.dsem(dma) if dma is not None else None
        op.inc = False
        op.cum = None
        op.tag = tag
        dd = [d for d in deps if d is not None]
        for ap in reads:
            self._access(ap, op, False, dd)
        for ap in writes:
            self._access(ap, op, True, dd)
        seen = set()
        op.deps = []
        for d in dd:
            if d is not op and id(d) not in seen:
                seen.add(id(d))
                assert d.idx < op.idx, f"forward dep at op {op.idx} ({tag})"
                op.deps.append(d)
        self.ops.append(op)
        return op

    def finalize(self, ctx):
        nc = self.nc
        for op in self.ops:
            for d in op.deps:
                if d.is_dma:
                    continue
                if d.stream == "pe" and op.stream == "pe" and not op.is_dma:
                    continue
                d.inc = True
        for op in self.ops:
            if op.is_dma:
                s = self.sems[op.sem]
                s.count += 16
                op.cum = s.count
            elif op.inc:
                sname = "e_" + op.stream
                op.sem = self.dsem(sname)
                s = self.sems[sname]
                s.count += 1
                op.cum = s.count
        for s in self.sems.values():
            s.h = ctx.enter_context(nc.semaphore(s.name))
        per = {st: [op for op in self.ops if op.stream == st] for st in self.STREAMS}
        block = ctx.enter_context(nc.Block())
        waited = {st: {} for st in self.STREAMS}

        def run_stream(st, eng):
            w = waited[st]
            for op in per[st]:
                for d in op.deps:
                    if d.cum is None:
                        continue
                    if (not d.is_dma) and d.stream == "pe" and st == "pe" and not op.is_dma:
                        continue
                    if w.get(d.sem, 0) < d.cum:
                        eng.wait_ge(self.sems[d.sem].h, d.cum)
                        w[d.sem] = d.cum
                inst = op.emit(eng)
                if op.is_dma:
                    inst.then_inc(self.sems[op.sem].h, 16)
                elif op.inc:
                    inst.then_inc(self.sems[op.sem].h, 1)
            if st == "sy":
                for s in self.sems.values():
                    if not s.name.startswith("e_") and s.count > 0:
                        if w.get(s.name, 0) < s.count:
                            eng.wait_ge(s.h, s.count)
                            w[s.name] = s.count

        for st, attr in (("pe", "tensor"), ("ve", "vector"), ("sc", "scalar"),
                         ("gp", "gpsimd"), ("sy", "sync")):
            if per[st] or st == "sy":
                getattr(block, attr)(lambda eng, _st=st: run_stream(_st, eng))
        return len(self.ops)


class Arena:
    def __init__(self, nc, ctx):
        self.nc = nc
        self.ctx = ctx
        self.t = {}

    def sb(self, name, shape, dtype=F32):
        h = self.ctx.enter_context(self.nc.sbuf_tensor(name, shape, dtype))
        self.t[name] = h
        return h

    def ps(self, name, shape, dtype=F32):
        h = self.ctx.enter_context(self.nc.psum_tensor(name, shape, dtype))
        self.t[name] = h
        return h



FFN_TG = 256
FFN_NG = L // FFN_TG
FFN_NI = 64


def alloc_shared(A):
    A.sb("conesb", [128, 128], BF16)
    A.sb("xg", [128, 16 * 512])
    A.sb("xn", [128, 16 * 512])
    A.sb("slab", [128, 64 * 256])
    for b in range(2):
        A.sb(f"wgb{b}", [128, 2048])
        A.sb(f"wub{b}", [128, 2048])
        A.sb(f"wdb{b}", [128, 2048])
    A.sb("rowsA", [1, 512])
    A.sb("rowsB", [1, 512])
    A.sb("rowsC", [1, 512])
    A.sb("rowsD", [1, 512])
    A.sb("epsb", [1, 1])
    A.sb("tscr", [128, 2 * 512])
    for b in range(2):
        A.sb(f"outr{b}", [128, 512])


def alloc_ffn0(A):
    A.sb("sgt", [128, 64])
    A.sb("sut", [128, 64])
    A.sb("sdt", [128, 16])


def emit_consts(S, A):
    eps = A.t["epsb"].ap()
    S.add("ve", lambda e: e.memset(eps, 1e-5), writes=[eps], tag="eps")


_ROWMAP = {0: "rowsA", 1: "rowsD", 2: "rowsC", 3: "rowsD", 4: "rowsC",
           5: "rowsB", 6: "rowsC"}


def row_slicer(A, TG):
    def R(i):
        return A.t[_ROWMAP[i]].ap()[0:1, 0:TG]
    return R


def emit_ln(S, A, P, xg, xn, TG, tag, split_hi=None, split_lo=None):
    t = A.t
    cones = t["cones"].ap()
    eps = t["epsb"].ap()
    sq = t["tscr"].ap()
    tscr = t["tscr"].ap()
    R = row_slicer(A, TG)

    p_sx = P[6].ap()[0:1, 0:TG]
    p_sq = P[7].ap()[0:1, 0:TG]
    p_mb = P[4].ap()[:, 0:TG]
    p_rb = P[5].ap()[:, 0:TG]

    for c in range(16):
        xgc = xg[:, c * TG:(c + 1) * TG]
        sqs = sq[:, (c % 2) * TG:(c % 2 + 1) * TG]
        S.add("pe", lambda e, xgc=xgc, c=c: e.matmul(
            p_sx, cones[:, 0:1], xgc, start=(c == 0), stop=(c == 15)),
            reads=[cones[:, 0:1], xgc], writes=[p_sx], tag=f"{tag}sx{c}")
        S.add("sc", lambda e, xgc=xgc, sqs=sqs: e.activation(
            out=sqs, in_=xgc, func=AF.Square),
            reads=[xgc], writes=[sqs], tag=f"{tag}sq{c}")
        S.add("pe", lambda e, sqs=sqs, c=c: e.matmul(
            p_sq, cones[:, 0:1], sqs, start=(c == 0), stop=(c == 15)),
            reads=[cones[:, 0:1], sqs], writes=[p_sq], tag=f"{tag}sqm{c}")
    S.add("ve", lambda e: e.tensor_scalar(R(0), p_sx, -1.0 / D, None, op0=ALU.mult),
          reads=[p_sx], writes=[R(0)], tag=f"{tag}negm")
    S.add("ve", lambda e: e.tensor_scalar(R(1), p_sq, 1.0 / D, None, op0=ALU.mult),
          reads=[p_sq], writes=[R(1)], tag=f"{tag}ex2")
    S.add("ve", lambda e: e.tensor_tensor(R(2), R(0), R(0), op=ALU.mult),
          reads=[R(0)], writes=[R(2)], tag=f"{tag}m2")
    S.add("ve", lambda e: e.tensor_tensor(R(3), R(1), R(2), op=ALU.subtract),
          reads=[R(1), R(2)], writes=[R(3)], tag=f"{tag}var")
    S.add("sc", lambda e: e.activation(out=R(4), in_=R(3), func=AF.Sqrt, bias=eps),
          reads=[R(3), eps], writes=[R(4)], tag=f"{tag}std")
    S.add("ve", lambda e: e.reciprocal(R(5), R(4)),
          reads=[R(4)], writes=[R(5)], tag=f"{tag}r")
    S.add("pe", lambda e: e.matmul(p_mb, cones[0:1, :], R(0), start=True, stop=True),
          reads=[cones[0:1, :], R(0)], writes=[p_mb], tag=f"{tag}mb")
    S.add("pe", lambda e: e.matmul(p_rb, cones[0:1, :], R(5), start=True, stop=True),
          reads=[cones[0:1, :], R(5)], writes=[p_rb], tag=f"{tag}rb")
    for c in range(16):
        xgc = xg[:, c * TG:(c + 1) * TG]
        tsl = tscr[:, (c % 2) * TG:(c % 2 + 1) * TG]
        if split_hi is None:
            xnc = xn[:, c * TG:(c + 1) * TG]
        else:
            xnc = tscr[:, (2 + c % 2) * TG:(3 + c % 2) * TG]
        S.add("ve", lambda e, xgc=xgc, tsl=tsl: e.tensor_tensor(
            tsl, xgc, p_mb, op=ALU.add),
            reads=[xgc, p_mb], writes=[tsl], tag=f"{tag}xna{c}")
        S.add("ve", lambda e, xnc=xnc, tsl=tsl: e.tensor_tensor(
            xnc, tsl, p_rb, op=ALU.mult),
            reads=[tsl, p_rb], writes=[xnc], tag=f"{tag}xnb{c}")
        if split_hi is not None:
            hic = split_hi[:, c * TG:(c + 1) * TG]
            loc = split_lo[:, c * TG:(c + 1) * TG]
            S.add("sc", lambda e, xnc=xnc, hic=hic: e.copy(out=hic, in_=xnc),
                  reads=[xnc], writes=[hic], tag=f"{tag}xh{c}")
            S.add("ve", lambda e, xnc=xnc, hic=hic, loc=loc: e.tensor_tensor(
                loc, xnc, hic, op=ALU.subtract),
                reads=[xnc, hic], writes=[loc], tag=f"{tag}xl{c}")


def emit_ffn0(S, A, P, W, x_in, x_out, uid):
    t = A.t
    TG, NG, NI = FFN_TG, FFN_NG, FFN_NI
    xg, xn = t["xg"].ap(), t["xn"].ap()
    xnb = xn.bitcast(BF16)
    xnh = xnb[:, 0:16 * TG]
    xnl = xnb[:, 16 * TG:32 * TG]
    slabb = t["slab"].ap().bitcast(BF16)
    hhh = slabb[:, 0:NI * TG]
    hhl = slabb[:, NI * TG:2 * NI * TG]
    conesb = t["conesb"].ap()
    cones = t["cones"].ap()
    sgt, sut, sdt = t["sgt"].ap(), t["sut"].ap(), t["sdt"].ap()
    gs = xn[:, 4096:4096 + TG]
    gsil = xn[:, 4352:4352 + TG]
    us = xn[:, 4608:4608 + TG]
    tscr = t["tscr"].ap()
    R = row_slicer(A, TG)

    x_in3 = x_in.rearrange("(c p) t -> p c t", p=128)
    x_out3 = x_out.rearrange("(c p) t -> p c t", p=128)

    S.add("sy", lambda e: e.dma_start(out=sgt, in_=W["sg"]), writes=[sgt], dma=f"{uid}p0")
    S.add("sy", lambda e: e.dma_start(out=sut, in_=W["su"]), writes=[sut], dma=f"{uid}p1")
    S.add("sy", lambda e: e.dma_start(out=sdt, in_=W["sd"]), writes=[sdt], dma=f"{uid}p2")

    p_g = [P[0].ap()[:, 0:TG], P[1].ap()[:, 0:TG]]
    p_u = [P[2].ap()[:, 0:TG], P[3].ap()[:, 0:TG]]
    p_mb = P[4].ap()[:, 0:TG]
    p_rb = P[5].ap()[:, 0:TG]
    p_d = [P[4].ap()[:, 0:TG], P[5].ap()[:, 0:TG]]
    p_sx = P[6].ap()[0:1, 0:TG]
    p_cs = P[6].ap()[0:1, 256:256 + TG]
    p_sq = P[7].ap()[0:1, 0:TG]

    for g in range(NG):
        S.add("sy", lambda e, g=g: e.dma_start(
            out=xg[:, 0:16 * TG].rearrange("p (c t) -> p c t", t=TG),
            in_=x_in3[:, :, g * TG:(g + 1) * TG]),
            reads=[x_in3[:, :, g * TG:(g + 1) * TG]], writes=[xg[:, 0:16 * TG]],
            dma=f"{uid}xg", tag=f"ldx{g}")
        emit_ln(S, A, P, xg, xn, TG, tag=f"{uid}ln{g}", split_hi=xnh, split_lo=xnl)
        def ldcast(i):
            b = i % 2
            wgb = t[f"wgb{b}"].ap().bitcast(BF16)[:, 0:2048]
            wub = t[f"wub{b}"].ap().bitcast(BF16)[:, 0:2048]
            stg = t[f"wgb{b}"].ap().bitcast(U8C)[:, 4096:8192]
            S.add("sy", lambda e, stg=stg, i=i: e.dma_start(out=stg, in_=W["wgm"][i]),
                  reads=[W["wgm"][i]], writes=[stg], dma=f"{uid}wg{b}", tag=f"ldwg{g}.{i}")
            S.add("ve", lambda e, wgb=wgb, stg=stg: e.tensor_copy(wgb, stg[:, 0:2048]),
                  reads=[stg[:, 0:2048]], writes=[wgb], tag=f"cvg{g}.{i}")
            S.add("sc", lambda e, wub=wub, stg=stg: e.copy(out=wub, in_=stg[:, 2048:4096]),
                  reads=[stg[:, 2048:4096]], writes=[wub], tag=f"cvu{g}.{i}")

        def gumms(i):
            b = i % 2
            wgb = t[f"wgb{b}"].ap().bitcast(BF16)[:, 0:2048]
            wub = t[f"wub{b}"].ap().bitcast(BF16)[:, 0:2048]
            for hl, xsrc in ((0, xnh), (1, xnl)):
                for c in range(16):
                    xnc = xsrc[:, c * TG:(c + 1) * TG]
                    ws = wgb[:, c * 128:(c + 1) * 128]
                    S.add("pe", lambda e, ws=ws, xnc=xnc, c=c, b=b, hl=hl: e.matmul(
                        p_g[b], ws, xnc, start=(hl == 0 and c == 0), stop=(hl == 1 and c == 15)),
                        reads=[ws, xnc], writes=[p_g[b]], tag=f"mmg{g}.{i}.{hl}.{c}")
            for hl, xsrc in ((0, xnh), (1, xnl)):
                for c in range(16):
                    xnc = xsrc[:, c * TG:(c + 1) * TG]
                    ws = wub[:, c * 128:(c + 1) * 128]
                    S.add("pe", lambda e, ws=ws, xnc=xnc, c=c, b=b, hl=hl: e.matmul(
                        p_u[b], ws, xnc, start=(hl == 0 and c == 0), stop=(hl == 1 and c == 15)),
                        reads=[ws, xnc], writes=[p_u[b]], tag=f"mmu{g}.{i}.{hl}.{c}")

        def guevac(i):
            b = i % 2
            S.add("ve", lambda e, i=i, b=b: e.tensor_scalar(
                gs, p_g[b], sgt[:, i:i + 1], None, op0=ALU.mult),
                reads=[p_g[b], sgt[:, i:i + 1]], writes=[gs], tag=f"gs{g}.{i}")
            S.add("sc", lambda e: e.activation(out=gsil, in_=gs, func=AF.Silu),
                  reads=[gs], writes=[gsil], tag=f"sil{g}.{i}")
            S.add("ve", lambda e, i=i, b=b: e.tensor_scalar(
                us, p_u[b], sut[:, i:i + 1], None, op0=ALU.mult),
                reads=[p_u[b], sut[:, i:i + 1]], writes=[us], tag=f"us{g}.{i}")
            S.add("ve", lambda e: e.tensor_tensor(gs, gsil, us, op=ALU.mult),
                  reads=[gsil, us], writes=[gs], tag=f"h{g}.{i}")
            hih = hhh[:, i * TG:(i + 1) * TG]
            hil = hhl[:, i * TG:(i + 1) * TG]
            S.add("sc", lambda e, hih=hih: e.copy(out=hih, in_=gs),
                  reads=[gs], writes=[hih], tag=f"hh{g}.{i}")
            S.add("ve", lambda e, hih=hih, hil=hil: e.tensor_tensor(
                hil, gs, hih, op=ALU.subtract),
                reads=[gs, hih], writes=[hil], tag=f"hl{g}.{i}")
            S.add("pe", lambda e, hih=hih, i=i: e.matmul(
                p_cs, conesb[:, 0:1], hih, start=(i == 0), stop=False),
                reads=[conesb[:, 0:1], hih], writes=[p_cs], tag=f"cs{g}.{i}")
            S.add("pe", lambda e, hil=hil, i=i: e.matmul(
                p_cs, conesb[:, 0:1], hil, start=False, stop=(i == NI - 1)),
                reads=[conesb[:, 0:1], hil], writes=[p_cs], tag=f"cs2{g}.{i}")

        for i in range(NI):
            ldcast(i)
            gumms(i)
            guevac(i)
        S.add("ve", lambda e: e.tensor_scalar(R(6), p_cs, -128.0, None, op0=ALU.mult),
              reads=[p_cs], writes=[R(6)], tag=f"csr{g}")
        NQ = (NI + 15) // 16
        for o in range(16):
            b = o % 2
            for qp in range(max(1, NQ // 2)):
                stg = t[f"wdb{qp % 2}"].ap().bitcast(U8C)[:, 4096:8192]
                S.add("sy", lambda e, stg=stg, o=o, qp=qp: e.dma_start(
                    out=stg, in_=W["wdm"][o, qp]),
                    reads=[W["wdm"][o, qp]], writes=[stg], dma=f"{uid}wd{qp % 2}",
                    tag=f"ldwd{g}.{o}.{qp}")
            for q in range(NQ):
                bb = q % 2
                wdb = t[f"wdb{bb}"].ap().bitcast(BF16)[:, 0:2048]
                srcq = t[f"wdb{q // 2}"].ap().bitcast(U8C)[:, 4096 + (q % 2) * 2048: 4096 + (q % 2 + 1) * 2048]
                eng = "ve" if q % 2 == 0 else "sc"
                if eng == "ve":
                    S.add("ve", lambda e, wdb=wdb, srcq=srcq: e.tensor_copy(wdb, srcq),
                          reads=[srcq], writes=[wdb], tag=f"cvd{g}.{o}.{q}")
                else:
                    S.add("sc", lambda e, wdb=wdb, srcq=srcq: e.copy(out=wdb, in_=srcq),
                          reads=[srcq], writes=[wdb], tag=f"cvd{g}.{o}.{q}")
                for ii in range(min(16, NI - q * 16)):
                    iglob = q * 16 + ii
                    ws = wdb[:, ii * 128:(ii + 1) * 128]
                    for hl, hsrc in ((0, hhh), (1, hhl)):
                        hsl = hsrc[:, iglob * TG:(iglob + 1) * TG]
                        S.add("pe", lambda e, ws=ws, hsl=hsl, q=q, ii=ii, b=b, hl=hl: e.matmul(
                            p_d[b], ws, hsl, start=(q == 0 and ii == 0 and hl == 0), stop=False),
                            reads=[ws, hsl], writes=[p_d[b]], tag=f"mmd{g}.{o}.{q}.{ii}.{hl}")
            S.add("pe", lambda e, b=b: e.matmul(
                p_d[b], cones[0:1, :], R(6), start=False, stop=True),
                reads=[cones[0:1, :], R(6)], writes=[p_d[b]], tag=f"rk{g}.{o}")
            oslot = o % 2
            outr = t[f"outr{oslot}"].ap()[:, 0:TG]
            xres = t[f"outr{oslot}"].ap()[:, 256:256 + TG]
            S.add("sc", lambda e, o=o, g=g, xres=xres: e.dma_start(
                out=xres, in_=x_in3[:, o, g * TG:(g + 1) * TG]),
                reads=[x_in3[:, o, g * TG:(g + 1) * TG]], writes=[xres],
                dma=f"{uid}xr{oslot}", tag=f"ldxr{g}.{o}")
            S.add("ve", lambda e, o=o, b=b: e.tensor_scalar(
                tscr[:, 0:TG], p_d[b], sdt[:, o:o + 1], 0.5, op0=ALU.mult, op1=ALU.mult),
                reads=[p_d[b], sdt[:, o:o + 1]], writes=[tscr[:, 0:TG]], tag=f"dts{g}.{o}")
            S.add("ve", lambda e, o=o, outr=outr, xres=xres: e.tensor_tensor(
                outr, tscr[:, 0:TG], xres, op=ALU.add),
                reads=[tscr[:, 0:TG], xres], writes=[outr],
                tag=f"res{g}.{o}")
            S.add("sc", lambda e, o=o, g=g, outr=outr: e.dma_start(
                out=x_out3[:, o, g * TG:(g + 1) * TG], in_=outr),
                reads=[outr], writes=[x_out3[:, o, g * TG:(g + 1) * TG]],
                dma=f"{uid}st{oslot}", tag=f"st{g}.{o}")



def pack_ffn0_weights(gate_w, gate_scale, up_w, up_scale, down_w, down_scale):
    g = np.ascontiguousarray(
        np.asarray(gate_w).reshape(64, 128, 16, 128).transpose(0, 3, 2, 1)).reshape(64, 128, 2048).astype(np.uint8)
    u = np.ascontiguousarray(
        np.asarray(up_w).reshape(64, 128, 16, 128).transpose(0, 3, 2, 1)).reshape(64, 128, 2048).astype(np.uint8)
    g = np.ascontiguousarray(np.concatenate([g, u], axis=2))
    dnw = np.asarray(down_w).reshape(16, 128, 64, 128)
    dn = np.ascontiguousarray(
        dnw.transpose(0, 2, 3, 1).reshape(16, 4, 16, 128, 128).transpose(0, 1, 3, 2, 4)
    ).reshape(16, 4, 128, 2048).astype(np.uint8)
    dn = np.ascontiguousarray(dn.reshape(16, 2, 128 * 0 + 2, 128, 2048).reshape(16, 2, 2, 128, 2048)
                              .transpose(0, 1, 3, 2, 4)).reshape(16, 2, 128, 4096)
    sg = np.ascontiguousarray(np.asarray(gate_scale).reshape(64, 128).T)
    su = np.ascontiguousarray(np.asarray(up_scale).reshape(64, 128).T)
    sd = np.ascontiguousarray(np.asarray(down_scale).reshape(16, 128).T)
    return dict(wgm=g, wdm=dn, sg=sg, su=su, sd=sd)


def declare_ffn0_weights(nc, prefix=""):
    d = {}
    U8 = mybir.dt.uint8
    d["wgm"] = nc.dram_tensor(prefix + "wgm", [64, 128, 4096], U8, kind="ExternalInput").ap()
    d["wdm"] = nc.dram_tensor(prefix + "wdm", [16, 2, 128, 4096], U8, kind="ExternalInput").ap()
    d["sg"] = nc.dram_tensor(prefix + "sg", [128, 64], F32, kind="ExternalInput").ap()
    d["su"] = nc.dram_tensor(prefix + "su", [128, 64], F32, kind="ExternalInput").ap()
    d["sd"] = nc.dram_tensor(prefix + "sd", [128, 16], F32, kind="ExternalInput").ap()
    return d


def build_ffn0_program():
    nc = bass.Bass("TRN2", target_bir_lowering=False)
    Lx = FFN_NG * FFN_TG
    x_in = nc.dram_tensor("x_in", [D, Lx], F32, kind="ExternalInput").ap()
    x_out = nc.dram_tensor("x_out", [D, Lx], F32, kind="ExternalOutput").ap()
    W = declare_ffn0_weights(nc)
    cones_d = nc.dram_tensor("cones_d", [128, 128], F32, kind="ExternalInput").ap()
    conesb_d = nc.dram_tensor("conesb_d", [128, 128], BF16, kind="ExternalInput").ap()
    ctx = ExitStack()
    with ctx:
        A = Arena(nc, ctx)
        A.sb("cones", [128, 128])
        alloc_shared(A)
        alloc_ffn0(A)
        P = [A.ps(f"pb{i}", [128, 512]) for i in range(8)]
        S = Sched(nc, psum_names=[f"pb{i}" for i in range(8)])
        S.add("sy", lambda e: e.dma_start(out=A.t["cones"].ap(), in_=cones_d),
              writes=[A.t["cones"].ap()], dma="cones")
        S.add("sy", lambda e: e.dma_start(out=A.t["conesb"].ap(), in_=conesb_d),
              writes=[A.t["conesb"].ap()], dma="conesb")
        emit_consts(S, A)
        emit_ffn0(S, A, P, W, x_in, x_out, uid="f")
        n = S.finalize(ctx)
    return nc, n





MOE_TG = 512
MOE_NG = L // MOE_TG


def alloc_moe1(A):
    A.sb("wrt", [128, 16 * 8])
    A.sb("gate", [128, 32])
    A.sb("gsc", [128, 32])
    A.sb("lsb", [128, 8])
    A.sb("g8", [8, 512])
    A.sb("sel8", [8, 1024])
    A.sb("beC", [128, 128])
    A.sb("gbsb", [128, 512])


def emit_moe1(S, A, P, W, x_in, x_out, uid):
    t = A.t
    TG, NG = MOE_TG, MOE_NG
    xg, xn = t["xg"].ap(), t["xn"].ap()
    slab = t["slab"].ap()
    acc = [slab[:, o * TG:(o + 1) * TG] for o in range(16)]
    cones = t["cones"].ap()
    ceye = t["ceye"].ap()
    tscr = t["tscr"].ap()
    wrt = t["wrt"].ap()
    gate = t["gate"].ap()
    gsc = t["gsc"].ap()
    lsb = t["lsb"].ap()
    g8 = t["g8"].ap()
    sel8 = t["sel8"].ap()
    beC = t["beC"].ap()
    gbsb = t["gbsb"].ap()

    x_in3 = x_in.rearrange("(c p) t -> p c t", p=128)
    x_out3 = x_out.rearrange("(c p) t -> p c t", p=128)

    S.add("sy", lambda e: e.dma_start(
        out=wrt.rearrange("p (c r) -> p c r", r=8),
        in_=W["wr"].rearrange("(c p) r -> p c r", p=128)),
        writes=[wrt], dma=f"{uid}p0")
    S.add("sy", lambda e: e.dma_start(out=sel8, in_=W["sel8"]),
          writes=[sel8], dma=f"{uid}p1")
    S.add("sy", lambda e: e.dma_start(out=beC, in_=W["beC"]),
          writes=[beC], dma=f"{uid}p2")

    p_y = [P[0].ap()[:, 0:TG], P[1].ap()[:, 0:TG]]
    p_gb = P[2].ap()[:, 0:TG]
    p_tr = P[3].ap()[0:8, 0:128]
    p_l = P[3].ap()[:, 256:264]

    def GS(i):
        return gsc[:, i * 8:(i + 1) * 8]

    def GC(i):
        return gsc[:, 24 + i:25 + i]

    for g in range(NG):
        S.add("sy", lambda e, g=g: e.dma_start(
            out=xg[:, 0:16 * TG].rearrange("p (c t) -> p c t", t=TG),
            in_=x_in3[:, :, g * TG:(g + 1) * TG]),
            reads=[x_in3[:, :, g * TG:(g + 1) * TG]], writes=[xg[:, 0:16 * TG]],
            dma=f"{uid}xg", tag=f"{uid}ldx{g}")
        emit_ln(S, A, P, xg, xn, TG, tag=f"{uid}ln{g}")
        for tc in range(4):
            for c in range(16):
                xsl = xn[:, c * TG + tc * 128: c * TG + tc * 128 + 128]
                S.add("pe", lambda e, xsl=xsl, c=c: e.matmul(
                    p_l, xsl, wrt[:, c * 8:(c + 1) * 8], start=(c == 0), stop=(c == 15)),
                    reads=[xsl, wrt[:, c * 8:(c + 1) * 8]], writes=[p_l],
                    tag=f"{uid}lg{g}.{tc}.{c}")
            S.add("ve", lambda e: e.tensor_copy(lsb, p_l), reads=[p_l], writes=[lsb])
            S.add("ve", lambda e: e.reduce_max(GC(0), lsb, axis=AX.X),
                  reads=[lsb], writes=[GC(0)], tag=f"{uid}m1")
            S.add("ve", lambda e: e.tensor_scalar(GS(0), lsb, GC(0), None, op0=ALU.is_equal),
                  reads=[lsb, GC(0)], writes=[GS(0)], tag=f"{uid}eq1")
            S.add("ve", lambda e: e.tensor_scalar(GS(1), GS(0), -1e30, None, op0=ALU.mult),
                  reads=[GS(0)], writes=[GS(1)])
            S.add("ve", lambda e: e.tensor_tensor(GS(1), lsb, GS(1), op=ALU.add),
                  reads=[lsb, GS(1)], writes=[GS(1)])
            S.add("ve", lambda e: e.reduce_max(GC(1), GS(1), axis=AX.X),
                  reads=[GS(1)], writes=[GC(1)], tag=f"{uid}m2")
            S.add("ve", lambda e: e.tensor_scalar(GS(1), lsb, GC(1), None, op0=ALU.is_equal),
                  reads=[lsb, GC(1)], writes=[GS(1)], tag=f"{uid}eq2")
            S.add("ve", lambda e: e.tensor_tensor(GS(0), GS(0), GS(1), op=ALU.add),
                  reads=[GS(0), GS(1)], writes=[GS(0)], tag=f"{uid}sel")
            S.add("ve", lambda e: e.tensor_scalar(GC(2), GC(0), -1.0, None, op0=ALU.mult),
                  reads=[GC(0)], writes=[GC(2)], tag=f"{uid}nm1")
            S.add("sc", lambda e: e.activation(out=GS(2), in_=lsb, func=AF.Exp, bias=GC(2)),
                  reads=[lsb, GC(2)], writes=[GS(2)], tag=f"{uid}expP")
            S.add("ve", lambda e: e.tensor_tensor(GC(3), GC(1), GC(0), op=ALU.subtract),
                  reads=[GC(1), GC(0)], writes=[GC(3)], tag=f"{uid}d21")
            S.add("sc", lambda e: e.activation(out=GC(4), in_=GC(3), func=AF.Exp),
                  reads=[GC(3)], writes=[GC(4)])
            S.add("ve", lambda e: e.tensor_scalar(GC(5), GC(4), 1.0, None, op0=ALU.add),
                  reads=[GC(4)], writes=[GC(5)])
            S.add("ve", lambda e: e.reciprocal(GC(6), GC(5)),
                  reads=[GC(5)], writes=[GC(6)])
            S.add("ve", lambda e: e.tensor_tensor(GS(2), GS(2), GS(0), op=ALU.mult),
                  reads=[GS(2), GS(0)], writes=[GS(2)])
            gsl = gate[:, tc * 8:(tc + 1) * 8]
            S.add("ve", lambda e, gsl=gsl: e.tensor_scalar(
                gsl, GS(2), GC(6), 0.5, op0=ALU.mult, op1=ALU.mult),
                reads=[GS(2), GC(6)], writes=[gsl], tag=f"{uid}gate{g}.{tc}")
            S.add("pe", lambda e, gsl=gsl: e.transpose(p_tr, gsl, ceye),
                  reads=[gsl, ceye], writes=[p_tr], tag=f"{uid}gt{g}.{tc}")
            S.add("ve", lambda e, tc=tc: e.tensor_copy(g8[:, tc * 128:(tc + 1) * 128], p_tr),
                  reads=[p_tr], writes=[g8[:, tc * 128:(tc + 1) * 128]],
                  tag=f"{uid}gtc{g}.{tc}")
        for ei in range(8):
            S.add("pe", lambda e, ei=ei: e.matmul(
                p_gb, sel8[:, ei * 128:(ei + 1) * 128], g8, start=True, stop=True),
                reads=[sel8[:, ei * 128:(ei + 1) * 128], g8], writes=[p_gb],
                tag=f"{uid}gb{g}.{ei}")
            S.add("ve", lambda e: e.tensor_copy(gbsb[:, 0:TG], p_gb),
                  reads=[p_gb], writes=[gbsb[:, 0:TG]], tag=f"{uid}gbe{g}.{ei}")
            for o in range(16):
                b = o % 2
                wb = t[f"wgb{b}"].ap()
                S.add("sy", lambda e, wb=wb, ei=ei, o=o: e.dma_start(
                    out=wb, in_=W["wem"][ei, o]),
                    reads=[W["wem"][ei, o]], writes=[wb], dma=f"{uid}we{b}",
                    tag=f"{uid}ldw{g}.{ei}.{o}")
                for c in range(16):
                    xnc = xn[:, c * TG:(c + 1) * TG]
                    ws = wb[:, c * 128:(c + 1) * 128]
                    S.add("pe", lambda e, ws=ws, xnc=xnc, c=c, b=b: e.matmul(
                        p_y[b], ws, xnc, start=(c == 0), stop=(c == 15)),
                        reads=[ws, xnc], writes=[p_y[b]], tag=f"{uid}mm{g}.{ei}.{o}.{c}")
                becol = beC[:, o * 8 + ei: o * 8 + ei + 1]
                t1 = tscr[:, 0:TG]
                S.add("ve", lambda e, b=b, becol=becol, t1=t1: e.tensor_scalar(
                    t1, p_y[b], becol, None, op0=ALU.add),
                    reads=[p_y[b], becol], writes=[t1], tag=f"{uid}bea{g}.{ei}.{o}")
                if ei == 0:
                    S.add("ve", lambda e, o=o, t1=t1: e.tensor_tensor(
                        acc[o], t1, gbsb[:, 0:TG], op=ALU.mult),
                        reads=[t1, gbsb[:, 0:TG]], writes=[acc[o]], tag=f"{uid}a0{g}.{o}")
                else:
                    t2 = tscr[:, TG:2 * TG]
                    S.add("ve", lambda e, t1=t1, t2=t2: e.tensor_tensor(
                        t2, t1, gbsb[:, 0:TG], op=ALU.mult),
                        reads=[t1, gbsb[:, 0:TG]], writes=[t2], tag=f"{uid}gm{g}.{ei}.{o}")
                    if ei < 7:
                        S.add("ve", lambda e, o=o, t2=t2: e.tensor_tensor(
                            acc[o], acc[o], t2, op=ALU.add),
                            reads=[acc[o], t2], writes=[acc[o]], tag=f"{uid}ac{g}.{ei}.{o}")
                    else:
                        oslot = o % 2
                        outr = t[f"outr{oslot}"].ap()[:, 0:TG]
                        S.add("ve", lambda e, o=o, t2=t2, t1=t1: e.tensor_tensor(
                            t1, acc[o], t2, op=ALU.add),
                            reads=[acc[o], t2], writes=[t1], tag=f"{uid}af{g}.{o}")
                        S.add("ve", lambda e, o=o, outr=outr, t1=t1: e.tensor_tensor(
                            outr, t1, xg[:, o * TG:(o + 1) * TG], op=ALU.add),
                            reads=[t1, xg[:, o * TG:(o + 1) * TG]], writes=[outr],
                            tag=f"{uid}res{g}.{o}")
                        S.add("sc", lambda e, o=o, g=g, outr=outr, oslot=oslot: e.dma_start(
                            out=x_out3[:, o, g * TG:(g + 1) * TG], in_=outr),
                            reads=[outr], writes=[x_out3[:, o, g * TG:(g + 1) * TG]],
                            dma=f"{uid}st{oslot}", tag=f"{uid}st{g}.{o}")


def pack_moe1_weights(Wr, We, be):
    wem = np.ascontiguousarray(
        np.asarray(We).reshape(8, 16, 128, 16, 128).transpose(0, 3, 2, 1, 4)
    ).reshape(8, 16, 128, 2048)
    beC = np.ascontiguousarray(np.asarray(be).reshape(8, 16, 128).transpose(2, 1, 0)).reshape(128, 128)
    sel8 = np.kron(np.eye(8, dtype=np.float32), np.ones((1, 128), np.float32))
    return dict(wem=wem, wr=np.ascontiguousarray(np.asarray(Wr)), beC_d=beC,
                sel8_d=np.ascontiguousarray(sel8))


def declare_moe1_weights(nc, prefix=""):
    d = {}
    d["wem"] = nc.dram_tensor(prefix + "wem", [8, 16, 128, 2048], F32, kind="ExternalInput").ap()
    d["wr"] = nc.dram_tensor(prefix + "wr", [2048, 8], F32, kind="ExternalInput").ap()
    d["beC"] = nc.dram_tensor(prefix + "beC_d", [128, 128], F32, kind="ExternalInput").ap()
    d["sel8"] = nc.dram_tensor(prefix + "sel8_d", [8, 1024], F32, kind="ExternalInput").ap()
    return d


def build_moe1_program():
    nc = bass.Bass("TRN2", target_bir_lowering=False)
    Lx = MOE_NG * MOE_TG
    x_in = nc.dram_tensor("x_in", [D, Lx], F32, kind="ExternalInput").ap()
    x_out = nc.dram_tensor("x_out", [D, Lx], F32, kind="ExternalOutput").ap()
    W = declare_moe1_weights(nc)
    cones_d = nc.dram_tensor("cones_d", [128, 128], F32, kind="ExternalInput").ap()
    ceye_d = nc.dram_tensor("ceye_d", [128, 128], F32, kind="ExternalInput").ap()
    ctx = ExitStack()
    with ctx:
        A = Arena(nc, ctx)
        A.sb("cones", [128, 128])
        A.sb("ceye", [128, 128])
        alloc_shared(A)
        alloc_moe1(A)
        P = [A.ps(f"pb{i}", [128, 512]) for i in range(8)]
        S = Sched(nc, psum_names=[f"pb{i}" for i in range(8)])
        S.add("sy", lambda e: e.dma_start(out=A.t["cones"].ap(), in_=cones_d),
              writes=[A.t["cones"].ap()], dma="cones")
        S.add("sy", lambda e: e.dma_start(out=A.t["ceye"].ap(), in_=ceye_d),
              writes=[A.t["ceye"].ap()], dma="ceye")
        emit_consts(S, A)
        emit_moe1(S, A, P, W, x_in, x_out, uid="m")
        n = S.finalize(ctx)
    return nc, n




ATT_TG = 512
ATT_NG = L // ATT_TG
ATT_SCALE = float(128 ** -0.5)

_AS = dict(qh0=0, qh1=2048, vt=4096, ex0=6144, ex1=6656, at0=7168, at1=7680,
           la=8192, nfr=10240, zq=12288, mcol=12800, scol=12816, fsc=12832,
           asc=12848, lb4=12864, bb=13888)


def emit_attn0(S, A, P, W, x_in, x_out, qs, ks, vs, os_, uid):
    t = A.t
    TG, NG = ATT_TG, ATT_NG
    Lx = NG * TG
    NQC = Lx // 128
    NQT = Lx // 512
    xg, xn = t["xg"].ap(), t["xn"].ap()
    slab = t["slab"].ap()
    cones = t["cones"].ap()
    ceye = t["ceye"].ap()

    qh = [slab[:, _AS["qh0"]:_AS["qh0"] + Lx], slab[:, _AS["qh1"]:_AS["qh1"] + Lx]]
    vt = slab[:, _AS["vt"]:_AS["vt"] + Lx]
    ex = [slab[:, _AS["ex0"]:_AS["ex0"] + 512], slab[:, _AS["ex1"]:_AS["ex1"] + 512]]
    at = [slab[:, _AS["at0"]:_AS["at0"] + 512], slab[:, _AS["at1"]:_AS["at1"] + 512]]
    la = slab[0:16, _AS["la"]:_AS["la"] + 2048]
    nfr = slab[0:1, _AS["nfr"]:_AS["nfr"] + Lx]
    zq = slab[0:16, _AS["zq"]:_AS["zq"] + TG]
    mcol = slab[:, _AS["mcol"]:_AS["mcol"] + NQC]
    scol = slab[:, _AS["scol"]:_AS["scol"] + NQC]
    fsc = slab[:, _AS["fsc"]:_AS["fsc"] + 16]
    asc = slab[:, _AS["asc"]:_AS["asc"] + 16]
    lb4 = slab[:, _AS["lb4"]:_AS["lb4"] + 1024]
    bb = slab[:, _AS["bb"]:_AS["bb"] + 64]

    x_in3 = x_in.rearrange("(c p) t -> p c t", p=128)
    x_out3 = x_out.rearrange("(c p) t -> p c t", p=128)
    sp3 = {n: s.rearrange("(c p) t -> p c t", p=128) for n, s in
           (("q", qs), ("k", ks), ("v", vs), ("o", os_))}

    S.add("sy", lambda e: e.dma_start(out=lb4, in_=W["lb4"]), writes=[lb4], dma=f"{uid}p0")
    S.add("sy", lambda e: e.dma_start(out=bb, in_=W["bb"]), writes=[bb], dma=f"{uid}p1")

    p_pr = [P[0].ap()[:, 0:TG], P[1].ap()[:, 0:TG]]
    p_z = P[2].ap()[0:16, 0:TG]
    p_vt = P[2].ap()[:, 0:128]
    p_tr2 = P[2].ap()[0:1, 0:128]

    def proj(g, wi, wm, dst3, src_tile, resid):
        S.add("sy", lambda e: e.dma_start(out=la, in_=W["aqT4"][wi]),
              writes=[la], dma=f"{uid}la", tag=f"{uid}la{g}.{wi}")
        for c in range(16):
            xnc = src_tile[:, c * TG:(c + 1) * TG]
            lbs = lb4[:, wi * 256 + c * 16: wi * 256 + (c + 1) * 16]
            S.add("pe", lambda e, xnc=xnc, lbs=lbs, c=c: e.matmul(
                p_z, lbs, xnc, start=(c == 0), stop=(c == 15)),
                reads=[lbs, xnc], writes=[p_z], tag=f"{uid}z{g}.{wi}.{c}")
        S.add("ve", lambda e: e.tensor_scalar(zq, p_z, 0.5, None, op0=ALU.mult),
              reads=[p_z], writes=[zq], tag=f"{uid}zev{g}.{wi}")
        for o in range(16):
            b = o % 2
            wb = t[f"wgb{b}"].ap()
            S.add("sy", lambda e, wb=wb, o=o, wm=wm: e.dma_start(out=wb, in_=wm[o]),
                  reads=[wm[o]], writes=[wb], dma=f"{uid}w{b}", tag=f"{uid}ldw{g}.{wi}.{o}")
            for c in range(16):
                xnc = src_tile[:, c * TG:(c + 1) * TG]
                ws = wb[:, c * 128:(c + 1) * 128]
                S.add("pe", lambda e, ws=ws, xnc=xnc, c=c, b=b: e.matmul(
                    p_pr[b], ws, xnc, start=(c == 0), stop=False),
                    reads=[ws, xnc], writes=[p_pr[b]], tag=f"{uid}pm{g}.{wi}.{o}.{c}")
            las = la[:, o * 128:(o + 1) * 128]
            S.add("pe", lambda e, las=las, b=b: e.matmul(
                p_pr[b], las, zq, start=False, stop=True),
                reads=[las, zq], writes=[p_pr[b]], tag=f"{uid}aug{g}.{wi}.{o}")
            oslot = o % 2
            outr = t[f"outr{oslot}"].ap()[:, 0:TG]
            bcol = bb[:, wi * 16 + o: wi * 16 + o + 1]
            if resid is None:
                S.add("ve", lambda e, b=b, outr=outr, bcol=bcol: e.tensor_scalar(
                    outr, p_pr[b], bcol, None, op0=ALU.add),
                    reads=[p_pr[b], bcol], writes=[outr], tag=f"{uid}pev{g}.{wi}.{o}")
                S.add("sc", lambda e, o=o, g=g, outr=outr, dst3=dst3, oslot=oslot: e.dma_start(
                    out=dst3[:, o, g * TG:(g + 1) * TG], in_=outr),
                    reads=[outr], writes=[dst3[:, o, g * TG:(g + 1) * TG]],
                    dma=f"{uid}st{oslot}", tag=f"{uid}sts{g}.{wi}.{o}")
            else:
                tsc = t["tscr"].ap()[:, 0:TG]
                S.add("ve", lambda e, b=b, bcol=bcol, tsc=tsc: e.tensor_scalar(
                    tsc, p_pr[b], bcol, None, op0=ALU.add),
                    reads=[p_pr[b], bcol], writes=[tsc], tag=f"{uid}pev{g}.{wi}.{o}")
                S.add("ve", lambda e, o=o, outr=outr, tsc=tsc: e.tensor_tensor(
                    outr, tsc, xg[:, o * TG:(o + 1) * TG], op=ALU.add),
                    reads=[tsc, xg[:, o * TG:(o + 1) * TG]], writes=[outr],
                    tag=f"{uid}res{g}.{o}")
                S.add("sc", lambda e, o=o, g=g, outr=outr, oslot=oslot: e.dma_start(
                    out=x_out3[:, o, g * TG:(g + 1) * TG], in_=outr),
                    reads=[outr], writes=[x_out3[:, o, g * TG:(g + 1) * TG]],
                    dma=f"{uid}st{oslot}", tag=f"{uid}stx{g}.{o}")

    for g in range(NG):
        S.add("sy", lambda e, g=g: e.dma_start(
            out=xg[:, 0:16 * TG].rearrange("p (c t) -> p c t", t=TG),
            in_=x_in3[:, :, g * TG:(g + 1) * TG]),
            reads=[x_in3[:, :, g * TG:(g + 1) * TG]], writes=[xg[:, 0:16 * TG]],
            dma=f"{uid}xg", tag=f"{uid}ldx{g}")
        emit_ln(S, A, P, xg, xn, TG, tag=f"{uid}ln{g}")
        for wi, (wm, dst) in enumerate(((W["wqm"], sp3["q"]), (W["wkm"], sp3["k"]),
                                        (W["wvm"], sp3["v"]))):
            proj(g, wi, wm, dst, xn, None)

    for h in range(16):
        b = h % 2
        khb = t[f"wub{b}"].ap()[:, 0:Lx]
        vhb = t[f"wdb{b}"].ap()[:, 0:Lx]
        qhb = qh[b]
        for (nm, buf, s3) in (("k", khb, sp3["k"]), ("v", vhb, sp3["v"]), ("q", qhb, sp3["q"])):
            S.add("sy", lambda e, buf=buf, s3=s3, h=h: e.dma_start(
                out=buf, in_=s3[:, h, :]),
                reads=[s3[:, h, :]], writes=[buf], dma=f"{uid}h{nm}{b}",
                tag=f"{uid}ld{nm}{h}")
        for kc in range(NQC):
            S.add("pe", lambda e, kc=kc, vhb=vhb: e.transpose(p_vt, vhb[:, kc * 128:(kc + 1) * 128], ceye),
                  reads=[vhb[:, kc * 128:(kc + 1) * 128], ceye], writes=[p_vt],
                  tag=f"{uid}vt{h}.{kc}")
            S.add("ve", lambda e, kc=kc: e.tensor_copy(vt[:, kc * 128:(kc + 1) * 128], p_vt),
                  reads=[p_vt], writes=[vt[:, kc * 128:(kc + 1) * 128]],
                  tag=f"{uid}vtc{h}.{kc}")
        for qc in range(NQC):
            PB = [P[(qc % 2) * 4 + kt] for kt in range(NQT)]
            for kt in range(NQT):
                S.add("pe", lambda e, qc=qc, kt=kt, qhb=qhb, khb=khb, PB=PB: e.matmul(
                    PB[kt].ap(), qhb[:, qc * 128:(qc + 1) * 128],
                    khb[:, kt * 512:(kt + 1) * 512], start=True, stop=True),
                    reads=[qhb[:, qc * 128:(qc + 1) * 128], khb[:, kt * 512:(kt + 1) * 512]],
                    writes=[PB[kt].ap()], tag=f"{uid}sc{h}.{qc}.{kt}")
                S.add("ve", lambda e, qc=qc, kt=kt, PB=PB: e.reduce_max(
                    asc[:, kt:kt + 1], PB[kt].ap(), axis=AX.X),
                    reads=[PB[kt].ap()], writes=[asc[:, kt:kt + 1]],
                    tag=f"{uid}mx{h}.{qc}.{kt}")
            mc = mcol[:, qc:qc + 1]
            if NQT == 4:
                S.add("ve", lambda e: e.tensor_tensor(asc[:, 4:5], asc[:, 0:1], asc[:, 1:2], op=ALU.max),
                      reads=[asc[:, 0:2]], writes=[asc[:, 4:5]])
                S.add("ve", lambda e: e.tensor_tensor(asc[:, 5:6], asc[:, 2:3], asc[:, 3:4], op=ALU.max),
                      reads=[asc[:, 2:4]], writes=[asc[:, 5:6]])
                S.add("ve", lambda e, mc=mc: e.tensor_tensor(mc, asc[:, 4:5], asc[:, 5:6], op=ALU.max),
                      reads=[asc[:, 4:6]], writes=[mc], tag=f"{uid}m{h}.{qc}")
            else:
                S.add("ve", lambda e, mc=mc: e.tensor_copy(mc, asc[:, 0:1]),
                      reads=[asc[:, 0:1]], writes=[mc], tag=f"{uid}m{h}.{qc}")
            S.add("ve", lambda e, mc=mc: e.tensor_scalar(
                asc[:, 6:7], mc, -ATT_SCALE, None, op0=ALU.mult),
                reads=[mc], writes=[asc[:, 6:7]], tag=f"{uid}nm{h}.{qc}")
            for kt in range(NQT):
                S.add("sc", lambda e, kt=kt, PB=PB: e.activation(
                    out=ex[kt % 2], in_=PB[kt].ap(), func=AF.Exp, bias=asc[:, 6:7],
                    scale=ATT_SCALE, accum_out=asc[:, 8 + kt:9 + kt]),
                    reads=[PB[kt].ap(), asc[:, 6:7]], writes=[ex[kt % 2], asc[:, 8 + kt:9 + kt]],
                    tag=f"{uid}ex{h}.{qc}.{kt}")
            sc_ = scol[:, qc:qc + 1]
            if NQT == 4:
                S.add("ve", lambda e: e.tensor_tensor(asc[:, 12:13], asc[:, 8:9], asc[:, 9:10], op=ALU.add),
                      reads=[asc[:, 8:10]], writes=[asc[:, 12:13]])
                S.add("ve", lambda e: e.tensor_tensor(asc[:, 13:14], asc[:, 10:11], asc[:, 11:12], op=ALU.add),
                      reads=[asc[:, 10:12]], writes=[asc[:, 13:14]])
                S.add("ve", lambda e, sc_=sc_: e.tensor_tensor(sc_, asc[:, 12:13], asc[:, 13:14], op=ALU.add),
                      reads=[asc[:, 12:14]], writes=[sc_], tag=f"{uid}s{h}.{qc}")
            else:
                S.add("ve", lambda e, sc_=sc_: e.tensor_copy(sc_, asc[:, 8:9]),
                      reads=[asc[:, 8:9]], writes=[sc_], tag=f"{uid}s{h}.{qc}")
        S.add("sc", lambda e: e.activation(out=fsc[:, 0:NQC], in_=scol[:, 0:NQC], func=AF.Ln),
              reads=[scol[:, 0:NQC]], writes=[fsc[:, 0:NQC]], tag=f"{uid}lns{h}")
        S.add("ve", lambda e: e.tensor_scalar(asc[:, 0:NQC], mcol[:, 0:NQC], ATT_SCALE, None, op0=ALU.mult),
              reads=[mcol[:, 0:NQC]], writes=[asc[:, 0:NQC]], tag=f"{uid}msc{h}")
        S.add("ve", lambda e: e.tensor_tensor(fsc[:, 0:NQC], asc[:, 0:NQC], fsc[:, 0:NQC], op=ALU.add),
              reads=[asc[:, 0:NQC], fsc[:, 0:NQC]], writes=[fsc[:, 0:NQC]], tag=f"{uid}f{h}")
        S.add("ve", lambda e: e.tensor_scalar(fsc[:, 0:NQC], fsc[:, 0:NQC], -1.0 / ATT_SCALE, None, op0=ALU.mult),
              reads=[fsc[:, 0:NQC]], writes=[fsc[:, 0:NQC]], tag=f"{uid}nf{h}")
        for qc in range(NQC):
            S.add("pe", lambda e, qc=qc: e.transpose(p_tr2, fsc[:, qc:qc + 1], ceye),
                  reads=[fsc[:, qc:qc + 1], ceye], writes=[p_tr2], tag=f"{uid}ft{h}.{qc}")
            S.add("ve", lambda e, qc=qc: e.tensor_copy(nfr[0:1, qc * 128:(qc + 1) * 128], p_tr2),
                  reads=[p_tr2], writes=[nfr[0:1, qc * 128:(qc + 1) * 128]],
                  tag=f"{uid}ftc{h}.{qc}")
        steps = [(kc, qt) for kc in range(NQC) for qt in range(NQT)]

        def emit_sT(i):
            kc, qt = steps[i]
            pb = P[4 + i % 2].ap()
            S.add("pe", lambda e, kc=kc, qt=qt, pb=pb, khb=khb, qhb=qhb: e.matmul(
                pb, khb[:, kc * 128:(kc + 1) * 128], qhb[:, qt * 512:(qt + 1) * 512],
                start=True, stop=False),
                reads=[khb[:, kc * 128:(kc + 1) * 128], qhb[:, qt * 512:(qt + 1) * 512]],
                writes=[pb], tag=f"{uid}sT{h}.{kc}.{qt}")
            S.add("pe", lambda e, qt=qt, pb=pb: e.matmul(
                pb, cones[0:1, :], nfr[0:1, qt * 512:(qt + 1) * 512],
                start=False, stop=True),
                reads=[cones[0:1, :], nfr[0:1, qt * 512:(qt + 1) * 512]],
                writes=[pb], tag=f"{uid}rk{h}.{kc}.{qt}")
            ats = at[i % 2]
            S.add("sc", lambda e, pb=pb, ats=ats: e.activation(
                out=ats, in_=pb, func=AF.Exp, scale=ATT_SCALE),
                reads=[pb], writes=[ats], tag=f"{uid}ea{h}.{kc}.{qt}")
            return ats

        def emit_av(i, ats):
            kc, qt = steps[i]
            S.add("pe", lambda e, kc=kc, qt=qt, ats=ats: e.matmul(
                P[qt].ap(), vt[:, kc * 128:(kc + 1) * 128], ats,
                start=(kc == 0), stop=(kc == NQC - 1)),
                reads=[vt[:, kc * 128:(kc + 1) * 128], ats], writes=[P[qt].ap()],
                tag=f"{uid}av{h}.{kc}.{qt}")

        prev = emit_sT(0)
        for i in range(1, len(steps)):
            cur = emit_sT(i)
            emit_av(i - 1, prev)
            prev = cur
        emit_av(len(steps) - 1, prev)
        for qt in range(NQT):
            oslot = qt % 2
            outr = t[f"outr{oslot}"].ap()
            S.add("ve", lambda e, qt=qt, outr=outr: e.tensor_copy(outr, P[qt].ap()),
                  reads=[P[qt].ap()], writes=[outr], tag=f"{uid}oev{h}.{qt}")
            S.add("sc", lambda e, h=h, qt=qt, outr=outr, oslot=oslot: e.dma_start(
                out=sp3["o"][:, h, qt * 512:(qt + 1) * 512], in_=outr),
                reads=[outr], writes=[sp3["o"][:, h, qt * 512:(qt + 1) * 512]],
                dma=f"{uid}so{oslot}", tag=f"{uid}sto{h}.{qt}")

    for g in range(NG):
        S.add("sy", lambda e, g=g: e.dma_start(
            out=xg[:, 0:16 * TG].rearrange("p (c t) -> p c t", t=TG),
            in_=x_in3[:, :, g * TG:(g + 1) * TG]),
            reads=[x_in3[:, :, g * TG:(g + 1) * TG]], writes=[xg[:, 0:16 * TG]],
            dma=f"{uid}xg", tag=f"{uid}ldx2{g}")
        S.add("sy", lambda e, g=g: e.dma_start(
            out=xn[:, 0:16 * TG].rearrange("p (c t) -> p c t", t=TG),
            in_=sp3["o"][:, :, g * TG:(g + 1) * TG]),
            reads=[sp3["o"][:, :, g * TG:(g + 1) * TG]], writes=[xn[:, 0:16 * TG]],
            dma=f"{uid}xo", tag=f"{uid}ldo{g}")
        proj(g, 3, W["wom"], None, xn, resid=True)


def pack_proj(Wn):
    return np.ascontiguousarray(
        np.asarray(Wn).reshape(16, 128, 16, 128).transpose(0, 3, 2, 1)).reshape(16, 128, 2048)


def pack_attn0_weights(Wq, bq, Bq, Aq, Wk, bk, Bk, Ak, Wv, bv, Bv, Av, Wo, bo, Bo, Ao):
    d = {}
    d["wqm"], d["wkm"], d["wvm"], d["wom"] = (pack_proj(w) for w in (Wq, Wk, Wv, Wo))
    lb = []
    for B in (Bq, Bk, Bv, Bo):
        lb.append(np.ascontiguousarray(np.asarray(B).T).reshape(16, 128, 16).transpose(1, 0, 2).reshape(128, 256))
    d["lb4"] = np.ascontiguousarray(np.concatenate(lb, axis=1))
    d["aqT4"] = np.ascontiguousarray(np.stack([np.asarray(A_).T for A_ in (Aq, Ak, Av, Ao)]))
    bbs = [np.asarray(b).reshape(16, 128).T for b in (bq, bk, bv, bo)]
    d["bb"] = np.ascontiguousarray(np.concatenate(bbs, axis=1))
    return d


def declare_attn0_weights(nc, prefix=""):
    d = {}
    for n in ("wqm", "wkm", "wvm", "wom"):
        d[n] = nc.dram_tensor(prefix + n, [16, 128, 2048], F32, kind="ExternalInput").ap()
    d["lb4"] = nc.dram_tensor(prefix + "lb4_d", [128, 1024], F32, kind="ExternalInput").ap()
    d["aqT4"] = nc.dram_tensor(prefix + "aqT4_d", [4, 16, 2048], F32, kind="ExternalInput").ap()
    d["bb"] = nc.dram_tensor(prefix + "bb_d", [128, 64], F32, kind="ExternalInput").ap()
    return d


def build_attn0_program():
    nc = bass.Bass("TRN2", target_bir_lowering=False)
    Lx = ATT_NG * ATT_TG
    x_in = nc.dram_tensor("x_in", [D, Lx], F32, kind="ExternalInput").ap()
    x_out = nc.dram_tensor("x_out", [D, Lx], F32, kind="ExternalOutput").ap()
    qs = nc.dram_tensor("q_s", [D, Lx], F32, kind="Internal").ap()
    ks = nc.dram_tensor("k_s", [D, Lx], F32, kind="Internal").ap()
    vs = nc.dram_tensor("v_s", [D, Lx], F32, kind="Internal").ap()
    os_ = nc.dram_tensor("o_s", [D, Lx], F32, kind="Internal").ap()
    W = declare_attn0_weights(nc)
    cones_d = nc.dram_tensor("cones_d", [128, 128], F32, kind="ExternalInput").ap()
    ceye_d = nc.dram_tensor("ceye_d", [128, 128], F32, kind="ExternalInput").ap()
    ctx = ExitStack()
    with ctx:
        A = Arena(nc, ctx)
        A.sb("cones", [128, 128])
        A.sb("ceye", [128, 128])
        alloc_shared(A)
        P = [A.ps(f"pb{i}", [128, 512]) for i in range(8)]
        S = Sched(nc, psum_names=[f"pb{i}" for i in range(8)])
        S.add("sy", lambda e: e.dma_start(out=A.t["cones"].ap(), in_=cones_d),
              writes=[A.t["cones"].ap()], dma="cones")
        S.add("sy", lambda e: e.dma_start(out=A.t["ceye"].ap(), in_=ceye_d),
              writes=[A.t["ceye"].ap()], dma="ceye")
        emit_consts(S, A)
        emit_attn0(S, A, P, W, x_in, x_out, qs, ks, vs, os_, uid="a")
        n = S.finalize(ctx)
    return nc, n




AT1_TG = 512
AT1_NG = L // AT1_TG
AT1_EPS = 1e-6
AT1_DBG = False
AT1_PF = True
AT1_LEVEL = 0
AT1_SECTS = 'zlcaetsbfw'

_A1 = dict(qb0=0, qb1=2048, xcwb=4096, dfm=6144, w1s=8192, TMg=10240,
           cxb=12288, ATsb=14336, Etsb=14848, AEtsb=15360, sTsb=15872,
           sTTsb=15888, s2sb=15920, coTM=16176, sfm=15872)
_X1 = dict(z1=0, z1e=256, sqs=512, xh=768, g0=1024, grad=1280, cog=1536,
           zsb=1792, zbe=2048, t1=2304, t2=2560, b1s=2816, yo=3072, outf=3328,
           gx=3584, stat=3840, stat2=4352, red=4864)


def alloc_attn1(A):
    A.sb("lwt", [128, 512])
    A.sb("hind", [2, 128])
    A.sb("o2bd", [128, 2])
    A.sb("a2c", [2, 256])
    A.sb("aTMc", [16, 1])
    A.sb("lbT", [1, 32])
    A.sb("eps2", [2, 1])


def emit_attn1(S, A, P, W, x_in, x_out, qs, ks, vs, xcws, sd, sdT, uid):
    t = A.t
    TG, NG = AT1_TG, AT1_NG
    Lx = NG * TG
    NCH = Lx // 16
    NMG = NCH // 8
    xg, xn = t["xg"].ap(), t["xn"].ap()
    slab = t["slab"].ap()
    cones = t["cones"].ap()
    ceye = t["ceye"].ap()

    qb = [slab[0:128, _A1["qb0"]:_A1["qb0"] + 2048],
          slab[0:128, _A1["qb1"]:_A1["qb1"] + 2048]]
    xcwb = slab[0:128, _A1["xcwb"]:_A1["xcwb"] + 2048]
    dfm = slab[0:128, _A1["dfm"]:_A1["dfm"] + 2048]
    w1s = slab[0:128, _A1["w1s"]:_A1["w1s"] + 2048]
    TMg = slab[0:16, _A1["TMg"]:_A1["TMg"] + 2048]
    cxb = slab[0:16, _A1["cxb"]:_A1["cxb"] + 2048]
    ATsb = slab[0:16, _A1["ATsb"]:_A1["ATsb"] + 512]
    Etsb = slab[0:16, _A1["Etsb"]:_A1["Etsb"] + 512]
    AEtsb = slab[0:16, _A1["AEtsb"]:_A1["AEtsb"] + 512]
    sTsb = slab[0:32, _A1["sTsb"]:_A1["sTsb"] + 16]
    sTTsb = slab[0:16, _A1["sTTsb"]:_A1["sTTsb"] + 32]
    s2sb = slab[0:2, _A1["s2sb"]:_A1["s2sb"] + 256]
    coTM = slab[0:16, _A1["coTM"]:_A1["coTM"] + 32]
    lwt = t["lwt"].ap()
    mkTnA = xn[0:16, 5392:5904]
    csel = xn[0:32, 5904:6416]
    gfm = xn[0:128, 4880:5136]
    bfm = xn[0:128, 5136:5392]
    hind = t["hind"].ap()
    o2bd = t["o2bd"].ap()
    a2c = t["a2c"].ap()
    aTMc = t["aTMc"].ap()
    lbT = t["lbT"].ap()
    eps2 = t["eps2"].ap()

    def XV(nm, cols):
        return xn[0:128, _X1[nm]:_X1[nm] + cols]

    x_in3 = x_in.rearrange("(c p) t -> p c t", p=128)
    x_out3 = x_out.rearrange("(c p) t -> p c t", p=128)
    sp3 = {n: s.rearrange("(c p) t -> p c t", p=128) for n, s in
           (("q", qs), ("k", ks), ("v", vs), ("x", xcws))}

    for nm in ("lwt", "hind", "o2bd", "a2c", "aTMc", "lbT"):
        ap = t[nm].ap()
        S.add("sy", lambda e, ap=ap, nm=nm: e.dma_start(out=ap, in_=W[nm]),
              writes=[ap], dma=f"{uid}c{nm}")
    S.add("sy", lambda e: e.dma_start(
        out=w1s.rearrange("p (pr o) -> p pr o", o=128),
        in_=W["w1bd"].rearrange("pr p o -> p pr o")),
        reads=[W["w1bd"]], writes=[w1s], dma=f"{uid}w1i")
    b1s = XV("b1s", 256)
    S.add("ve", lambda e: e.memset(eps2, AT1_EPS), writes=[eps2], tag=f"{uid}eps2")

    p_pr = [P[0].ap()[:, 0:TG], P[1].ap()[:, 0:TG]]

    for g in range(NG):
        S.add("sy", lambda e, g=g: e.dma_start(
            out=xg[:, 0:16 * TG].rearrange("p (c t) -> p c t", t=TG),
            in_=x_in3[:, :, g * TG:(g + 1) * TG]),
            reads=[x_in3[:, :, g * TG:(g + 1) * TG]], writes=[xg[:, 0:16 * TG]],
            dma=f"{uid}xg", tag=f"{uid}ldx{g}")
        emit_ln(S, A, P, xg, xn, TG, tag=f"{uid}ln{g}")
        p_il = P[2].ap()[:, 0:32]
        p_tr0 = P[3].ap()[0:32, 0:128]
        sfm = slab[0:32, _A1["sfm"]:_A1["sfm"] + 512]
        for tc in range(4):
            for c in range(16):
                xsl = xn[:, c * TG + tc * 128: c * TG + tc * 128 + 128]
                S.add("pe", lambda e, xsl=xsl, c=c: e.matmul(
                    p_il, xsl, lwt[:, c * 32:(c + 1) * 32], start=(c == 0), stop=False),
                    reads=[xsl, lwt[:, c * 32:(c + 1) * 32]], writes=[p_il],
                    tag=f"{uid}il{g}.{tc}.{c}")
            S.add("pe", lambda e: e.matmul(p_il, cones[0:1, :], lbT, start=False, stop=True),
                  reads=[cones[0:1, :], lbT], writes=[p_il], tag=f"{uid}ilb{g}.{tc}")
            stm = t["tscr"].ap()[:, 0:32]
            S.add("sc", lambda e, stm=stm: e.activation(out=stm, in_=p_il, func=AF.Sigmoid),
                  reads=[p_il], writes=[stm], tag=f"{uid}sig{g}.{tc}")
            row0 = g * 512 + tc * 128
            S.add("sc", lambda e, row0=row0, stm=stm: e.dma_start(
                out=sdT[row0:row0 + 128, :], in_=stm),
                reads=[stm], writes=[sdT[row0:row0 + 128, :]],
                dma=f"{uid}sdt", tag=f"{uid}ssdT{g}.{tc}")
            S.add("pe", lambda e, stm=stm, p_tr0=p_tr0: e.transpose(p_tr0, stm, ceye),
                  reads=[stm, ceye], writes=[p_tr0], tag=f"{uid}strp{g}.{tc}")
            S.add("ve", lambda e, tc=tc, sfm=sfm, p_tr0=p_tr0: e.tensor_copy(
                sfm[:, tc * 128:(tc + 1) * 128], p_tr0),
                reads=[p_tr0], writes=[sfm[:, tc * 128:(tc + 1) * 128]],
                tag=f"{uid}sTc{g}.{tc}")
        S.add("sc", lambda e, g=g, sfm=sfm: e.dma_start(
            out=sd[:, g * 512:(g + 1) * 512], in_=sfm),
            reads=[sfm], writes=[sd[:, g * 512:(g + 1) * 512]],
            dma=f"{uid}sds", tag=f"{uid}ssd{g}")
        for wi, (wmk, dst) in enumerate((("wqm1", "q"), ("wkm1", "k"), ("wvm1", "v"))):
            wm = W[wmk]
            for o in range(16):
                b = o % 2
                wb = t[f"wgb{b}"].ap()
                S.add("sy", lambda e, wb=wb, o=o, wm=wm: e.dma_start(out=wb, in_=wm[o]),
                      reads=[wm[o]], writes=[wb], dma=f"{uid}w{b}",
                      tag=f"{uid}ldw{g}.{wi}.{o}")
                for c in range(16):
                    xnc = xn[:, c * TG:(c + 1) * TG]
                    ws = wb[:, c * 128:(c + 1) * 128]
                    S.add("pe", lambda e, ws=ws, xnc=xnc, c=c, b=b: e.matmul(
                        p_pr[b], ws, xnc, start=(c == 0), stop=(c == 15)),
                        reads=[ws, xnc], writes=[p_pr[b]], tag=f"{uid}pm{g}.{wi}.{o}.{c}")
                oslot = o % 2
                outr = t[f"outr{oslot}"].ap()[:, 0:TG]
                S.add("ve", lambda e, b=b, outr=outr: e.tensor_copy(outr, p_pr[b]),
                      reads=[p_pr[b]], writes=[outr], tag=f"{uid}pev{g}.{wi}.{o}")
                d3 = sp3[dst]
                S.add("sc", lambda e, o=o, g=g, outr=outr, d3=d3, oslot=oslot: e.dma_start(
                    out=d3[:, o, g * TG:(g + 1) * TG], in_=outr),
                    reads=[outr], writes=[d3[:, o, g * TG:(g + 1) * TG]],
                    dma=f"{uid}st{oslot}", tag=f"{uid}sts{g}.{wi}.{o}")

    S.add("sy", lambda e: e.dma_start(out=b1s, in_=W["b1fm"]),
          reads=[W["b1fm"]], writes=[b1s], dma=f"{uid}b1i")
    for nm, ap in (("mkTnA", mkTnA), ("csel", csel), ("gfm", gfm), ("bfm", bfm)):
        S.add("sy", lambda e, ap=ap, nm=nm: e.dma_start(out=ap, in_=W[nm]),
              reads=[W[nm]], writes=[ap], dma=f"{uid}c{nm}")
    z1 = XV("z1", 256)
    z1e = XV("z1e", 256)
    sqs = XV("sqs", 256)
    xh = XV("xh", 256)
    g0 = XV("g0", 256)
    grad = XV("grad", 256)
    cog = XV("cog", 256)
    zsb = XV("zsb", 256)
    zbe = XV("zbe", 256)
    t1s = XV("t1", 256)
    t2s = XV("t2", 256)
    outf = XV("outf", 256)
    gxs = XV("gx", 256)
    red3 = XV("red", 16)
    stat = xn[0:2, _X1["stat"]:_X1["stat"] + 512]
    stat2 = xn[0:2, _X1["stat2"]:_X1["stat2"] + 512]

    ATp = [ATsb, xn[0:16, 6416:6928]]
    Etp = [Etsb, xn[0:16, 6928:7440]]
    AEp = [AEtsb, xn[0:16, 7440:7952]]

    pZ = P[0].ap()[:, 0:256]
    pR = P[1].ap()[0:2, 0:512]
    pB = [P[2].ap()[:, 0:256], P[3].ap()[:, 0:256]]
    pCO = P[4].ap()[:, 0:256]
    pCT = P[1].ap()[0:16, 0:512]
    pAT = P[5].ap()[0:16, 0:256]
    pATb = P[2].ap()[0:16, 256:512]
    pTR = [P[6].ap()[0:16, 0:512], P[7].ap()[0:16, 0:512]]
    pWF = [P[0].ap(), P[4].ap()]
    pend = [None, None]
    bci = [0]

    def bcast(vals_ap, tag):
        b = bci[0] % 2
        bci[0] += 1
        pb = pB[b]
        S.add("pe", lambda e, vals_ap=vals_ap, pb=pb: e.matmul(
            pb, hind, vals_ap, start=True, stop=True),
            reads=[hind, vals_ap], writes=[pb], tag=tag)
        return pb

    def bn_stats(src, tag):
        S.add("sc", lambda e, src=src: e.activation(out=sqs, in_=src, func=AF.Square),
              reads=[src], writes=[sqs], tag=f"{tag}sq")
        S.add("pe", lambda e, src=src: e.matmul(pR[:, 0:256], o2bd, src, start=True, stop=True),
              reads=[o2bd, src], writes=[pR], tag=f"{tag}rs")
        S.add("pe", lambda e: e.matmul(pR[:, 256:512], o2bd, sqs, start=True, stop=True),
              reads=[o2bd, sqs], writes=[pR], tag=f"{tag}rq")
        S.add("ve", lambda e: e.tensor_copy(stat, pR), reads=[pR], writes=[stat],
              tag=f"{tag}ev")
        S.add("ve", lambda e: e.tensor_scalar(stat2, stat, 1.0 / 64, None, op0=ALU.mult),
              reads=[stat], writes=[stat2], tag=f"{tag}mu")
        S.add("ve", lambda e: e.tensor_tensor(stat[:, 0:256], stat2[:, 0:256], stat2[:, 0:256], op=ALU.mult),
              reads=[stat2[:, 0:256]], writes=[stat[:, 0:256]], tag=f"{tag}m2")
        S.add("ve", lambda e: e.tensor_tensor(stat[:, 256:512], stat2[:, 256:512], stat[:, 0:256], op=ALU.subtract),
              reads=[stat2[:, 256:512], stat[:, 0:256]], writes=[stat[:, 256:512]], tag=f"{tag}var")
        S.add("sc", lambda e: e.activation(out=stat[:, 0:256], in_=stat[:, 256:512],
                                           func=AF.Sqrt, bias=eps2),
              reads=[stat[:, 256:512], eps2], writes=[stat[:, 0:256]], tag=f"{tag}std")
        S.add("ve", lambda e: e.reciprocal(stat[:, 256:512], stat[:, 0:256]),
              reads=[stat[:, 0:256]], writes=[stat[:, 256:512]], tag=f"{tag}rst")

    def emit_ce(ch):
        p = ch % 2
        Ett = Etp[p]
        S.add("sy", lambda e, ch=ch: e.dma_start(out=sTsb, in_=W["sd_ap"][:, ch * 16:(ch + 1) * 16]),
              reads=[W["sd_ap"][:, ch * 16:(ch + 1) * 16]], writes=[sTsb],
              dma=f"{uid}lsT", tag=f"{uid}lsT{ch}")
        S.add("pe", lambda e: e.matmul(pCT, sTsb, csel, start=True, stop=True),
              reads=[sTsb, csel], writes=[pCT], tag=f"{uid}ct{ch}")
        S.add("ve", lambda e, Ett=Ett: e.tensor_tensor(Ett, mkTnA, pCT, op=ALU.mult),
              reads=[mkTnA, pCT], writes=[Ett], tag=f"{uid}et{ch}")

    def emit_aet(ch):
        p = ch % 2
        ATt, Ett, AEt = ATp[p], Etp[p], AEp[p]
        S.add("ve", lambda e, AEt=AEt, ATt=ATt, Ett=Ett: e.tensor_tensor(
            AEt, ATt, Ett, op=ALU.mult),
            reads=[ATt, Ett], writes=[AEt], tag=f"{uid}aet{ch}")

    def emit_atb(ch, XB, XC):
        p = ch % 2
        ATt = ATp[p]
        for h in range(32):
            pr, par = h // 2, h % 2
            xbh = XB[par * 64:(par + 1) * 64, pr, :]
            xch = XC[par * 64:(par + 1) * 64, pr, :]
            pdst = pAT if par == 0 else pATb
            S.add("pe", lambda e, h=h, xbh=xbh, xch=xch, pdst=pdst: e.matmul(
                pdst[:, (h // 2) * 16:(h // 2 + 1) * 16], xbh, xch, start=True, stop=True),
                reads=[xbh, xch], writes=[pdst], tag=f"{uid}at{ch}.{h}")
        ATv = ATt.rearrange("j (hp par k) -> j hp par k", par=2, k=16)
        S.add("ve", lambda e, ATv=ATv, ATt=ATt: e.tensor_copy(
            ATv[:, :, 0, :], pAT.rearrange("j (hp k) -> j hp k", k=16)),
            reads=[pAT], writes=[ATt], tag=f"{uid}atv{ch}")
        S.add("ve", lambda e, ATv=ATv, ATt=ATt: e.tensor_copy(
            ATv[:, :, 1, :], pATb.rearrange("j (hp k) -> j hp k", k=16)),
            reads=[pATb], writes=[ATt], tag=f"{uid}atv2{ch}")

    for mg in range(NMG):
        mb = mg % 2
        kb = t[f"wub{mb}"].ap()[:, 0:2048]
        vb = t[f"wdb{mb}"].ap()[:, 0:2048]
        qbb = qb[mb]
        for (nm, buf, s3) in (("k", kb, sp3["k"]), ("v", vb, sp3["v"]), ("q", qbb, sp3["q"])):
            S.add("sy", lambda e, buf=buf, s3=s3, mg=mg: e.dma_start(
                out=buf.rearrange("p (pr t) -> p pr t", t=128),
                in_=s3[:, :, mg * 128:(mg + 1) * 128]),
                reads=[s3[:, :, mg * 128:(mg + 1) * 128]], writes=[buf],
                dma=f"{uid}b{nm}{mb}", tag=f"{uid}mg{nm}{mg}")
        S.add("ve", lambda e, kb=kb, vb=vb: e.tensor_tensor(dfm, vb, kb, op=ALU.subtract),
              reads=[vb, kb], writes=[dfm], tag=f"{uid}dfm{mg}")
        for cc in range(8):
            ch = mg * 8 + cc
            kb3 = kb.rearrange("p (pr t) -> p pr t", t=128)
            qb3 = qbb.rearrange("p (pr t) -> p pr t", t=128)
            db3 = dfm.rearrange("p (pr t) -> p pr t", t=128)
            XB = kb3[:, :, cc * 16:(cc + 1) * 16]
            XC = qb3[:, :, cc * 16:(cc + 1) * 16]
            DD = db3[:, :, cc * 16:(cc + 1) * 16]
            if cc == 0 or not AT1_PF:
                emit_atb(ch, XB, XC)
            S.add("sy", lambda e, ch=ch: e.dma_start(out=sTTsb, in_=W["sdT_ap"][ch * 16:(ch + 1) * 16, :]),
                  reads=[W["sdT_ap"][ch * 16:(ch + 1) * 16, :]], writes=[sTTsb],
                  dma=f"{uid}lsTT", tag=f"{uid}lsTT{ch}")
            S.add("sy", lambda e, ch=ch: e.dma_start(
                out=s2sb.rearrange("r (pr k) -> r pr k", k=16),
                in_=W["sd_ap"].rearrange("(pr r) (cg k) -> cg r pr k", r=2, k=16)[ch]),
                reads=[W["sd_ap"]], writes=[s2sb], dma=f"{uid}ls2", tag=f"{uid}ls2{ch}")
            if AT1_LEVEL >= 2:
                continue
            for pr in range(16 if "z" in AT1_SECTS else 0):
                S.add("pe", lambda e, pr=pr, XB=XB: e.matmul(
                    pZ[:, pr * 16:(pr + 1) * 16], w1s[:, pr * 128:(pr + 1) * 128],
                    XB[:, pr, :], start=True, stop=True),
                    reads=[w1s[:, pr * 128:(pr + 1) * 128], XB[:, pr, :]],
                    writes=[pZ], tag=f"{uid}z1.{ch}.{pr}")
            if pend[0] is not None:
                pend[0]()
                pend[0] = None
            S.add("ve", lambda e: e.tensor_tensor(z1e, pZ, b1s, op=ALU.add),
                  reads=[pZ, b1s], writes=[z1e], tag=f"{uid}z1b{ch}")
            S.add("ve", lambda e: e.tensor_tensor(s2sb, s2sb, a2c, op=ALU.mult),
                  reads=[s2sb, a2c], writes=[s2sb], tag=f"{uid}s2a{ch}")
            S.add("pe", lambda e: e.matmul(pCO, hind, s2sb, start=True, stop=True),
                  reads=[hind, s2sb], writes=[pCO], tag=f"{uid}cof{ch}")
            emit_ce(ch)
            emit_aet(ch)
            if cc < 7 and AT1_PF:
                XBn = kb3[:, :, (cc + 1) * 16:(cc + 2) * 16]
                XCn = qb3[:, :, (cc + 1) * 16:(cc + 2) * 16]
                emit_atb(ch + 1, XBn, XCn)
            if "s" in AT1_SECTS:
                for pr in range(16):
                    S.add("pe", lambda e, pr=pr, XC=XC, stp=(AT1_LEVEL >= 1 and pr == 15): e.matmul(
                        pZ[:, pr * 16:(pr + 1) * 16], w1s[:, pr * 128:(pr + 1) * 128],
                        XC[:, pr, :], start=(pr == 0), stop=stp, skip_group_check=True),
                        reads=[w1s[:, pr * 128:(pr + 1) * 128], XC[:, pr, :]],
                        writes=[pZ], tag=f"{uid}s1.{ch}.{pr}")
            S.add("ve", lambda e: e.tensor_scalar(coTM, sTTsb, aTMc, None, op0=ALU.mult),
                  reads=[sTTsb, aTMc], writes=[coTM], tag=f"{uid}coT{ch}")
            for quad in range(4):
                ptr = pTR[quad % 2]
                for q4 in range(4):
                    pr = quad * 4 + q4
                    S.add("pe", lambda e, pr=pr, q4=q4, ptr=ptr, XB=XB: e.transpose(
                        ptr[:, q4 * 128:(q4 + 1) * 128], XB[:, pr, :], ceye),
                        reads=[XB[:, pr, :], ceye], writes=[ptr],
                        tag=f"{uid}tx{ch}.{pr}")
                for q4 in range(4):
                    pr = quad * 4 + q4
                    for par in range(2):
                        h = pr * 2 + par
                        dst = cxb[:, pr * 128 + par * 64: pr * 128 + (par + 1) * 64]
                        srcp = ptr[:, q4 * 128 + par * 64: q4 * 128 + (par + 1) * 64]
                        S.add("ve", lambda e, dst=dst, srcp=srcp, h=h: e.tensor_scalar(
                            dst, srcp, coTM[:, h:h + 1], None, op0=ALU.mult),
                            reads=[srcp, coTM[:, h:h + 1]], writes=[dst],
                            tag=f"{uid}cxb{ch}.{h}")
            if pend[1] is not None:
                pend[1]()
                pend[1] = None
            bn_stats(z1e, f"{uid}bn{ch}")
            pmu = bcast(stat2[:, 0:256], f"{uid}bmu{ch}")
            prs = bcast(stat[:, 256:512], f"{uid}brs{ch}")
            S.add("ve", lambda e, pmu=pmu: e.tensor_tensor(t1s, z1e, pmu, op=ALU.subtract),
                  reads=[z1e, pmu], writes=[t1s], tag=f"{uid}xh1{ch}")
            S.add("ve", lambda e, prs=prs: e.tensor_tensor(xh, t1s, prs, op=ALU.mult),
                  reads=[t1s, prs], writes=[xh], tag=f"{uid}xh2{ch}")
            S.add("ve", lambda e: e.tensor_scalar(stat[:, 0:256], stat[:, 256:512], 1.0 / 64, None, op0=ALU.mult),
                  reads=[stat[:, 256:512]], writes=[stat[:, 0:256]], tag=f"{uid}r64{ch}")
            S.add("ve", lambda e: e.tensor_tensor(t1s, xh, gfm, op=ALU.mult),
                  reads=[xh, gfm], writes=[t1s], tag=f"{uid}y1{ch}")
            S.add("ve", lambda e: e.tensor_tensor(t2s, t1s, bfm, op=ALU.add),
                  reads=[t1s, bfm], writes=[t2s], tag=f"{uid}y2{ch}")
            S.add("ve", lambda e, DD=DD: e.tensor_tensor(
                t1s.rearrange("p (pr k) -> p pr k", k=16), t2s.rearrange("p (pr k) -> p pr k", k=16),
                DD, op=ALU.subtract),
                reads=[t2s, DD], writes=[t1s], tag=f"{uid}ymt{ch}")
            S.add("ve", lambda e: e.tensor_tensor(g0, t1s, gfm, op=ALU.mult),
                  reads=[t1s, gfm], writes=[g0], tag=f"{uid}g0{ch}")
            S.add("pe", lambda e: e.matmul(pR[:, 0:256], o2bd, g0, start=True, stop=True),
                  reads=[o2bd, g0], writes=[pR], tag=f"{uid}rsg{ch}")
            S.add("ve", lambda e: e.tensor_tensor(gxs, g0, xh, op=ALU.mult),
                  reads=[g0, xh], writes=[gxs], tag=f"{uid}gx{ch}")
            S.add("pe", lambda e: e.matmul(pR[:, 256:512], o2bd, gxs, start=True, stop=True),
                  reads=[o2bd, gxs], writes=[pR], tag=f"{uid}rgx{ch}")
            S.add("ve", lambda e: e.tensor_copy(stat2, pR), reads=[pR], writes=[stat2],
                  tag=f"{uid}sev2{ch}")
            psg = bcast(stat2[:, 0:256], f"{uid}bsg{ch}")
            psgx = bcast(stat2[:, 256:512], f"{uid}bsgx{ch}")
            S.add("ve", lambda e: e.tensor_scalar(t1s, g0, 64.0, None, op0=ALU.mult),
                  reads=[g0], writes=[t1s], tag=f"{uid}g64{ch}")
            S.add("ve", lambda e, psg=psg: e.tensor_tensor(t2s, t1s, psg, op=ALU.subtract),
                  reads=[t1s, psg], writes=[t2s], tag=f"{uid}gm1{ch}")
            S.add("ve", lambda e, psgx=psgx: e.tensor_tensor(t1s, xh, psgx, op=ALU.mult),
                  reads=[xh, psgx], writes=[t1s], tag=f"{uid}gm2{ch}")
            S.add("ve", lambda e: e.tensor_tensor(t2s, t2s, t1s, op=ALU.subtract),
                  reads=[t2s, t1s], writes=[t2s], tag=f"{uid}gm3{ch}")
            prs64 = bcast(stat[:, 0:256], f"{uid}brs64{ch}")
            S.add("ve", lambda e, prs64=prs64: e.tensor_tensor(grad, t2s, prs64, op=ALU.mult),
                  reads=[t2s, prs64], writes=[grad], tag=f"{uid}grad{ch}")
            S.add("ve", lambda e: e.tensor_tensor(cog, grad, pCO, op=ALU.mult),
                  reads=[grad, pCO], writes=[cog], tag=f"{uid}cgr{ch}")
            for half in range(2):
                ptr = pTR[half]
                for q4 in range(4):
                    pr = half * 4 + q4
                    S.add("pe", lambda e, pr=pr, q4=q4, ptr=ptr: e.transpose(
                        ptr[:, q4 * 128:(q4 + 1) * 128], grad[:, pr * 16:(pr + 1) * 16], ceye),
                        reads=[grad[:, pr * 16:(pr + 1) * 16], ceye], writes=[ptr],
                        tag=f"{uid}tg{ch}.{pr}")
                S.add("ve", lambda e, half=half, ptr=ptr: e.tensor_copy(
                    TMg[:, half * 512:(half + 1) * 512], ptr),
                    reads=[ptr], writes=[TMg[:, half * 512:(half + 1) * 512]],
                    tag=f"{uid}tgv{ch}.{half}")
            for half in range(2):
                ptr = pTR[half]
                for q4 in range(4):
                    pr = 8 + half * 4 + q4
                    S.add("pe", lambda e, pr=pr, q4=q4, ptr=ptr: e.transpose(
                        ptr[:, q4 * 128:(q4 + 1) * 128], grad[:, pr * 16:(pr + 1) * 16], ceye),
                        reads=[grad[:, pr * 16:(pr + 1) * 16], ceye], writes=[ptr],
                        tag=f"{uid}tg2{ch}.{pr}")
                S.add("ve", lambda e, half=half, ptr=ptr: e.tensor_copy(
                    TMg[:, 1024 + half * 512:1024 + (half + 1) * 512], ptr),
                    reads=[ptr], writes=[TMg[:, 1024 + half * 512:1024 + (half + 1) * 512]],
                    tag=f"{uid}tgv2{ch}.{half}")
            if "s" not in AT1_SECTS:
                continue
            if AT1_LEVEL < 1:
                for pr in range(16):
                    for par in range(2):
                        h = pr * 2 + par
                        tmslc = TMg[:, pr * 128 + par * 64: pr * 128 + (par + 1) * 64]
                        for ei, esrc in ((0, Etp[ch % 2]), (1, AEp[ch % 2])):
                            S.add("pe", lambda e, pr=pr, par=par, h=h, tmslc=tmslc,
                                  esrc=esrc, ei=ei: e.matmul(
                                pZ[par * 64:(par + 1) * 64, pr * 16:(pr + 1) * 16],
                                tmslc, esrc[:, h * 16:(h + 1) * 16],
                                start=False, stop=(pr == 15 and par == 1 and ei == 1),
                                skip_group_check=True),
                                reads=[tmslc, esrc[:, h * 16:(h + 1) * 16]],
                                writes=[pZ], tag=f"{uid}eg{ch}.{h}.{ei}")
            S.add("ve", lambda e: e.tensor_tensor(zbe, pZ, b1s, op=ALU.add),
                  reads=[pZ, b1s], writes=[zbe], tag=f"{uid}zbb{ch}")
            for pr in range(16 if AT1_LEVEL < 1 else 0):
                b = pr // 8
                for par in range(2):
                    h = pr * 2 + par
                    cxslc = cxb[:, pr * 128 + par * 64: pr * 128 + (par + 1) * 64]
                    tmslc = TMg[:, pr * 128 + par * 64: pr * 128 + (par + 1) * 64]
                    dst = pWF[b][par * 64:(par + 1) * 64, (pr % 8) * 64:(pr % 8 + 1) * 64]
                    S.add("pe", lambda e, cxslc=cxslc, tmslc=tmslc, dst=dst: e.matmul(
                        dst, cxslc, tmslc, start=True, stop=True),
                        reads=[cxslc, tmslc], writes=[pWF[b]], tag=f"{uid}wu{ch}.{h}")
            if AT1_LEVEL < 1:
                w1v = w1s.rearrange("p (pr o) -> p pr o", o=128)
                for b in range(2):
                    psv = pWF[b].rearrange("p (pr o) -> p pr o", o=64)
                    for par in range(2):
                        wslc = w1v[par * 64:(par + 1) * 64, b * 8:(b + 1) * 8,
                                   par * 64:(par + 1) * 64]
                        pslc = psv[par * 64:(par + 1) * 64, :, :]
                        S.add("ve", lambda e, wslc=wslc, pslc=pslc: e.tensor_tensor(
                            wslc, wslc, pslc, op=ALU.subtract),
                            reads=[wslc, pWF[b]], writes=[wslc],
                            tag=f"{uid}wus{ch}.{b}.{par}")
            if "b" not in AT1_SECTS:
                continue
            S.add("ve", lambda e: e.reduce_sum(
                red3, cog.rearrange("p (pr k) -> p pr k", k=16), axis=AX.X),
                reads=[cog], writes=[red3], tag=f"{uid}b1r{ch}")

            def tail_b1(ch=ch):
                for k in range(16):
                    b1v = b1s.rearrange("p (pr k) -> p k pr", k=16)[:, k, :]
                    S.add("ve", lambda e, b1v=b1v: e.tensor_tensor(b1v, b1v, red3, op=ALU.subtract),
                          reads=[b1v, red3], writes=[b1v], tag=f"{uid}b1u{ch}.{k}")
            pend[0] = tail_b1
            if "f" not in AT1_SECTS:
                continue

            def tail_fwd(ch=ch, cc=cc, XC=XC):
                bn_stats(zbe, f"{uid}bf{ch}")
                pmu2 = bcast(stat2[:, 0:256], f"{uid}bmu2{ch}")
                prs2 = bcast(stat[:, 256:512], f"{uid}brs2{ch}")
                S.add("ve", lambda e, pmu2=pmu2: e.tensor_tensor(t1s, zbe, pmu2, op=ALU.subtract),
                      reads=[zbe, pmu2], writes=[t1s], tag=f"{uid}f1{ch}")
                S.add("ve", lambda e, prs2=prs2: e.tensor_tensor(t2s, t1s, prs2, op=ALU.mult),
                      reads=[t1s, prs2], writes=[t2s], tag=f"{uid}f2{ch}")
                S.add("ve", lambda e: e.tensor_tensor(t1s, t2s, gfm, op=ALU.mult),
                      reads=[t2s, gfm], writes=[t1s], tag=f"{uid}f3{ch}")
                S.add("ve", lambda e: e.tensor_tensor(t2s, t1s, bfm, op=ALU.add),
                      reads=[t1s, bfm], writes=[t2s], tag=f"{uid}f4{ch}")
                S.add("ve", lambda e, XC=XC: e.tensor_tensor(
                    outf.rearrange("p (pr k) -> p pr k", k=16),
                    t2s.rearrange("p (pr k) -> p pr k", k=16), XC, op=ALU.add),
                    reads=[t2s, XC], writes=[outf], tag=f"{uid}fo{ch}")
                xw3 = xcwb.rearrange("p (pr t) -> p pr t", t=128)[:, :, cc * 16:(cc + 1) * 16]
                S.add("ve", lambda e, xw3=xw3: e.tensor_copy(
                    xw3, outf.rearrange("p (pr k) -> p pr k", k=16)),
                    reads=[outf], writes=[xw3], tag=f"{uid}xw{ch}")
            pend[1] = tail_fwd
        for pi in range(2):
            if pend[pi] is not None:
                pend[pi]()
                pend[pi] = None
        S.add("sc", lambda e, mg=mg: e.dma_start(
            out=sp3["x"][:, :, mg * 128:(mg + 1) * 128],
            in_=xcwb.rearrange("p (pr t) -> p pr t", t=128)),
            reads=[xcwb], writes=[sp3["x"][:, :, mg * 128:(mg + 1) * 128]],
            dma=f"{uid}xws", tag=f"{uid}xws{mg}")

    for g in range(NG):
        S.add("sy", lambda e, g=g: e.dma_start(
            out=xg[:, 0:16 * TG].rearrange("p (c t) -> p c t", t=TG),
            in_=x_in3[:, :, g * TG:(g + 1) * TG]),
            reads=[x_in3[:, :, g * TG:(g + 1) * TG]], writes=[xg[:, 0:16 * TG]],
            dma=f"{uid}xg", tag=f"{uid}ldx2{g}")
        S.add("sy", lambda e, g=g: e.dma_start(
            out=xn[:, 0:16 * TG].rearrange("p (c t) -> p c t", t=TG),
            in_=sp3["x"][:, :, g * TG:(g + 1) * TG]),
            reads=[sp3["x"][:, :, g * TG:(g + 1) * TG]], writes=[xn[:, 0:16 * TG]],
            dma=f"{uid}xcwl", tag=f"{uid}ldxc{g}")
        wm = W["wom1"]
        for o in range(16):
            b = o % 2
            wb = t[f"wgb{b}"].ap()
            S.add("sy", lambda e, wb=wb, o=o, wm=wm: e.dma_start(out=wb, in_=wm[o]),
                  reads=[wm[o]], writes=[wb], dma=f"{uid}w{b}", tag=f"{uid}ldwo{g}.{o}")
            for c in range(16):
                xnc = xn[:, c * TG:(c + 1) * TG]
                ws = wb[:, c * 128:(c + 1) * 128]
                S.add("pe", lambda e, ws=ws, xnc=xnc, c=c, b=b: e.matmul(
                    p_pr[b], ws, xnc, start=(c == 0), stop=(c == 15)),
                    reads=[ws, xnc], writes=[p_pr[b]], tag=f"{uid}om{g}.{o}.{c}")
            oslot = o % 2
            outr = t[f"outr{oslot}"].ap()[:, 0:TG]
            tsc = t["tscr"].ap()[:, 0:TG]
            S.add("ve", lambda e, b=b, tsc=tsc: e.tensor_copy(tsc, p_pr[b]),
                  reads=[p_pr[b]], writes=[tsc], tag=f"{uid}oev{g}.{o}")
            S.add("ve", lambda e, o=o, outr=outr, tsc=tsc: e.tensor_tensor(
                outr, tsc, xg[:, o * TG:(o + 1) * TG], op=ALU.add),
                reads=[tsc, xg[:, o * TG:(o + 1) * TG]], writes=[outr],
                tag=f"{uid}ores{g}.{o}")
            S.add("sc", lambda e, o=o, g=g, outr=outr, oslot=oslot: e.dma_start(
                out=x_out3[:, o, g * TG:(g + 1) * TG], in_=outr),
                reads=[outr], writes=[x_out3[:, o, g * TG:(g + 1) * TG]],
                dma=f"{uid}st{oslot}", tag=f"{uid}ostx{g}.{o}")


def pack_attn1_weights(Wq, Wk, Wv, Wo, lw, lb, ln_w, ln_b, W1i, b1i):
    d = {}
    d["wqm1"], d["wkm1"], d["wvm1"], d["wom1"] = (pack_proj(w) for w in (Wq, Wk, Wv, Wo))
    lwm = np.asarray(lw)[:, 0, :]
    d["lwt"] = np.ascontiguousarray(lwm.T.reshape(16, 128, 32).transpose(1, 0, 2)).reshape(128, 512)
    d["lbT"] = np.ascontiguousarray(np.asarray(lb).reshape(1, 32))
    gfm = np.zeros((128, 256), np.float32)
    bfm = np.zeros((128, 256), np.float32)
    b1fm = np.zeros((128, 256), np.float32)
    for par in range(2):
        for pr in range(16):
            h = 2 * pr + par
            gfm[par * 64:(par + 1) * 64, pr * 16:(pr + 1) * 16] = np.asarray(ln_w)[h][:, None]
            bfm[par * 64:(par + 1) * 64, pr * 16:(pr + 1) * 16] = np.asarray(ln_b)[h][:, None]
            b1fm[par * 64:(par + 1) * 64, pr * 16:(pr + 1) * 16] = np.asarray(b1i)[h, 0][:, None]
    d["gfm"], d["bfm"], d["b1fm"] = gfm, bfm, b1fm
    w1bd = np.zeros((16, 128, 128), np.float32)
    for pr in range(16):
        for par in range(2):
            w1bd[pr, par * 64:(par + 1) * 64, par * 64:(par + 1) * 64] = np.asarray(W1i)[2 * pr + par]
    d["w1bd"] = w1bd
    a = (1.0 / (64.0 * np.arange(1, 17, dtype=np.float64))).astype(np.float32)
    maskT = np.triu(np.ones((16, 16), np.float32))
    d["mkTnA"] = np.ascontiguousarray(np.tile(-(a[:, None]) * maskT, (1, 32)))
    d["csel"] = np.ascontiguousarray(np.kron(np.eye(32, dtype=np.float32), np.ones((1, 16), np.float32)))
    hind = np.zeros((2, 128), np.float32)
    hind[0, 0:64] = 1.0
    hind[1, 64:128] = 1.0
    d["hind"] = hind
    d["o2bd"] = np.ascontiguousarray(hind.T)
    d["a2c"] = np.ascontiguousarray(np.tile(a, (2, 16)).reshape(2, 256))
    d["aTMc"] = np.ascontiguousarray(a.reshape(16, 1))
    return d


def declare_attn1_weights(nc, prefix=""):
    d = {}
    shapes = dict(wqm1=[16, 128, 2048], wkm1=[16, 128, 2048], wvm1=[16, 128, 2048],
                  wom1=[16, 128, 2048], lwt=[128, 512], lbT=[1, 32],
                  gfm=[128, 256], bfm=[128, 256], b1fm=[128, 256],
                  w1bd=[16, 128, 128], mkTnA=[16, 512], csel=[32, 512],
                  hind=[2, 128], o2bd=[128, 2], a2c=[2, 256], aTMc=[16, 1])
    for n, s in shapes.items():
        d[n] = nc.dram_tensor(prefix + n + "_d", s, F32, kind="ExternalInput").ap()
    return d


def build_attn1_program():
    nc = bass.Bass("TRN2", target_bir_lowering=False)
    Lx = AT1_NG * AT1_TG
    x_in = nc.dram_tensor("x_in", [D, Lx], F32, kind="ExternalInput").ap()
    x_out = nc.dram_tensor("x_out", [D, Lx], F32, kind="ExternalOutput").ap()
    qs = nc.dram_tensor("q_s", [D, Lx], F32, kind="Internal").ap()
    ks = nc.dram_tensor("k_s", [D, Lx], F32, kind="Internal").ap()
    vs = nc.dram_tensor("v_s", [D, Lx], F32, kind="Internal").ap()
    xcws = nc.dram_tensor("xcw_s", [D, Lx], F32, kind="Internal").ap()
    sd = nc.dram_tensor("sd_s", [32, Lx], F32, kind="Internal").ap()
    sdT = nc.dram_tensor("sdT_s", [Lx, 32], F32, kind="Internal").ap()
    W = declare_attn1_weights(nc)
    W["sd_ap"] = sd
    W["sdT_ap"] = sdT
    if AT1_DBG:
        W["dbg"] = {}
        for dnm, shp in (("dz1e", [128, 256]), ("dgrad", [128, 256]), ("dcog", [128, 256]),
                         ("dzsb", [128, 256]), ("dzbe", [128, 256]), ("dxh", [128, 256]),
                         ("dg0", [128, 256]), ("dEt", [16, 512]), ("dAEt", [16, 512]),
                         ("dAT", [16, 512]), ("dTMg", [16, 2048]), ("dcxb", [16, 2048])):
            W["dbg"][dnm] = nc.dram_tensor(dnm, shp, F32, kind="Internal").ap()
    cones_d = nc.dram_tensor("cones_d", [128, 128], F32, kind="ExternalInput").ap()
    ceye_d = nc.dram_tensor("ceye_d", [128, 128], F32, kind="ExternalInput").ap()
    ctx = ExitStack()
    with ctx:
        A = Arena(nc, ctx)
        A.sb("cones", [128, 128])
        A.sb("ceye", [128, 128])
        alloc_shared(A)
        alloc_attn1(A)
        P = [A.ps(f"pb{i}", [128, 512]) for i in range(8)]
        S = Sched(nc, psum_names=[f"pb{i}" for i in range(8)])
        S.add("sy", lambda e: e.dma_start(out=A.t["cones"].ap(), in_=cones_d),
              writes=[A.t["cones"].ap()], dma="cones")
        S.add("sy", lambda e: e.dma_start(out=A.t["ceye"].ap(), in_=ceye_d),
              writes=[A.t["ceye"].ap()], dma="ceye")
        emit_consts(S, A)
        emit_attn1(S, A, P, W, x_in, x_out, qs, ks, vs, xcws, sd, sdT, uid="t")
        n = S.finalize(ctx)
    return nc, n


def host_pack_x(x_td):
    return np.ascontiguousarray(np.asarray(x_td).T)


def build_full_program():
    nc = bass.Bass("TRN2", target_bir_lowering=False)
    x_in = nc.dram_tensor("x_in", [D, L], F32, kind="ExternalInput").ap()
    x_out = nc.dram_tensor("x_out", [D, L], F32, kind="ExternalOutput").ap()
    xa = nc.dram_tensor("xa_s", [D, L], F32, kind="Internal").ap()
    xb = nc.dram_tensor("xb_s", [D, L], F32, kind="Internal").ap()
    qs = nc.dram_tensor("q_s", [D, L], F32, kind="Internal").ap()
    ks = nc.dram_tensor("k_s", [D, L], F32, kind="Internal").ap()
    vs = nc.dram_tensor("v_s", [D, L], F32, kind="Internal").ap()
    os_ = nc.dram_tensor("o_s", [D, L], F32, kind="Internal").ap()
    sd = nc.dram_tensor("sd_s", [32, L], F32, kind="Internal").ap()
    sdT = nc.dram_tensor("sdT_s", [L, 32], F32, kind="Internal").ap()
    Wf = declare_ffn0_weights(nc)
    Wa = declare_attn0_weights(nc)
    Wm = declare_moe1_weights(nc)
    Wt = declare_attn1_weights(nc)
    Wt["sd_ap"] = sd
    Wt["sdT_ap"] = sdT
    cones_d = nc.dram_tensor("cones_d", [128, 128], F32, kind="ExternalInput").ap()
    conesb_d = nc.dram_tensor("conesb_d", [128, 128], BF16, kind="ExternalInput").ap()
    ceye_d = nc.dram_tensor("ceye_d", [128, 128], F32, kind="ExternalInput").ap()
    ctx = ExitStack()
    with ctx:
        A = Arena(nc, ctx)
        A.sb("cones", [128, 128])
        A.sb("ceye", [128, 128])
        alloc_shared(A)
        alloc_ffn0(A)
        alloc_moe1(A)
        alloc_attn1(A)
        P = [A.ps(f"pb{i}", [128, 512]) for i in range(8)]
        S = Sched(nc, psum_names=[f"pb{i}" for i in range(8)])
        S.add("sy", lambda e: e.dma_start(out=A.t["cones"].ap(), in_=cones_d),
              writes=[A.t["cones"].ap()], dma="cones")
        S.add("sy", lambda e: e.dma_start(out=A.t["conesb"].ap(), in_=conesb_d),
              writes=[A.t["conesb"].ap()], dma="conesb")
        S.add("sy", lambda e: e.dma_start(out=A.t["ceye"].ap(), in_=ceye_d),
              writes=[A.t["ceye"].ap()], dma="ceye")
        emit_consts(S, A)
        emit_ffn0(S, A, P, Wf, x_in, xa, uid="f")
        emit_attn0(S, A, P, Wa, xa, xb, qs, ks, vs, os_, uid="a")
        emit_ffn0(S, A, P, Wf, xb, xa, uid="f")
        emit_moe1(S, A, P, Wm, xa, xb, uid="m")
        emit_attn1(S, A, P, Wt, xb, xa, qs, ks, vs, os_, sd, sdT, uid="t")
        emit_moe1(S, A, P, Wm, xa, x_out, uid="m")
        n = S.finalize(ctx)
    return nc, n


_PROGS = {}


def _get_progs():
    if not _PROGS:
        _PROGS["ffn0"] = build_ffn0_program()
        _PROGS["attn0"] = build_attn0_program()
        _PROGS["moe1"] = build_moe1_program()
        _PROGS["attn1"] = build_attn1_program()
    return _PROGS


def _pack_all(z):
    import ml_dtypes
    cones = np.ones((128, 128), np.float32)
    conesb = np.ones((128, 128), dtype=ml_dtypes.bfloat16)
    ceye = np.eye(128, dtype=np.float32)
    Wf = pack_ffn0_weights(z["L0_ffn_gate_w"], z["L0_ffn_gate_scale"],
                           z["L0_ffn_up_w"], z["L0_ffn_up_scale"],
                           z["L0_ffn_down_w"], z["L0_ffn_down_scale"])
    Wa = pack_attn0_weights(z["L0_attn_Wq"], z["L0_attn_bq"], z["L0_attn_Bq"], z["L0_attn_Aq"],
                            z["L0_attn_Wk"], z["L0_attn_bk"], z["L0_attn_Bk"], z["L0_attn_Ak"],
                            z["L0_attn_Wv"], z["L0_attn_bv"], z["L0_attn_Bv"], z["L0_attn_Av"],
                            z["L0_attn_Wo"], z["L0_attn_bo"], z["L0_attn_Bo"], z["L0_attn_Ao"])
    Wm = pack_moe1_weights(z["L1_ffn_Wr"], z["L1_ffn_We"], z["L1_ffn_be"])
    Wt = pack_attn1_weights(z["L1_attn_Wq"], z["L1_attn_Wk"], z["L1_attn_Wv"], z["L1_attn_Wo"],
                            z["L1_attn_linear_weight"], z["L1_attn_linear_bias"],
                            z["L1_attn_ln_weight"], z["L1_attn_ln_bias"],
                            z["L1_attn_W1_init"], z["L1_attn_b1_init"])
    base = dict(cones_d=cones, conesb_d=conesb, ceye_d=ceye, **Wf)
    base.update(wqm=Wa["wqm"], wkm=Wa["wkm"], wvm=Wa["wvm"], wom=Wa["wom"],
                lb4_d=Wa["lb4"], aqT4_d=Wa["aqT4"], bb_d=Wa["bb"])
    base.update(**Wm)
    base.update({k + "_d": v for k, v in Wt.items()})
    return base


def kernel(**inputs):
    if "full" not in _PROGS:
        _PROGS["full"] = build_full_program()
    nc, _ = _PROGS["full"]
    base = _pack_all(inputs)
    x = [host_pack_x(np.asarray(inputs["L0_attn_x"][b], np.float32)) for b in range(NCORES)]
    maps = []
    for b in range(NCORES):
        m = dict(base)
        m["x_in"] = x[b]
        maps.append(m)
    res = run_bass_kernel_spmd(nc, maps, core_ids=list(range(NCORES)))
    out = np.stack([np.ascontiguousarray(np.asarray(r["x_out"]).T) for r in res.results])
    return out.astype(np.float32)


def kernel_unfused(**inputs):
    progs = _get_progs()
    z = inputs
    cones = np.ones((128, 128), np.float32)
    ceye = np.eye(128, dtype=np.float32)
    Wf = pack_ffn0_weights(z["L0_ffn_gate_w"], z["L0_ffn_gate_scale"],
                           z["L0_ffn_up_w"], z["L0_ffn_up_scale"],
                           z["L0_ffn_down_w"], z["L0_ffn_down_scale"])
    Wa = pack_attn0_weights(z["L0_attn_Wq"], z["L0_attn_bq"], z["L0_attn_Bq"], z["L0_attn_Aq"],
                            z["L0_attn_Wk"], z["L0_attn_bk"], z["L0_attn_Bk"], z["L0_attn_Ak"],
                            z["L0_attn_Wv"], z["L0_attn_bv"], z["L0_attn_Bv"], z["L0_attn_Av"],
                            z["L0_attn_Wo"], z["L0_attn_bo"], z["L0_attn_Bo"], z["L0_attn_Ao"])
    Wm = pack_moe1_weights(z["L1_ffn_Wr"], z["L1_ffn_We"], z["L1_ffn_be"])
    Wt = pack_attn1_weights(z["L1_attn_Wq"], z["L1_attn_Wk"], z["L1_attn_Wv"], z["L1_attn_Wo"],
                            z["L1_attn_linear_weight"], z["L1_attn_linear_bias"],
                            z["L1_attn_ln_weight"], z["L1_attn_ln_bias"],
                            z["L1_attn_W1_init"], z["L1_attn_b1_init"])
    x = [host_pack_x(np.asarray(z["L0_attn_x"][b], np.float32)) for b in range(NCORES)]

    def launch(key, extra, xs):
        nc, _ = progs[key]
        maps = []
        for b in range(NCORES):
            m = dict(extra)
            m["x_in"] = xs[b]
            maps.append(m)
        res = run_bass_kernel_spmd(nc, maps, core_ids=list(range(NCORES)))
        return [np.asarray(r["x_out"]) for r in res.results]

    ef = dict(cones_d=cones, **Wf)
    ea = dict(cones_d=cones, ceye_d=ceye,
              wqm=Wa["wqm"], wkm=Wa["wkm"], wvm=Wa["wvm"], wom=Wa["wom"],
              lb4_d=Wa["lb4"], aqT4_d=Wa["aqT4"], bb_d=Wa["bb"])
    em = dict(cones_d=cones, ceye_d=ceye, **Wm)
    et = {k + "_d": v for k, v in Wt.items()}
    et.update(cones_d=cones, ceye_d=ceye)

    x = launch("ffn0", ef, x)
    x = launch("attn0", ea, x)
    x = launch("ffn0", ef, x)
    x = launch("moe1", em, x)
    x = launch("attn1", et, x)
    x = launch("moe1", em, x)
    out = np.stack([np.ascontiguousarray(xb.T) for xb in x])
    return out.astype(np.float32)


# version 07
# speedup vs baseline: 1.4741x; 1.1915x over previous
import numpy as np
from contextlib import ExitStack

import concourse.bass as bass
import concourse.mybir as mybir
from concourse.bass_utils import run_bass_kernel_spmd

F32 = mybir.dt.float32
BF16 = mybir.dt.bfloat16
I32 = mybir.dt.int32
U8C = mybir.dt.uint8
AF = mybir.ActivationFunctionType
ALU = mybir.AluOpType
AX = mybir.AxisListType

D = 2048
L = 2048
NCORES = 8


class _Sem:
    __slots__ = ("name", "h", "count")
    def __init__(self, name):
        self.name = name
        self.h = None
        self.count = 0


class _Op:
    __slots__ = ("idx", "stream", "emit", "deps", "is_dma", "sem", "inc", "cum", "tag")


def _ap_interval(ap):
    t = ap.tensor
    lo = ap.offset
    hi = ap.offset
    for step, count in ap.ap:
        hi += step * (count - 1)
    if str(t.space).endswith("SBUF"):
        row = 1
        for s in t.shape[1:]:
            row *= s
        itemsize = t.bytes() // t.size
        lo_c = ap.offset % row
        hi_c = lo_c
        for step, count in ap.ap:
            if step % row != 0:
                hi_c += step * (count - 1)
        return t.name, lo_c * itemsize, (hi_c + 1) * itemsize
    return t.name, min(lo, hi), max(lo, hi) + 1


class Sched:
    STREAMS = ("pe", "ve", "sc", "gp", "sy")

    def __init__(self, nc, psum_names=()):
        self.nc = nc
        self.ops = []
        self.sems = {}
        self.psum_names = set(psum_names)
        self.mem = {}

    def dsem(self, name):
        if name not in self.sems:
            self.sems[name] = _Sem(name)
        return name

    def _access(self, ap, op, is_write, deps):
        name, lo, hi = _ap_interval(ap)
        if name in self.psum_names:
            lo, hi = 0, 1 << 40
        rec = self.mem.setdefault(name, {"w": [], "r": []})
        if is_write:
            for (l2, h2, o2) in rec["w"]:
                if l2 < hi and lo < h2:
                    deps.append(o2)
            for (l2, h2, o2) in rec["r"]:
                if l2 < hi and lo < h2:
                    deps.append(o2)
            rec["w"] = [(l2, h2, o2) for (l2, h2, o2) in rec["w"]
                        if not (lo <= l2 and h2 <= hi)]
            rec["r"] = [(l2, h2, o2) for (l2, h2, o2) in rec["r"]
                        if not (l2 < hi and lo < h2)]
            rec["w"].append((lo, hi, op))
        else:
            for (l2, h2, o2) in rec["w"]:
                if l2 < hi and lo < h2:
                    deps.append(o2)
            rec["r"].append((lo, hi, op))

    def add(self, stream, emit, reads=(), writes=(), deps=(), dma=None, tag=""):
        op = _Op()
        op.idx = len(self.ops)
        op.stream = stream
        op.emit = emit
        op.is_dma = dma is not None
        op.sem = self.dsem(dma) if dma is not None else None
        op.inc = False
        op.cum = None
        op.tag = tag
        dd = [d for d in deps if d is not None]
        for ap in reads:
            self._access(ap, op, False, dd)
        for ap in writes:
            self._access(ap, op, True, dd)
        seen = set()
        op.deps = []
        for d in dd:
            if d is not op and id(d) not in seen:
                seen.add(id(d))
                assert d.idx < op.idx, f"forward dep at op {op.idx} ({tag})"
                op.deps.append(d)
        self.ops.append(op)
        return op

    def finalize(self, ctx):
        nc = self.nc
        for op in self.ops:
            for d in op.deps:
                if d.is_dma:
                    continue
                if d.stream == "pe" and op.stream == "pe" and not op.is_dma:
                    continue
                d.inc = True
        for op in self.ops:
            if op.is_dma:
                s = self.sems[op.sem]
                s.count += 16
                op.cum = s.count
            elif op.inc:
                sname = "e_" + op.stream
                op.sem = self.dsem(sname)
                s = self.sems[sname]
                s.count += 1
                op.cum = s.count
        for s in self.sems.values():
            s.h = ctx.enter_context(nc.semaphore(s.name))
        per = {st: [op for op in self.ops if op.stream == st] for st in self.STREAMS}
        block = ctx.enter_context(nc.Block())
        waited = {st: {} for st in self.STREAMS}

        def run_stream(st, eng):
            w = waited[st]
            for op in per[st]:
                for d in op.deps:
                    if d.cum is None:
                        continue
                    if (not d.is_dma) and d.stream == "pe" and st == "pe" and not op.is_dma:
                        continue
                    if w.get(d.sem, 0) < d.cum:
                        eng.wait_ge(self.sems[d.sem].h, d.cum)
                        w[d.sem] = d.cum
                inst = op.emit(eng)
                if op.is_dma:
                    inst.then_inc(self.sems[op.sem].h, 16)
                elif op.inc:
                    inst.then_inc(self.sems[op.sem].h, 1)
            if st == "sy":
                for s in self.sems.values():
                    if not s.name.startswith("e_") and s.count > 0:
                        if w.get(s.name, 0) < s.count:
                            eng.wait_ge(s.h, s.count)
                            w[s.name] = s.count

        for st, attr in (("pe", "tensor"), ("ve", "vector"), ("sc", "scalar"),
                         ("gp", "gpsimd"), ("sy", "sync")):
            if per[st] or st == "sy":
                getattr(block, attr)(lambda eng, _st=st: run_stream(_st, eng))
        return len(self.ops)


class Arena:
    def __init__(self, nc, ctx):
        self.nc = nc
        self.ctx = ctx
        self.t = {}

    def sb(self, name, shape, dtype=F32):
        h = self.ctx.enter_context(self.nc.sbuf_tensor(name, shape, dtype))
        self.t[name] = h
        return h

    def ps(self, name, shape, dtype=F32):
        h = self.ctx.enter_context(self.nc.psum_tensor(name, shape, dtype))
        self.t[name] = h
        return h



FFN_TG = 256
FFN_NG = L // FFN_TG
FFN_NI = 64


def alloc_shared(A):
    A.sb("conesb", [128, 128], BF16)
    A.sb("xg", [128, 16 * 512])
    A.sb("xn", [128, 16 * 512])
    A.sb("slab", [128, 64 * 256])
    for b in range(2):
        A.sb(f"wgb{b}", [128, 2048])
        A.sb(f"wub{b}", [128, 2048])
        A.sb(f"wdb{b}", [128, 2048])
    A.sb("rowsA", [1, 512])
    A.sb("rowsB", [1, 512])
    A.sb("rowsC", [1, 512])
    A.sb("rowsD", [1, 512])
    A.sb("epsb", [1, 1])
    A.sb("tscr", [128, 2 * 512])
    for b in range(2):
        A.sb(f"outr{b}", [128, 512])


def alloc_ffn0(A):
    A.sb("sgt", [128, 64])
    A.sb("sut", [128, 64])
    A.sb("sdt", [128, 16])


def emit_consts(S, A):
    eps = A.t["epsb"].ap()
    S.add("ve", lambda e: e.memset(eps, 1e-5), writes=[eps], tag="eps")


_ROWMAP = {0: "rowsA", 1: "rowsD", 2: "rowsC", 3: "rowsD", 4: "rowsC",
           5: "rowsB", 6: "rowsC"}


def row_slicer(A, TG):
    def R(i):
        return A.t[_ROWMAP[i]].ap()[0:1, 0:TG]
    return R


def emit_ln(S, A, P, xg, xn, TG, tag, split_hi=None, split_lo=None):
    t = A.t
    cones = t["cones"].ap()
    eps = t["epsb"].ap()
    sq = t["tscr"].ap()
    tscr = t["tscr"].ap()
    R = row_slicer(A, TG)

    p_sx = P[6].ap()[0:1, 0:TG]
    p_sq = P[7].ap()[0:1, 0:TG]
    p_mb = P[4].ap()[:, 0:TG]
    p_rb = P[5].ap()[:, 0:TG]

    for c in range(16):
        xgc = xg[:, c * TG:(c + 1) * TG]
        sqs = sq[:, (c % 2) * TG:(c % 2 + 1) * TG]
        S.add("pe", lambda e, xgc=xgc, c=c: e.matmul(
            p_sx, cones[:, 0:1], xgc, start=(c == 0), stop=(c == 15)),
            reads=[cones[:, 0:1], xgc], writes=[p_sx], tag=f"{tag}sx{c}")
        S.add("sc", lambda e, xgc=xgc, sqs=sqs: e.activation(
            out=sqs, in_=xgc, func=AF.Square),
            reads=[xgc], writes=[sqs], tag=f"{tag}sq{c}")
        S.add("pe", lambda e, sqs=sqs, c=c: e.matmul(
            p_sq, cones[:, 0:1], sqs, start=(c == 0), stop=(c == 15)),
            reads=[cones[:, 0:1], sqs], writes=[p_sq], tag=f"{tag}sqm{c}")
    S.add("ve", lambda e: e.tensor_scalar(R(0), p_sx, -1.0 / D, None, op0=ALU.mult),
          reads=[p_sx], writes=[R(0)], tag=f"{tag}negm")
    S.add("ve", lambda e: e.tensor_scalar(R(1), p_sq, 1.0 / D, None, op0=ALU.mult),
          reads=[p_sq], writes=[R(1)], tag=f"{tag}ex2")
    S.add("ve", lambda e: e.tensor_tensor(R(2), R(0), R(0), op=ALU.mult),
          reads=[R(0)], writes=[R(2)], tag=f"{tag}m2")
    S.add("ve", lambda e: e.tensor_tensor(R(3), R(1), R(2), op=ALU.subtract),
          reads=[R(1), R(2)], writes=[R(3)], tag=f"{tag}var")
    S.add("sc", lambda e: e.activation(out=R(4), in_=R(3), func=AF.Sqrt, bias=eps),
          reads=[R(3), eps], writes=[R(4)], tag=f"{tag}std")
    S.add("ve", lambda e: e.reciprocal(R(5), R(4)),
          reads=[R(4)], writes=[R(5)], tag=f"{tag}r")
    S.add("pe", lambda e: e.matmul(p_mb, cones[0:1, :], R(0), start=True, stop=True),
          reads=[cones[0:1, :], R(0)], writes=[p_mb], tag=f"{tag}mb")
    S.add("pe", lambda e: e.matmul(p_rb, cones[0:1, :], R(5), start=True, stop=True),
          reads=[cones[0:1, :], R(5)], writes=[p_rb], tag=f"{tag}rb")
    for c in range(16):
        xgc = xg[:, c * TG:(c + 1) * TG]
        tsl = tscr[:, (c % 2) * TG:(c % 2 + 1) * TG]
        if split_hi is None:
            xnc = xn[:, c * TG:(c + 1) * TG]
        else:
            xnc = tscr[:, (2 + c % 2) * TG:(3 + c % 2) * TG]
        S.add("ve", lambda e, xgc=xgc, tsl=tsl: e.tensor_tensor(
            tsl, xgc, p_mb, op=ALU.add),
            reads=[xgc, p_mb], writes=[tsl], tag=f"{tag}xna{c}")
        S.add("ve", lambda e, xnc=xnc, tsl=tsl: e.tensor_tensor(
            xnc, tsl, p_rb, op=ALU.mult),
            reads=[tsl, p_rb], writes=[xnc], tag=f"{tag}xnb{c}")
        if split_hi is not None:
            hic = split_hi[:, c * TG:(c + 1) * TG]
            loc = split_lo[:, c * TG:(c + 1) * TG]
            S.add("sc", lambda e, xnc=xnc, hic=hic: e.copy(out=hic, in_=xnc),
                  reads=[xnc], writes=[hic], tag=f"{tag}xh{c}")
            S.add("ve", lambda e, xnc=xnc, hic=hic, loc=loc: e.tensor_tensor(
                loc, xnc, hic, op=ALU.subtract),
                reads=[xnc, hic], writes=[loc], tag=f"{tag}xl{c}")


def emit_ffn0(S, A, P, W, x_in, x_out, uid):
    t = A.t
    TG, NG, NI = FFN_TG, FFN_NG, FFN_NI
    xg, xn = t["xg"].ap(), t["xn"].ap()
    xnb = xn.bitcast(BF16)
    xnh = xnb[:, 0:16 * TG]
    xnl = xnb[:, 16 * TG:32 * TG]
    slabb = t["slab"].ap().bitcast(BF16)
    hhh = slabb[:, 0:NI * TG]
    hhl = slabb[:, NI * TG:2 * NI * TG]
    conesb = t["conesb"].ap()
    cones = t["cones"].ap()
    sgt, sut, sdt = t["sgt"].ap(), t["sut"].ap(), t["sdt"].ap()
    gs = xn[:, 4096:4096 + TG]
    gsil = xn[:, 4352:4352 + TG]
    us = xn[:, 4608:4608 + TG]
    tscr = t["tscr"].ap()
    R = row_slicer(A, TG)

    x_in3 = x_in.rearrange("(c p) t -> p c t", p=128)
    x_out3 = x_out.rearrange("(c p) t -> p c t", p=128)

    S.add("sy", lambda e: e.dma_start(out=sgt, in_=W["sg"]), writes=[sgt], dma=f"{uid}p0")
    S.add("sy", lambda e: e.dma_start(out=sut, in_=W["su"]), writes=[sut], dma=f"{uid}p1")
    S.add("sy", lambda e: e.dma_start(out=sdt, in_=W["sd"]), writes=[sdt], dma=f"{uid}p2")

    p_g = [P[0].ap()[:, 0:TG], P[1].ap()[:, 0:TG]]
    p_u = [P[2].ap()[:, 0:TG], P[3].ap()[:, 0:TG]]
    p_mb = P[4].ap()[:, 0:TG]
    p_rb = P[5].ap()[:, 0:TG]
    p_d = [P[4].ap()[:, 0:TG], P[5].ap()[:, 0:TG]]
    p_sx = P[6].ap()[0:1, 0:TG]
    p_cs = P[6].ap()[0:1, 256:256 + TG]
    p_sq = P[7].ap()[0:1, 0:TG]

    for g in range(NG):
        S.add("sy", lambda e, g=g: e.dma_start(
            out=xg[:, 0:16 * TG].rearrange("p (c t) -> p c t", t=TG),
            in_=x_in3[:, :, g * TG:(g + 1) * TG]),
            reads=[x_in3[:, :, g * TG:(g + 1) * TG]], writes=[xg[:, 0:16 * TG]],
            dma=f"{uid}xg", tag=f"ldx{g}")
        emit_ln(S, A, P, xg, xn, TG, tag=f"{uid}ln{g}", split_hi=xnh, split_lo=xnl)
        def ldcast(i):
            b = i % 2
            wgb = t[f"wgb{b}"].ap().bitcast(BF16)[:, 0:2048]
            wub = t[f"wub{b}"].ap().bitcast(BF16)[:, 0:2048]
            stg = t[f"wgb{b}"].ap().bitcast(U8C)[:, 4096:8192]
            S.add("sy", lambda e, stg=stg, i=i: e.dma_start(out=stg, in_=W["wgm"][i]),
                  reads=[W["wgm"][i]], writes=[stg], dma=f"{uid}wg{b}", tag=f"ldwg{g}.{i}")
            S.add("ve", lambda e, wgb=wgb, stg=stg: e.tensor_copy(wgb, stg[:, 0:2048]),
                  reads=[stg[:, 0:2048]], writes=[wgb], tag=f"cvg{g}.{i}")
            S.add("sc", lambda e, wub=wub, stg=stg: e.copy(out=wub, in_=stg[:, 2048:4096]),
                  reads=[stg[:, 2048:4096]], writes=[wub], tag=f"cvu{g}.{i}")

        def gumms(i):
            b = i % 2
            wgb = t[f"wgb{b}"].ap().bitcast(BF16)[:, 0:2048]
            wub = t[f"wub{b}"].ap().bitcast(BF16)[:, 0:2048]
            for hl, xsrc in ((0, xnh), (1, xnl)):
                for c in range(16):
                    xnc = xsrc[:, c * TG:(c + 1) * TG]
                    ws = wgb[:, c * 128:(c + 1) * 128]
                    S.add("pe", lambda e, ws=ws, xnc=xnc, c=c, b=b, hl=hl: e.matmul(
                        p_g[b], ws, xnc, start=(hl == 0 and c == 0), stop=(hl == 1 and c == 15)),
                        reads=[ws, xnc], writes=[p_g[b]], tag=f"mmg{g}.{i}.{hl}.{c}")
            for hl, xsrc in ((0, xnh), (1, xnl)):
                for c in range(16):
                    xnc = xsrc[:, c * TG:(c + 1) * TG]
                    ws = wub[:, c * 128:(c + 1) * 128]
                    S.add("pe", lambda e, ws=ws, xnc=xnc, c=c, b=b, hl=hl: e.matmul(
                        p_u[b], ws, xnc, start=(hl == 0 and c == 0), stop=(hl == 1 and c == 15)),
                        reads=[ws, xnc], writes=[p_u[b]], tag=f"mmu{g}.{i}.{hl}.{c}")

        def guevac(i):
            b = i % 2
            S.add("ve", lambda e, i=i, b=b: e.tensor_scalar(
                gs, p_g[b], sgt[:, i:i + 1], None, op0=ALU.mult),
                reads=[p_g[b], sgt[:, i:i + 1]], writes=[gs], tag=f"gs{g}.{i}")
            S.add("sc", lambda e: e.activation(out=gsil, in_=gs, func=AF.Silu),
                  reads=[gs], writes=[gsil], tag=f"sil{g}.{i}")
            S.add("ve", lambda e, i=i, b=b: e.tensor_scalar(
                us, p_u[b], sut[:, i:i + 1], None, op0=ALU.mult),
                reads=[p_u[b], sut[:, i:i + 1]], writes=[us], tag=f"us{g}.{i}")
            S.add("ve", lambda e: e.tensor_tensor(gs, gsil, us, op=ALU.mult),
                  reads=[gsil, us], writes=[gs], tag=f"h{g}.{i}")
            hih = hhh[:, i * TG:(i + 1) * TG]
            hil = hhl[:, i * TG:(i + 1) * TG]
            S.add("sc", lambda e, hih=hih: e.copy(out=hih, in_=gs),
                  reads=[gs], writes=[hih], tag=f"hh{g}.{i}")
            S.add("ve", lambda e, hih=hih, hil=hil: e.tensor_tensor(
                hil, gs, hih, op=ALU.subtract),
                reads=[gs, hih], writes=[hil], tag=f"hl{g}.{i}")
            S.add("pe", lambda e, hih=hih, i=i: e.matmul(
                p_cs, conesb[:, 0:1], hih, start=(i == 0), stop=False),
                reads=[conesb[:, 0:1], hih], writes=[p_cs], tag=f"cs{g}.{i}")
            S.add("pe", lambda e, hil=hil, i=i: e.matmul(
                p_cs, conesb[:, 0:1], hil, start=False, stop=(i == NI - 1)),
                reads=[conesb[:, 0:1], hil], writes=[p_cs], tag=f"cs2{g}.{i}")

        for i in range(NI):
            ldcast(i)
            gumms(i)
            guevac(i)
        S.add("ve", lambda e: e.tensor_scalar(R(6), p_cs, -128.0, None, op0=ALU.mult),
              reads=[p_cs], writes=[R(6)], tag=f"csr{g}")
        NQ = (NI + 15) // 16
        for o in range(16):
            b = o % 2
            for qp in range(max(1, NQ // 2)):
                stg = t[f"wdb{qp % 2}"].ap().bitcast(U8C)[:, 4096:8192]
                S.add("sy", lambda e, stg=stg, o=o, qp=qp: e.dma_start(
                    out=stg, in_=W["wdm"][o, qp]),
                    reads=[W["wdm"][o, qp]], writes=[stg], dma=f"{uid}wd{qp % 2}",
                    tag=f"ldwd{g}.{o}.{qp}")
            for q in range(NQ):
                bb = q % 2
                wdb = t[f"wdb{bb}"].ap().bitcast(BF16)[:, 0:2048]
                srcq = t[f"wdb{q // 2}"].ap().bitcast(U8C)[:, 4096 + (q % 2) * 2048: 4096 + (q % 2 + 1) * 2048]
                eng = "ve" if q % 2 == 0 else "sc"
                if eng == "ve":
                    S.add("ve", lambda e, wdb=wdb, srcq=srcq: e.tensor_copy(wdb, srcq),
                          reads=[srcq], writes=[wdb], tag=f"cvd{g}.{o}.{q}")
                else:
                    S.add("sc", lambda e, wdb=wdb, srcq=srcq: e.copy(out=wdb, in_=srcq),
                          reads=[srcq], writes=[wdb], tag=f"cvd{g}.{o}.{q}")
                for ii in range(min(16, NI - q * 16)):
                    iglob = q * 16 + ii
                    ws = wdb[:, ii * 128:(ii + 1) * 128]
                    for hl, hsrc in ((0, hhh), (1, hhl)):
                        hsl = hsrc[:, iglob * TG:(iglob + 1) * TG]
                        S.add("pe", lambda e, ws=ws, hsl=hsl, q=q, ii=ii, b=b, hl=hl: e.matmul(
                            p_d[b], ws, hsl, start=(q == 0 and ii == 0 and hl == 0), stop=False),
                            reads=[ws, hsl], writes=[p_d[b]], tag=f"mmd{g}.{o}.{q}.{ii}.{hl}")
            S.add("pe", lambda e, b=b: e.matmul(
                p_d[b], cones[0:1, :], R(6), start=False, stop=True),
                reads=[cones[0:1, :], R(6)], writes=[p_d[b]], tag=f"rk{g}.{o}")
            oslot = o % 2
            outr = t[f"outr{oslot}"].ap()[:, 0:TG]
            xres = t[f"outr{oslot}"].ap()[:, 256:256 + TG]
            S.add("sc", lambda e, o=o, g=g, xres=xres: e.dma_start(
                out=xres, in_=x_in3[:, o, g * TG:(g + 1) * TG]),
                reads=[x_in3[:, o, g * TG:(g + 1) * TG]], writes=[xres],
                dma=f"{uid}xr{oslot}", tag=f"ldxr{g}.{o}")
            S.add("ve", lambda e, o=o, b=b: e.tensor_scalar(
                tscr[:, 0:TG], p_d[b], sdt[:, o:o + 1], 0.5, op0=ALU.mult, op1=ALU.mult),
                reads=[p_d[b], sdt[:, o:o + 1]], writes=[tscr[:, 0:TG]], tag=f"dts{g}.{o}")
            S.add("ve", lambda e, o=o, outr=outr, xres=xres: e.tensor_tensor(
                outr, tscr[:, 0:TG], xres, op=ALU.add),
                reads=[tscr[:, 0:TG], xres], writes=[outr],
                tag=f"res{g}.{o}")
            S.add("sc", lambda e, o=o, g=g, outr=outr: e.dma_start(
                out=x_out3[:, o, g * TG:(g + 1) * TG], in_=outr),
                reads=[outr], writes=[x_out3[:, o, g * TG:(g + 1) * TG]],
                dma=f"{uid}st{oslot}", tag=f"st{g}.{o}")



def pack_ffn0_weights(gate_w, gate_scale, up_w, up_scale, down_w, down_scale):
    g = np.ascontiguousarray(
        np.asarray(gate_w).reshape(64, 128, 16, 128).transpose(0, 3, 2, 1)).reshape(64, 128, 2048).astype(np.uint8)
    u = np.ascontiguousarray(
        np.asarray(up_w).reshape(64, 128, 16, 128).transpose(0, 3, 2, 1)).reshape(64, 128, 2048).astype(np.uint8)
    g = np.ascontiguousarray(np.concatenate([g, u], axis=2))
    dnw = np.asarray(down_w).reshape(16, 128, 64, 128)
    dn = np.ascontiguousarray(
        dnw.transpose(0, 2, 3, 1).reshape(16, 4, 16, 128, 128).transpose(0, 1, 3, 2, 4)
    ).reshape(16, 4, 128, 2048).astype(np.uint8)
    dn = np.ascontiguousarray(dn.reshape(16, 2, 128 * 0 + 2, 128, 2048).reshape(16, 2, 2, 128, 2048)
                              .transpose(0, 1, 3, 2, 4)).reshape(16, 2, 128, 4096)
    sg = np.ascontiguousarray(np.asarray(gate_scale).reshape(64, 128).T)
    su = np.ascontiguousarray(np.asarray(up_scale).reshape(64, 128).T)
    sd = np.ascontiguousarray(np.asarray(down_scale).reshape(16, 128).T)
    return dict(wgm=g, wdm=dn, sg=sg, su=su, sd=sd)


def declare_ffn0_weights(nc, prefix=""):
    d = {}
    U8 = mybir.dt.uint8
    d["wgm"] = nc.dram_tensor(prefix + "wgm", [64, 128, 4096], U8, kind="ExternalInput").ap()
    d["wdm"] = nc.dram_tensor(prefix + "wdm", [16, 2, 128, 4096], U8, kind="ExternalInput").ap()
    d["sg"] = nc.dram_tensor(prefix + "sg", [128, 64], F32, kind="ExternalInput").ap()
    d["su"] = nc.dram_tensor(prefix + "su", [128, 64], F32, kind="ExternalInput").ap()
    d["sd"] = nc.dram_tensor(prefix + "sd", [128, 16], F32, kind="ExternalInput").ap()
    return d


def build_ffn0_program():
    nc = bass.Bass("TRN2", target_bir_lowering=False)
    Lx = FFN_NG * FFN_TG
    x_in = nc.dram_tensor("x_in", [D, Lx], F32, kind="ExternalInput").ap()
    x_out = nc.dram_tensor("x_out", [D, Lx], F32, kind="ExternalOutput").ap()
    W = declare_ffn0_weights(nc)
    cones_d = nc.dram_tensor("cones_d", [128, 128], F32, kind="ExternalInput").ap()
    conesb_d = nc.dram_tensor("conesb_d", [128, 128], BF16, kind="ExternalInput").ap()
    ctx = ExitStack()
    with ctx:
        A = Arena(nc, ctx)
        A.sb("cones", [128, 128])
        alloc_shared(A)
        alloc_ffn0(A)
        P = [A.ps(f"pb{i}", [128, 512]) for i in range(8)]
        S = Sched(nc, psum_names=[f"pb{i}" for i in range(8)])
        S.add("sy", lambda e: e.dma_start(out=A.t["cones"].ap(), in_=cones_d),
              writes=[A.t["cones"].ap()], dma="cones")
        S.add("sy", lambda e: e.dma_start(out=A.t["conesb"].ap(), in_=conesb_d),
              writes=[A.t["conesb"].ap()], dma="conesb")
        emit_consts(S, A)
        emit_ffn0(S, A, P, W, x_in, x_out, uid="f")
        n = S.finalize(ctx)
    return nc, n





MOE_TG = 512
MOE_NG = L // MOE_TG


def alloc_moe1(A):
    A.sb("wrt", [128, 16 * 8])
    A.sb("gate", [128, 32])
    A.sb("gsc", [128, 32])
    A.sb("lsb", [128, 8])
    A.sb("g8", [8, 512])
    A.sb("sel8", [8, 1024])
    A.sb("beC", [128, 128])
    A.sb("gbsb", [128, 512])


def emit_moe1(S, A, P, W, x_in, x_out, uid):
    t = A.t
    TG, NG = MOE_TG, MOE_NG
    xg, xn = t["xg"].ap(), t["xn"].ap()
    slab = t["slab"].ap()
    acc = [slab[:, o * TG:(o + 1) * TG] for o in range(16)]
    cones = t["cones"].ap()
    ceye = t["ceye"].ap()
    tscr = t["tscr"].ap()
    wrt = t["wrt"].ap()
    gate = t["gate"].ap()
    gsc = t["gsc"].ap()
    lsb = t["lsb"].ap()
    g8 = t["g8"].ap()
    sel8 = t["sel8"].ap()
    beC = t["beC"].ap()
    gbsb = t["gbsb"].ap()

    x_in3 = x_in.rearrange("(c p) t -> p c t", p=128)
    x_out3 = x_out.rearrange("(c p) t -> p c t", p=128)

    S.add("sy", lambda e: e.dma_start(
        out=wrt.rearrange("p (c r) -> p c r", r=8),
        in_=W["wr"].rearrange("(c p) r -> p c r", p=128)),
        writes=[wrt], dma=f"{uid}p0")
    S.add("sy", lambda e: e.dma_start(out=sel8, in_=W["sel8"]),
          writes=[sel8], dma=f"{uid}p1")
    S.add("sy", lambda e: e.dma_start(out=beC, in_=W["beC"]),
          writes=[beC], dma=f"{uid}p2")

    p_y = [P[0].ap()[:, 0:TG], P[1].ap()[:, 0:TG]]
    p_gb = P[2].ap()[:, 0:TG]
    p_tr = P[3].ap()[0:8, 0:128]
    p_l = P[3].ap()[:, 256:264]

    def GS(i):
        return gsc[:, i * 8:(i + 1) * 8]

    def GC(i):
        return gsc[:, 24 + i:25 + i]

    for g in range(NG):
        S.add("sy", lambda e, g=g: e.dma_start(
            out=xg[:, 0:16 * TG].rearrange("p (c t) -> p c t", t=TG),
            in_=x_in3[:, :, g * TG:(g + 1) * TG]),
            reads=[x_in3[:, :, g * TG:(g + 1) * TG]], writes=[xg[:, 0:16 * TG]],
            dma=f"{uid}xg", tag=f"{uid}ldx{g}")
        emit_ln(S, A, P, xg, xn, TG, tag=f"{uid}ln{g}")
        for tc in range(4):
            for c in range(16):
                xsl = xn[:, c * TG + tc * 128: c * TG + tc * 128 + 128]
                S.add("pe", lambda e, xsl=xsl, c=c: e.matmul(
                    p_l, xsl, wrt[:, c * 8:(c + 1) * 8], start=(c == 0), stop=(c == 15)),
                    reads=[xsl, wrt[:, c * 8:(c + 1) * 8]], writes=[p_l],
                    tag=f"{uid}lg{g}.{tc}.{c}")
            S.add("ve", lambda e: e.tensor_copy(lsb, p_l), reads=[p_l], writes=[lsb])
            S.add("ve", lambda e: e.reduce_max(GC(0), lsb, axis=AX.X),
                  reads=[lsb], writes=[GC(0)], tag=f"{uid}m1")
            S.add("ve", lambda e: e.tensor_scalar(GS(0), lsb, GC(0), None, op0=ALU.is_equal),
                  reads=[lsb, GC(0)], writes=[GS(0)], tag=f"{uid}eq1")
            S.add("ve", lambda e: e.tensor_scalar(GS(1), GS(0), -1e30, None, op0=ALU.mult),
                  reads=[GS(0)], writes=[GS(1)])
            S.add("ve", lambda e: e.tensor_tensor(GS(1), lsb, GS(1), op=ALU.add),
                  reads=[lsb, GS(1)], writes=[GS(1)])
            S.add("ve", lambda e: e.reduce_max(GC(1), GS(1), axis=AX.X),
                  reads=[GS(1)], writes=[GC(1)], tag=f"{uid}m2")
            S.add("ve", lambda e: e.tensor_scalar(GS(1), lsb, GC(1), None, op0=ALU.is_equal),
                  reads=[lsb, GC(1)], writes=[GS(1)], tag=f"{uid}eq2")
            S.add("ve", lambda e: e.tensor_tensor(GS(0), GS(0), GS(1), op=ALU.add),
                  reads=[GS(0), GS(1)], writes=[GS(0)], tag=f"{uid}sel")
            S.add("ve", lambda e: e.tensor_scalar(GC(2), GC(0), -1.0, None, op0=ALU.mult),
                  reads=[GC(0)], writes=[GC(2)], tag=f"{uid}nm1")
            S.add("sc", lambda e: e.activation(out=GS(2), in_=lsb, func=AF.Exp, bias=GC(2)),
                  reads=[lsb, GC(2)], writes=[GS(2)], tag=f"{uid}expP")
            S.add("ve", lambda e: e.tensor_tensor(GC(3), GC(1), GC(0), op=ALU.subtract),
                  reads=[GC(1), GC(0)], writes=[GC(3)], tag=f"{uid}d21")
            S.add("sc", lambda e: e.activation(out=GC(4), in_=GC(3), func=AF.Exp),
                  reads=[GC(3)], writes=[GC(4)])
            S.add("ve", lambda e: e.tensor_scalar(GC(5), GC(4), 1.0, None, op0=ALU.add),
                  reads=[GC(4)], writes=[GC(5)])
            S.add("ve", lambda e: e.reciprocal(GC(6), GC(5)),
                  reads=[GC(5)], writes=[GC(6)])
            S.add("ve", lambda e: e.tensor_tensor(GS(2), GS(2), GS(0), op=ALU.mult),
                  reads=[GS(2), GS(0)], writes=[GS(2)])
            gsl = gate[:, tc * 8:(tc + 1) * 8]
            S.add("ve", lambda e, gsl=gsl: e.tensor_scalar(
                gsl, GS(2), GC(6), 0.5, op0=ALU.mult, op1=ALU.mult),
                reads=[GS(2), GC(6)], writes=[gsl], tag=f"{uid}gate{g}.{tc}")
            S.add("pe", lambda e, gsl=gsl: e.transpose(p_tr, gsl, ceye),
                  reads=[gsl, ceye], writes=[p_tr], tag=f"{uid}gt{g}.{tc}")
            S.add("ve", lambda e, tc=tc: e.tensor_copy(g8[:, tc * 128:(tc + 1) * 128], p_tr),
                  reads=[p_tr], writes=[g8[:, tc * 128:(tc + 1) * 128]],
                  tag=f"{uid}gtc{g}.{tc}")
        for ei in range(8):
            S.add("pe", lambda e, ei=ei: e.matmul(
                p_gb, sel8[:, ei * 128:(ei + 1) * 128], g8, start=True, stop=True),
                reads=[sel8[:, ei * 128:(ei + 1) * 128], g8], writes=[p_gb],
                tag=f"{uid}gb{g}.{ei}")
            S.add("ve", lambda e: e.tensor_copy(gbsb[:, 0:TG], p_gb),
                  reads=[p_gb], writes=[gbsb[:, 0:TG]], tag=f"{uid}gbe{g}.{ei}")
            for o in range(16):
                b = o % 2
                wb = t[f"wgb{b}"].ap()
                S.add("sy", lambda e, wb=wb, ei=ei, o=o: e.dma_start(
                    out=wb, in_=W["wem"][ei, o]),
                    reads=[W["wem"][ei, o]], writes=[wb], dma=f"{uid}we{b}",
                    tag=f"{uid}ldw{g}.{ei}.{o}")
                for c in range(16):
                    xnc = xn[:, c * TG:(c + 1) * TG]
                    ws = wb[:, c * 128:(c + 1) * 128]
                    S.add("pe", lambda e, ws=ws, xnc=xnc, c=c, b=b: e.matmul(
                        p_y[b], ws, xnc, start=(c == 0), stop=(c == 15)),
                        reads=[ws, xnc], writes=[p_y[b]], tag=f"{uid}mm{g}.{ei}.{o}.{c}")
                becol = beC[:, o * 8 + ei: o * 8 + ei + 1]
                t1 = tscr[:, 0:TG]
                S.add("ve", lambda e, b=b, becol=becol, t1=t1: e.tensor_scalar(
                    t1, p_y[b], becol, None, op0=ALU.add),
                    reads=[p_y[b], becol], writes=[t1], tag=f"{uid}bea{g}.{ei}.{o}")
                if ei == 0:
                    S.add("ve", lambda e, o=o, t1=t1: e.tensor_tensor(
                        acc[o], t1, gbsb[:, 0:TG], op=ALU.mult),
                        reads=[t1, gbsb[:, 0:TG]], writes=[acc[o]], tag=f"{uid}a0{g}.{o}")
                else:
                    t2 = tscr[:, TG:2 * TG]
                    S.add("ve", lambda e, t1=t1, t2=t2: e.tensor_tensor(
                        t2, t1, gbsb[:, 0:TG], op=ALU.mult),
                        reads=[t1, gbsb[:, 0:TG]], writes=[t2], tag=f"{uid}gm{g}.{ei}.{o}")
                    if ei < 7:
                        S.add("ve", lambda e, o=o, t2=t2: e.tensor_tensor(
                            acc[o], acc[o], t2, op=ALU.add),
                            reads=[acc[o], t2], writes=[acc[o]], tag=f"{uid}ac{g}.{ei}.{o}")
                    else:
                        oslot = o % 2
                        outr = t[f"outr{oslot}"].ap()[:, 0:TG]
                        S.add("ve", lambda e, o=o, t2=t2, t1=t1: e.tensor_tensor(
                            t1, acc[o], t2, op=ALU.add),
                            reads=[acc[o], t2], writes=[t1], tag=f"{uid}af{g}.{o}")
                        S.add("ve", lambda e, o=o, outr=outr, t1=t1: e.tensor_tensor(
                            outr, t1, xg[:, o * TG:(o + 1) * TG], op=ALU.add),
                            reads=[t1, xg[:, o * TG:(o + 1) * TG]], writes=[outr],
                            tag=f"{uid}res{g}.{o}")
                        S.add("sc", lambda e, o=o, g=g, outr=outr, oslot=oslot: e.dma_start(
                            out=x_out3[:, o, g * TG:(g + 1) * TG], in_=outr),
                            reads=[outr], writes=[x_out3[:, o, g * TG:(g + 1) * TG]],
                            dma=f"{uid}st{oslot}", tag=f"{uid}st{g}.{o}")


def pack_moe1_weights(Wr, We, be):
    wem = np.ascontiguousarray(
        np.asarray(We).reshape(8, 16, 128, 16, 128).transpose(0, 3, 2, 1, 4)
    ).reshape(8, 16, 128, 2048)
    beC = np.ascontiguousarray(np.asarray(be).reshape(8, 16, 128).transpose(2, 1, 0)).reshape(128, 128)
    sel8 = np.kron(np.eye(8, dtype=np.float32), np.ones((1, 128), np.float32))
    return dict(wem=wem, wr=np.ascontiguousarray(np.asarray(Wr)), beC_d=beC,
                sel8_d=np.ascontiguousarray(sel8))


def declare_moe1_weights(nc, prefix=""):
    d = {}
    d["wem"] = nc.dram_tensor(prefix + "wem", [8, 16, 128, 2048], F32, kind="ExternalInput").ap()
    d["wr"] = nc.dram_tensor(prefix + "wr", [2048, 8], F32, kind="ExternalInput").ap()
    d["beC"] = nc.dram_tensor(prefix + "beC_d", [128, 128], F32, kind="ExternalInput").ap()
    d["sel8"] = nc.dram_tensor(prefix + "sel8_d", [8, 1024], F32, kind="ExternalInput").ap()
    return d


def build_moe1_program():
    nc = bass.Bass("TRN2", target_bir_lowering=False)
    Lx = MOE_NG * MOE_TG
    x_in = nc.dram_tensor("x_in", [D, Lx], F32, kind="ExternalInput").ap()
    x_out = nc.dram_tensor("x_out", [D, Lx], F32, kind="ExternalOutput").ap()
    W = declare_moe1_weights(nc)
    cones_d = nc.dram_tensor("cones_d", [128, 128], F32, kind="ExternalInput").ap()
    ceye_d = nc.dram_tensor("ceye_d", [128, 128], F32, kind="ExternalInput").ap()
    ctx = ExitStack()
    with ctx:
        A = Arena(nc, ctx)
        A.sb("cones", [128, 128])
        A.sb("ceye", [128, 128])
        alloc_shared(A)
        alloc_moe1(A)
        P = [A.ps(f"pb{i}", [128, 512]) for i in range(8)]
        S = Sched(nc, psum_names=[f"pb{i}" for i in range(8)])
        S.add("sy", lambda e: e.dma_start(out=A.t["cones"].ap(), in_=cones_d),
              writes=[A.t["cones"].ap()], dma="cones")
        S.add("sy", lambda e: e.dma_start(out=A.t["ceye"].ap(), in_=ceye_d),
              writes=[A.t["ceye"].ap()], dma="ceye")
        emit_consts(S, A)
        emit_moe1(S, A, P, W, x_in, x_out, uid="m")
        n = S.finalize(ctx)
    return nc, n




ATT_TG = 512
ATT_NG = L // ATT_TG
ATT_SCALE = float(128 ** -0.5)

_AS = dict(qh0=0, qh1=2048, vt=4096, ex0=6144, ex1=6656, at0=7168, at1=7680,
           la=8192, nfr=10240, zq=12288, mcol=12800, scol=12816, fsc=12832,
           asc=12848, lb4=12864, bb=13888)


def emit_attn0(S, A, P, W, x_in, x_out, qs, ks, vs, os_, uid):
    t = A.t
    TG, NG = ATT_TG, ATT_NG
    Lx = NG * TG
    NQC = Lx // 128
    NQT = Lx // 512
    xg, xn = t["xg"].ap(), t["xn"].ap()
    slab = t["slab"].ap()
    cones = t["cones"].ap()
    ceye = t["ceye"].ap()

    qh = [slab[:, _AS["qh0"]:_AS["qh0"] + Lx], slab[:, _AS["qh1"]:_AS["qh1"] + Lx]]
    vt = slab[:, _AS["vt"]:_AS["vt"] + Lx]
    ex = [slab[:, _AS["ex0"]:_AS["ex0"] + 512], slab[:, _AS["ex1"]:_AS["ex1"] + 512]]
    at = [slab[:, _AS["at0"]:_AS["at0"] + 512], slab[:, _AS["at1"]:_AS["at1"] + 512]]
    la = slab[0:16, _AS["la"]:_AS["la"] + 2048]
    nfr = slab[0:1, _AS["nfr"]:_AS["nfr"] + Lx]
    zq = slab[0:16, _AS["zq"]:_AS["zq"] + TG]
    mcol = slab[:, _AS["mcol"]:_AS["mcol"] + NQC]
    scol = slab[:, _AS["scol"]:_AS["scol"] + NQC]
    fsc = slab[:, _AS["fsc"]:_AS["fsc"] + 16]
    asc = slab[:, _AS["asc"]:_AS["asc"] + 16]
    lb4 = slab[:, _AS["lb4"]:_AS["lb4"] + 1024]
    bb = slab[:, _AS["bb"]:_AS["bb"] + 64]

    x_in3 = x_in.rearrange("(c p) t -> p c t", p=128)
    x_out3 = x_out.rearrange("(c p) t -> p c t", p=128)
    sp3 = {n: s.rearrange("(c p) t -> p c t", p=128) for n, s in
           (("q", qs), ("k", ks), ("v", vs), ("o", os_))}

    S.add("sy", lambda e: e.dma_start(out=lb4, in_=W["lb4"]), writes=[lb4], dma=f"{uid}p0")
    S.add("sy", lambda e: e.dma_start(out=bb, in_=W["bb"]), writes=[bb], dma=f"{uid}p1")

    p_pr = [P[0].ap()[:, 0:TG], P[1].ap()[:, 0:TG]]
    p_z = P[2].ap()[0:16, 0:TG]
    p_vt = P[2].ap()[:, 0:128]
    p_tr2 = P[2].ap()[0:1, 0:128]

    def proj(g, wi, wm, dst3, src_tile, resid):
        S.add("sy", lambda e: e.dma_start(out=la, in_=W["aqT4"][wi]),
              writes=[la], dma=f"{uid}la", tag=f"{uid}la{g}.{wi}")
        for c in range(16):
            xnc = src_tile[:, c * TG:(c + 1) * TG]
            lbs = lb4[:, wi * 256 + c * 16: wi * 256 + (c + 1) * 16]
            S.add("pe", lambda e, xnc=xnc, lbs=lbs, c=c: e.matmul(
                p_z, lbs, xnc, start=(c == 0), stop=(c == 15)),
                reads=[lbs, xnc], writes=[p_z], tag=f"{uid}z{g}.{wi}.{c}")
        S.add("ve", lambda e: e.tensor_scalar(zq, p_z, 0.5, None, op0=ALU.mult),
              reads=[p_z], writes=[zq], tag=f"{uid}zev{g}.{wi}")
        for o in range(16):
            b = o % 2
            wb = t[f"wgb{b}"].ap()
            S.add("sy", lambda e, wb=wb, o=o, wm=wm: e.dma_start(out=wb, in_=wm[o]),
                  reads=[wm[o]], writes=[wb], dma=f"{uid}w{b}", tag=f"{uid}ldw{g}.{wi}.{o}")
            for c in range(16):
                xnc = src_tile[:, c * TG:(c + 1) * TG]
                ws = wb[:, c * 128:(c + 1) * 128]
                S.add("pe", lambda e, ws=ws, xnc=xnc, c=c, b=b: e.matmul(
                    p_pr[b], ws, xnc, start=(c == 0), stop=False),
                    reads=[ws, xnc], writes=[p_pr[b]], tag=f"{uid}pm{g}.{wi}.{o}.{c}")
            las = la[:, o * 128:(o + 1) * 128]
            S.add("pe", lambda e, las=las, b=b: e.matmul(
                p_pr[b], las, zq, start=False, stop=True),
                reads=[las, zq], writes=[p_pr[b]], tag=f"{uid}aug{g}.{wi}.{o}")
            oslot = o % 2
            outr = t[f"outr{oslot}"].ap()[:, 0:TG]
            bcol = bb[:, wi * 16 + o: wi * 16 + o + 1]
            if resid is None:
                S.add("ve", lambda e, b=b, outr=outr, bcol=bcol: e.tensor_scalar(
                    outr, p_pr[b], bcol, None, op0=ALU.add),
                    reads=[p_pr[b], bcol], writes=[outr], tag=f"{uid}pev{g}.{wi}.{o}")
                S.add("sc", lambda e, o=o, g=g, outr=outr, dst3=dst3, oslot=oslot: e.dma_start(
                    out=dst3[:, o, g * TG:(g + 1) * TG], in_=outr),
                    reads=[outr], writes=[dst3[:, o, g * TG:(g + 1) * TG]],
                    dma=f"{uid}st{oslot}", tag=f"{uid}sts{g}.{wi}.{o}")
            else:
                tsc = t["tscr"].ap()[:, 0:TG]
                S.add("ve", lambda e, b=b, bcol=bcol, tsc=tsc: e.tensor_scalar(
                    tsc, p_pr[b], bcol, None, op0=ALU.add),
                    reads=[p_pr[b], bcol], writes=[tsc], tag=f"{uid}pev{g}.{wi}.{o}")
                S.add("ve", lambda e, o=o, outr=outr, tsc=tsc: e.tensor_tensor(
                    outr, tsc, xg[:, o * TG:(o + 1) * TG], op=ALU.add),
                    reads=[tsc, xg[:, o * TG:(o + 1) * TG]], writes=[outr],
                    tag=f"{uid}res{g}.{o}")
                S.add("sc", lambda e, o=o, g=g, outr=outr, oslot=oslot: e.dma_start(
                    out=x_out3[:, o, g * TG:(g + 1) * TG], in_=outr),
                    reads=[outr], writes=[x_out3[:, o, g * TG:(g + 1) * TG]],
                    dma=f"{uid}st{oslot}", tag=f"{uid}stx{g}.{o}")

    for g in range(NG):
        S.add("sy", lambda e, g=g: e.dma_start(
            out=xg[:, 0:16 * TG].rearrange("p (c t) -> p c t", t=TG),
            in_=x_in3[:, :, g * TG:(g + 1) * TG]),
            reads=[x_in3[:, :, g * TG:(g + 1) * TG]], writes=[xg[:, 0:16 * TG]],
            dma=f"{uid}xg", tag=f"{uid}ldx{g}")
        emit_ln(S, A, P, xg, xn, TG, tag=f"{uid}ln{g}")
        for wi, (wm, dst) in enumerate(((W["wqm"], sp3["q"]), (W["wkm"], sp3["k"]),
                                        (W["wvm"], sp3["v"]))):
            proj(g, wi, wm, dst, xn, None)

    for h in range(16):
        b = h % 2
        khb = t[f"wub{b}"].ap()[:, 0:Lx]
        vhb = t[f"wdb{b}"].ap()[:, 0:Lx]
        qhb = qh[b]
        for (nm, buf, s3) in (("k", khb, sp3["k"]), ("v", vhb, sp3["v"]), ("q", qhb, sp3["q"])):
            S.add("sy", lambda e, buf=buf, s3=s3, h=h: e.dma_start(
                out=buf, in_=s3[:, h, :]),
                reads=[s3[:, h, :]], writes=[buf], dma=f"{uid}h{nm}{b}",
                tag=f"{uid}ld{nm}{h}")
        for kc in range(NQC):
            S.add("pe", lambda e, kc=kc, vhb=vhb: e.transpose(p_vt, vhb[:, kc * 128:(kc + 1) * 128], ceye),
                  reads=[vhb[:, kc * 128:(kc + 1) * 128], ceye], writes=[p_vt],
                  tag=f"{uid}vt{h}.{kc}")
            S.add("ve", lambda e, kc=kc: e.tensor_copy(vt[:, kc * 128:(kc + 1) * 128], p_vt),
                  reads=[p_vt], writes=[vt[:, kc * 128:(kc + 1) * 128]],
                  tag=f"{uid}vtc{h}.{kc}")
        for qc in range(NQC):
            PB = [P[(qc % 2) * 4 + kt] for kt in range(NQT)]
            for kt in range(NQT):
                S.add("pe", lambda e, qc=qc, kt=kt, qhb=qhb, khb=khb, PB=PB: e.matmul(
                    PB[kt].ap(), qhb[:, qc * 128:(qc + 1) * 128],
                    khb[:, kt * 512:(kt + 1) * 512], start=True, stop=True),
                    reads=[qhb[:, qc * 128:(qc + 1) * 128], khb[:, kt * 512:(kt + 1) * 512]],
                    writes=[PB[kt].ap()], tag=f"{uid}sc{h}.{qc}.{kt}")
                S.add("ve", lambda e, qc=qc, kt=kt, PB=PB: e.reduce_max(
                    asc[:, kt:kt + 1], PB[kt].ap(), axis=AX.X),
                    reads=[PB[kt].ap()], writes=[asc[:, kt:kt + 1]],
                    tag=f"{uid}mx{h}.{qc}.{kt}")
            mc = mcol[:, qc:qc + 1]
            if NQT == 4:
                S.add("ve", lambda e: e.tensor_tensor(asc[:, 4:5], asc[:, 0:1], asc[:, 1:2], op=ALU.max),
                      reads=[asc[:, 0:2]], writes=[asc[:, 4:5]])
                S.add("ve", lambda e: e.tensor_tensor(asc[:, 5:6], asc[:, 2:3], asc[:, 3:4], op=ALU.max),
                      reads=[asc[:, 2:4]], writes=[asc[:, 5:6]])
                S.add("ve", lambda e, mc=mc: e.tensor_tensor(mc, asc[:, 4:5], asc[:, 5:6], op=ALU.max),
                      reads=[asc[:, 4:6]], writes=[mc], tag=f"{uid}m{h}.{qc}")
            else:
                S.add("ve", lambda e, mc=mc: e.tensor_copy(mc, asc[:, 0:1]),
                      reads=[asc[:, 0:1]], writes=[mc], tag=f"{uid}m{h}.{qc}")
            S.add("ve", lambda e, mc=mc: e.tensor_scalar(
                asc[:, 6:7], mc, -ATT_SCALE, None, op0=ALU.mult),
                reads=[mc], writes=[asc[:, 6:7]], tag=f"{uid}nm{h}.{qc}")
            for kt in range(NQT):
                S.add("sc", lambda e, kt=kt, PB=PB: e.activation(
                    out=ex[kt % 2], in_=PB[kt].ap(), func=AF.Exp, bias=asc[:, 6:7],
                    scale=ATT_SCALE, accum_out=asc[:, 8 + kt:9 + kt]),
                    reads=[PB[kt].ap(), asc[:, 6:7]], writes=[ex[kt % 2], asc[:, 8 + kt:9 + kt]],
                    tag=f"{uid}ex{h}.{qc}.{kt}")
            sc_ = scol[:, qc:qc + 1]
            if NQT == 4:
                S.add("ve", lambda e: e.tensor_tensor(asc[:, 12:13], asc[:, 8:9], asc[:, 9:10], op=ALU.add),
                      reads=[asc[:, 8:10]], writes=[asc[:, 12:13]])
                S.add("ve", lambda e: e.tensor_tensor(asc[:, 13:14], asc[:, 10:11], asc[:, 11:12], op=ALU.add),
                      reads=[asc[:, 10:12]], writes=[asc[:, 13:14]])
                S.add("ve", lambda e, sc_=sc_: e.tensor_tensor(sc_, asc[:, 12:13], asc[:, 13:14], op=ALU.add),
                      reads=[asc[:, 12:14]], writes=[sc_], tag=f"{uid}s{h}.{qc}")
            else:
                S.add("ve", lambda e, sc_=sc_: e.tensor_copy(sc_, asc[:, 8:9]),
                      reads=[asc[:, 8:9]], writes=[sc_], tag=f"{uid}s{h}.{qc}")
        S.add("sc", lambda e: e.activation(out=fsc[:, 0:NQC], in_=scol[:, 0:NQC], func=AF.Ln),
              reads=[scol[:, 0:NQC]], writes=[fsc[:, 0:NQC]], tag=f"{uid}lns{h}")
        S.add("ve", lambda e: e.tensor_scalar(asc[:, 0:NQC], mcol[:, 0:NQC], ATT_SCALE, None, op0=ALU.mult),
              reads=[mcol[:, 0:NQC]], writes=[asc[:, 0:NQC]], tag=f"{uid}msc{h}")
        S.add("ve", lambda e: e.tensor_tensor(fsc[:, 0:NQC], asc[:, 0:NQC], fsc[:, 0:NQC], op=ALU.add),
              reads=[asc[:, 0:NQC], fsc[:, 0:NQC]], writes=[fsc[:, 0:NQC]], tag=f"{uid}f{h}")
        S.add("ve", lambda e: e.tensor_scalar(fsc[:, 0:NQC], fsc[:, 0:NQC], -1.0 / ATT_SCALE, None, op0=ALU.mult),
              reads=[fsc[:, 0:NQC]], writes=[fsc[:, 0:NQC]], tag=f"{uid}nf{h}")
        for qc in range(NQC):
            S.add("pe", lambda e, qc=qc: e.transpose(p_tr2, fsc[:, qc:qc + 1], ceye),
                  reads=[fsc[:, qc:qc + 1], ceye], writes=[p_tr2], tag=f"{uid}ft{h}.{qc}")
            S.add("ve", lambda e, qc=qc: e.tensor_copy(nfr[0:1, qc * 128:(qc + 1) * 128], p_tr2),
                  reads=[p_tr2], writes=[nfr[0:1, qc * 128:(qc + 1) * 128]],
                  tag=f"{uid}ftc{h}.{qc}")
        steps = [(kc, qt) for kc in range(NQC) for qt in range(NQT)]

        def emit_sT(i):
            kc, qt = steps[i]
            pb = P[4 + i % 2].ap()
            S.add("pe", lambda e, kc=kc, qt=qt, pb=pb, khb=khb, qhb=qhb: e.matmul(
                pb, khb[:, kc * 128:(kc + 1) * 128], qhb[:, qt * 512:(qt + 1) * 512],
                start=True, stop=False),
                reads=[khb[:, kc * 128:(kc + 1) * 128], qhb[:, qt * 512:(qt + 1) * 512]],
                writes=[pb], tag=f"{uid}sT{h}.{kc}.{qt}")
            S.add("pe", lambda e, qt=qt, pb=pb: e.matmul(
                pb, cones[0:1, :], nfr[0:1, qt * 512:(qt + 1) * 512],
                start=False, stop=True),
                reads=[cones[0:1, :], nfr[0:1, qt * 512:(qt + 1) * 512]],
                writes=[pb], tag=f"{uid}rk{h}.{kc}.{qt}")
            ats = at[i % 2]
            S.add("sc", lambda e, pb=pb, ats=ats: e.activation(
                out=ats, in_=pb, func=AF.Exp, scale=ATT_SCALE),
                reads=[pb], writes=[ats], tag=f"{uid}ea{h}.{kc}.{qt}")
            return ats

        def emit_av(i, ats):
            kc, qt = steps[i]
            S.add("pe", lambda e, kc=kc, qt=qt, ats=ats: e.matmul(
                P[qt].ap(), vt[:, kc * 128:(kc + 1) * 128], ats,
                start=(kc == 0), stop=(kc == NQC - 1)),
                reads=[vt[:, kc * 128:(kc + 1) * 128], ats], writes=[P[qt].ap()],
                tag=f"{uid}av{h}.{kc}.{qt}")

        prev = emit_sT(0)
        for i in range(1, len(steps)):
            cur = emit_sT(i)
            emit_av(i - 1, prev)
            prev = cur
        emit_av(len(steps) - 1, prev)
        for qt in range(NQT):
            oslot = qt % 2
            outr = t[f"outr{oslot}"].ap()
            S.add("ve", lambda e, qt=qt, outr=outr: e.tensor_copy(outr, P[qt].ap()),
                  reads=[P[qt].ap()], writes=[outr], tag=f"{uid}oev{h}.{qt}")
            S.add("sc", lambda e, h=h, qt=qt, outr=outr, oslot=oslot: e.dma_start(
                out=sp3["o"][:, h, qt * 512:(qt + 1) * 512], in_=outr),
                reads=[outr], writes=[sp3["o"][:, h, qt * 512:(qt + 1) * 512]],
                dma=f"{uid}so{oslot}", tag=f"{uid}sto{h}.{qt}")

    for g in range(NG):
        S.add("sy", lambda e, g=g: e.dma_start(
            out=xg[:, 0:16 * TG].rearrange("p (c t) -> p c t", t=TG),
            in_=x_in3[:, :, g * TG:(g + 1) * TG]),
            reads=[x_in3[:, :, g * TG:(g + 1) * TG]], writes=[xg[:, 0:16 * TG]],
            dma=f"{uid}xg", tag=f"{uid}ldx2{g}")
        S.add("sy", lambda e, g=g: e.dma_start(
            out=xn[:, 0:16 * TG].rearrange("p (c t) -> p c t", t=TG),
            in_=sp3["o"][:, :, g * TG:(g + 1) * TG]),
            reads=[sp3["o"][:, :, g * TG:(g + 1) * TG]], writes=[xn[:, 0:16 * TG]],
            dma=f"{uid}xo", tag=f"{uid}ldo{g}")
        proj(g, 3, W["wom"], None, xn, resid=True)


def pack_proj(Wn):
    return np.ascontiguousarray(
        np.asarray(Wn).reshape(16, 128, 16, 128).transpose(0, 3, 2, 1)).reshape(16, 128, 2048)


def pack_attn0_weights(Wq, bq, Bq, Aq, Wk, bk, Bk, Ak, Wv, bv, Bv, Av, Wo, bo, Bo, Ao):
    d = {}
    d["wqm"], d["wkm"], d["wvm"], d["wom"] = (pack_proj(w) for w in (Wq, Wk, Wv, Wo))
    lb = []
    for B in (Bq, Bk, Bv, Bo):
        lb.append(np.ascontiguousarray(np.asarray(B).T).reshape(16, 128, 16).transpose(1, 0, 2).reshape(128, 256))
    d["lb4"] = np.ascontiguousarray(np.concatenate(lb, axis=1))
    d["aqT4"] = np.ascontiguousarray(np.stack([np.asarray(A_).T for A_ in (Aq, Ak, Av, Ao)]))
    bbs = [np.asarray(b).reshape(16, 128).T for b in (bq, bk, bv, bo)]
    d["bb"] = np.ascontiguousarray(np.concatenate(bbs, axis=1))
    return d


def declare_attn0_weights(nc, prefix=""):
    d = {}
    for n in ("wqm", "wkm", "wvm", "wom"):
        d[n] = nc.dram_tensor(prefix + n, [16, 128, 2048], F32, kind="ExternalInput").ap()
    d["lb4"] = nc.dram_tensor(prefix + "lb4_d", [128, 1024], F32, kind="ExternalInput").ap()
    d["aqT4"] = nc.dram_tensor(prefix + "aqT4_d", [4, 16, 2048], F32, kind="ExternalInput").ap()
    d["bb"] = nc.dram_tensor(prefix + "bb_d", [128, 64], F32, kind="ExternalInput").ap()
    return d


def build_attn0_program():
    nc = bass.Bass("TRN2", target_bir_lowering=False)
    Lx = ATT_NG * ATT_TG
    x_in = nc.dram_tensor("x_in", [D, Lx], F32, kind="ExternalInput").ap()
    x_out = nc.dram_tensor("x_out", [D, Lx], F32, kind="ExternalOutput").ap()
    qs = nc.dram_tensor("q_s", [D, Lx], F32, kind="Internal").ap()
    ks = nc.dram_tensor("k_s", [D, Lx], F32, kind="Internal").ap()
    vs = nc.dram_tensor("v_s", [D, Lx], F32, kind="Internal").ap()
    os_ = nc.dram_tensor("o_s", [D, Lx], F32, kind="Internal").ap()
    W = declare_attn0_weights(nc)
    cones_d = nc.dram_tensor("cones_d", [128, 128], F32, kind="ExternalInput").ap()
    ceye_d = nc.dram_tensor("ceye_d", [128, 128], F32, kind="ExternalInput").ap()
    ctx = ExitStack()
    with ctx:
        A = Arena(nc, ctx)
        A.sb("cones", [128, 128])
        A.sb("ceye", [128, 128])
        alloc_shared(A)
        P = [A.ps(f"pb{i}", [128, 512]) for i in range(8)]
        S = Sched(nc, psum_names=[f"pb{i}" for i in range(8)])
        S.add("sy", lambda e: e.dma_start(out=A.t["cones"].ap(), in_=cones_d),
              writes=[A.t["cones"].ap()], dma="cones")
        S.add("sy", lambda e: e.dma_start(out=A.t["ceye"].ap(), in_=ceye_d),
              writes=[A.t["ceye"].ap()], dma="ceye")
        emit_consts(S, A)
        emit_attn0(S, A, P, W, x_in, x_out, qs, ks, vs, os_, uid="a")
        n = S.finalize(ctx)
    return nc, n




AT1_TG = 512
AT1_NG = L // AT1_TG
AT1_EPS = 1e-6
AT1_DBG = False
AT1_PF = True
AT1_LEVEL = 0
AT1_SECTS = 'zlcaetsbfw'

_A1 = dict(qb0=0, qb1=2048, xcwb=4096, dfm=6144, w1s=8192, TMg=10240,
           cxb=12288, ATsb=14336, Etsb=14848, AEtsb=15360, sTsb=15872,
           sTTsb=15888, s2sb=15920, coTM=16176, sfm=15872)
_X1 = dict(z1=0, z1e=256, sqs=512, xh=768, g0=1024, grad=1280, cog=1536,
           zsb=1792, zbe=2048, t1=2304, t2=2560, b1s=2816, yo=3072, outf=3328,
           gx=3584, stat=3840, stat2=4352, red=4864)


def alloc_attn1(A):
    A.sb("lwt", [128, 512])
    A.sb("hind", [2, 128])
    A.sb("o2bd", [128, 2])
    A.sb("a2c", [2, 256])
    A.sb("aTMc", [16, 1])
    A.sb("lbT", [1, 32])
    A.sb("eps2", [2, 1])


def emit_attn1(S, A, P, W, x_in, x_out, qs, ks, vs, xcws, sd, sdT, uid):
    t = A.t
    TG, NG = AT1_TG, AT1_NG
    Lx = NG * TG
    NCH = Lx // 16
    NMG = NCH // 8
    xg, xn = t["xg"].ap(), t["xn"].ap()
    slab = t["slab"].ap()
    cones = t["cones"].ap()
    ceye = t["ceye"].ap()

    qb = [slab[0:128, _A1["qb0"]:_A1["qb0"] + 2048],
          slab[0:128, _A1["qb1"]:_A1["qb1"] + 2048]]
    xcwb = slab[0:128, _A1["xcwb"]:_A1["xcwb"] + 2048]
    dfm = slab[0:128, _A1["dfm"]:_A1["dfm"] + 2048]
    w1s = slab[0:128, _A1["w1s"]:_A1["w1s"] + 2048]
    TMg = slab[0:16, _A1["TMg"]:_A1["TMg"] + 2048]
    cxb = slab[0:16, _A1["cxb"]:_A1["cxb"] + 2048]
    ATsb = slab[0:16, _A1["ATsb"]:_A1["ATsb"] + 512]
    Etsb = slab[0:16, _A1["Etsb"]:_A1["Etsb"] + 512]
    AEtsb = slab[0:16, _A1["AEtsb"]:_A1["AEtsb"] + 512]
    sTsb = slab[0:32, _A1["sTsb"]:_A1["sTsb"] + 16]
    sTTsb = slab[0:16, _A1["sTTsb"]:_A1["sTTsb"] + 32]
    s2sb = slab[0:2, _A1["s2sb"]:_A1["s2sb"] + 256]
    coTM = slab[0:16, _A1["coTM"]:_A1["coTM"] + 32]
    lwt = t["lwt"].ap()
    mkTnA = xn[0:16, 5392:5904]
    csel = xn[0:32, 5904:6416]
    gfm = xn[0:128, 4880:5136]
    bfm = xn[0:128, 5136:5392]
    hind = t["hind"].ap()
    o2bd = t["o2bd"].ap()
    a2c = t["a2c"].ap()
    aTMc = t["aTMc"].ap()
    lbT = t["lbT"].ap()
    eps2 = t["eps2"].ap()

    def XV(nm, cols):
        return xn[0:128, _X1[nm]:_X1[nm] + cols]

    x_in3 = x_in.rearrange("(c p) t -> p c t", p=128)
    x_out3 = x_out.rearrange("(c p) t -> p c t", p=128)
    sp3 = {n: s.rearrange("(c p) t -> p c t", p=128) for n, s in
           (("q", qs), ("k", ks), ("v", vs), ("x", xcws))}

    for nm in ("lwt", "hind", "o2bd", "a2c", "aTMc", "lbT"):
        ap = t[nm].ap()
        S.add("sy", lambda e, ap=ap, nm=nm: e.dma_start(out=ap, in_=W[nm]),
              writes=[ap], dma=f"{uid}c{nm}")
    S.add("sy", lambda e: e.dma_start(
        out=w1s.rearrange("p (pr o) -> p pr o", o=128),
        in_=W["w1bd"].rearrange("pr p o -> p pr o")),
        reads=[W["w1bd"]], writes=[w1s], dma=f"{uid}w1i")
    b1s = XV("b1s", 256)
    S.add("ve", lambda e: e.memset(eps2, AT1_EPS), writes=[eps2], tag=f"{uid}eps2")

    p_pr = [P[0].ap()[:, 0:TG], P[1].ap()[:, 0:TG]]

    for g in range(NG):
        S.add("sy", lambda e, g=g: e.dma_start(
            out=xg[:, 0:16 * TG].rearrange("p (c t) -> p c t", t=TG),
            in_=x_in3[:, :, g * TG:(g + 1) * TG]),
            reads=[x_in3[:, :, g * TG:(g + 1) * TG]], writes=[xg[:, 0:16 * TG]],
            dma=f"{uid}xg", tag=f"{uid}ldx{g}")
        emit_ln(S, A, P, xg, xn, TG, tag=f"{uid}ln{g}")
        p_il = P[2].ap()[:, 0:32]
        p_tr0 = P[3].ap()[0:32, 0:128]
        sfm = slab[0:32, _A1["sfm"]:_A1["sfm"] + 512]
        for tc in range(4):
            for c in range(16):
                xsl = xn[:, c * TG + tc * 128: c * TG + tc * 128 + 128]
                S.add("pe", lambda e, xsl=xsl, c=c: e.matmul(
                    p_il, xsl, lwt[:, c * 32:(c + 1) * 32], start=(c == 0), stop=False),
                    reads=[xsl, lwt[:, c * 32:(c + 1) * 32]], writes=[p_il],
                    tag=f"{uid}il{g}.{tc}.{c}")
            S.add("pe", lambda e: e.matmul(p_il, cones[0:1, :], lbT, start=False, stop=True),
                  reads=[cones[0:1, :], lbT], writes=[p_il], tag=f"{uid}ilb{g}.{tc}")
            stm = t["tscr"].ap()[:, 0:32]
            S.add("sc", lambda e, stm=stm: e.activation(out=stm, in_=p_il, func=AF.Sigmoid),
                  reads=[p_il], writes=[stm], tag=f"{uid}sig{g}.{tc}")
            row0 = g * 512 + tc * 128
            S.add("sc", lambda e, row0=row0, stm=stm: e.dma_start(
                out=sdT[row0:row0 + 128, :], in_=stm),
                reads=[stm], writes=[sdT[row0:row0 + 128, :]],
                dma=f"{uid}sdt", tag=f"{uid}ssdT{g}.{tc}")
            S.add("pe", lambda e, stm=stm, p_tr0=p_tr0: e.transpose(p_tr0, stm, ceye),
                  reads=[stm, ceye], writes=[p_tr0], tag=f"{uid}strp{g}.{tc}")
            S.add("ve", lambda e, tc=tc, sfm=sfm, p_tr0=p_tr0: e.tensor_copy(
                sfm[:, tc * 128:(tc + 1) * 128], p_tr0),
                reads=[p_tr0], writes=[sfm[:, tc * 128:(tc + 1) * 128]],
                tag=f"{uid}sTc{g}.{tc}")
        S.add("sc", lambda e, g=g, sfm=sfm: e.dma_start(
            out=sd[:, g * 512:(g + 1) * 512], in_=sfm),
            reads=[sfm], writes=[sd[:, g * 512:(g + 1) * 512]],
            dma=f"{uid}sds", tag=f"{uid}ssd{g}")
        for wi, (wmk, dst) in enumerate((("wqm1", "q"), ("wkm1", "k"), ("wvm1", "v"))):
            wm = W[wmk]
            for o in range(16):
                b = o % 2
                wb = t[f"wgb{b}"].ap()
                S.add("sy", lambda e, wb=wb, o=o, wm=wm: e.dma_start(out=wb, in_=wm[o]),
                      reads=[wm[o]], writes=[wb], dma=f"{uid}w{b}",
                      tag=f"{uid}ldw{g}.{wi}.{o}")
                for c in range(16):
                    xnc = xn[:, c * TG:(c + 1) * TG]
                    ws = wb[:, c * 128:(c + 1) * 128]
                    S.add("pe", lambda e, ws=ws, xnc=xnc, c=c, b=b: e.matmul(
                        p_pr[b], ws, xnc, start=(c == 0), stop=(c == 15)),
                        reads=[ws, xnc], writes=[p_pr[b]], tag=f"{uid}pm{g}.{wi}.{o}.{c}")
                oslot = o % 2
                outr = t[f"outr{oslot}"].ap()[:, 0:TG]
                S.add("ve", lambda e, b=b, outr=outr: e.tensor_copy(outr, p_pr[b]),
                      reads=[p_pr[b]], writes=[outr], tag=f"{uid}pev{g}.{wi}.{o}")
                d3 = sp3[dst]
                S.add("sc", lambda e, o=o, g=g, outr=outr, d3=d3, oslot=oslot: e.dma_start(
                    out=d3[:, o, g * TG:(g + 1) * TG], in_=outr),
                    reads=[outr], writes=[d3[:, o, g * TG:(g + 1) * TG]],
                    dma=f"{uid}st{oslot}", tag=f"{uid}sts{g}.{wi}.{o}")

    S.add("sy", lambda e: e.dma_start(out=b1s, in_=W["b1fm"]),
          reads=[W["b1fm"]], writes=[b1s], dma=f"{uid}b1i")
    for nm, ap in (("mkTnA", mkTnA), ("csel", csel), ("gfm", gfm), ("bfm", bfm)):
        S.add("sy", lambda e, ap=ap, nm=nm: e.dma_start(out=ap, in_=W[nm]),
              reads=[W[nm]], writes=[ap], dma=f"{uid}c{nm}")
    z1 = XV("z1", 256)
    z1e = XV("z1e", 256)
    sqs = XV("sqs", 256)
    xh = XV("xh", 256)
    g0 = XV("g0", 256)
    grad = XV("grad", 256)
    cog = XV("cog", 256)
    zsb = XV("zsb", 256)
    zbe = XV("zbe", 256)
    t1s = XV("t1", 256)
    t2s = XV("t2", 256)
    outf = XV("outf", 256)
    gxs = XV("gx", 256)
    red3 = XV("red", 16)
    stat = xn[0:2, _X1["stat"]:_X1["stat"] + 512]
    stat2 = xn[0:2, _X1["stat2"]:_X1["stat2"] + 512]

    ATp = [ATsb, xn[0:16, 6416:6928]]
    Etp = [Etsb, xn[0:16, 6928:7440]]
    AEp = [AEtsb, xn[0:16, 7440:7952]]

    pZ = P[0].ap()[:, 0:256]
    pR = P[1].ap()[0:2, 0:512]
    pB = [P[2].ap()[:, 0:256], P[3].ap()[:, 0:256]]
    pCO = P[4].ap()[:, 0:256]
    pCT = P[1].ap()[0:16, 0:512]
    pAT = P[5].ap()[0:16, 0:256]
    pATb = P[2].ap()[0:16, 256:512]
    pTR = [P[6].ap()[0:16, 0:512], P[7].ap()[0:16, 0:512]]
    pWF = [P[0].ap(), P[4].ap()]
    pend = [None, None]
    bci = [0]

    def bcast(vals_ap, tag):
        b = bci[0] % 2
        bci[0] += 1
        pb = pB[b]
        S.add("pe", lambda e, vals_ap=vals_ap, pb=pb: e.matmul(
            pb, hind, vals_ap, start=True, stop=True),
            reads=[hind, vals_ap], writes=[pb], tag=tag)
        return pb

    def bn_stats(src, tag):
        S.add("sc", lambda e, src=src: e.activation(out=sqs, in_=src, func=AF.Square),
              reads=[src], writes=[sqs], tag=f"{tag}sq")
        S.add("pe", lambda e, src=src: e.matmul(pR[:, 0:256], o2bd, src, start=True, stop=True),
              reads=[o2bd, src], writes=[pR], tag=f"{tag}rs")
        S.add("pe", lambda e: e.matmul(pR[:, 256:512], o2bd, sqs, start=True, stop=True),
              reads=[o2bd, sqs], writes=[pR], tag=f"{tag}rq")
        S.add("ve", lambda e: e.tensor_copy(stat, pR), reads=[pR], writes=[stat],
              tag=f"{tag}ev")
        S.add("ve", lambda e: e.tensor_scalar(stat2, stat, 1.0 / 64, None, op0=ALU.mult),
              reads=[stat], writes=[stat2], tag=f"{tag}mu")
        S.add("ve", lambda e: e.tensor_tensor(stat[:, 0:256], stat2[:, 0:256], stat2[:, 0:256], op=ALU.mult),
              reads=[stat2[:, 0:256]], writes=[stat[:, 0:256]], tag=f"{tag}m2")
        S.add("ve", lambda e: e.tensor_tensor(stat[:, 256:512], stat2[:, 256:512], stat[:, 0:256], op=ALU.subtract),
              reads=[stat2[:, 256:512], stat[:, 0:256]], writes=[stat[:, 256:512]], tag=f"{tag}var")
        S.add("sc", lambda e: e.activation(out=stat[:, 0:256], in_=stat[:, 256:512],
                                           func=AF.Sqrt, bias=eps2),
              reads=[stat[:, 256:512], eps2], writes=[stat[:, 0:256]], tag=f"{tag}std")
        S.add("ve", lambda e: e.reciprocal(stat[:, 256:512], stat[:, 0:256]),
              reads=[stat[:, 0:256]], writes=[stat[:, 256:512]], tag=f"{tag}rst")

    def emit_ce(ch):
        p = ch % 2
        Ett = Etp[p]
        S.add("sy", lambda e, ch=ch: e.dma_start(out=sTsb, in_=W["sd_ap"][:, ch * 16:(ch + 1) * 16]),
              reads=[W["sd_ap"][:, ch * 16:(ch + 1) * 16]], writes=[sTsb],
              dma=f"{uid}lsT", tag=f"{uid}lsT{ch}")
        S.add("pe", lambda e: e.matmul(pCT, sTsb, csel, start=True, stop=True),
              reads=[sTsb, csel], writes=[pCT], tag=f"{uid}ct{ch}")
        S.add("ve", lambda e, Ett=Ett: e.tensor_tensor(Ett, mkTnA, pCT, op=ALU.mult),
              reads=[mkTnA, pCT], writes=[Ett], tag=f"{uid}et{ch}")

    def emit_aet(ch):
        p = ch % 2
        ATt, Ett, AEt = ATp[p], Etp[p], AEp[p]
        S.add("ve", lambda e, AEt=AEt, ATt=ATt, Ett=Ett: e.tensor_tensor(
            AEt, ATt, Ett, op=ALU.mult),
            reads=[ATt, Ett], writes=[AEt], tag=f"{uid}aet{ch}")

    def emit_atb(ch, XB, XC):
        p = ch % 2
        ATt = ATp[p]
        for h in range(32):
            pr, par = h // 2, h % 2
            xbh = XB[par * 64:(par + 1) * 64, pr, :]
            xch = XC[par * 64:(par + 1) * 64, pr, :]
            pdst = pAT if par == 0 else pATb
            S.add("pe", lambda e, h=h, xbh=xbh, xch=xch, pdst=pdst: e.matmul(
                pdst[:, (h // 2) * 16:(h // 2 + 1) * 16], xbh, xch, start=True, stop=True),
                reads=[xbh, xch], writes=[pdst], tag=f"{uid}at{ch}.{h}")
        ATv = ATt.rearrange("j (hp par k) -> j hp par k", par=2, k=16)
        S.add("ve", lambda e, ATv=ATv, ATt=ATt: e.tensor_copy(
            ATv[:, :, 0, :], pAT.rearrange("j (hp k) -> j hp k", k=16)),
            reads=[pAT], writes=[ATt], tag=f"{uid}atv{ch}")
        S.add("ve", lambda e, ATv=ATv, ATt=ATt: e.tensor_copy(
            ATv[:, :, 1, :], pATb.rearrange("j (hp k) -> j hp k", k=16)),
            reads=[pATb], writes=[ATt], tag=f"{uid}atv2{ch}")

    for mg in range(NMG):
        mb = mg % 2
        kb = t[f"wub{mb}"].ap()[:, 0:2048]
        vb = t[f"wdb{mb}"].ap()[:, 0:2048]
        qbb = qb[mb]
        for (nm, buf, s3) in (("k", kb, sp3["k"]), ("v", vb, sp3["v"]), ("q", qbb, sp3["q"])):
            S.add("sy", lambda e, buf=buf, s3=s3, mg=mg: e.dma_start(
                out=buf.rearrange("p (pr t) -> p pr t", t=128),
                in_=s3[:, :, mg * 128:(mg + 1) * 128]),
                reads=[s3[:, :, mg * 128:(mg + 1) * 128]], writes=[buf],
                dma=f"{uid}b{nm}{mb}", tag=f"{uid}mg{nm}{mg}")
        S.add("ve", lambda e, kb=kb, vb=vb: e.tensor_tensor(dfm, vb, kb, op=ALU.subtract),
              reads=[vb, kb], writes=[dfm], tag=f"{uid}dfm{mg}")
        for cc in range(8):
            ch = mg * 8 + cc
            kb3 = kb.rearrange("p (pr t) -> p pr t", t=128)
            qb3 = qbb.rearrange("p (pr t) -> p pr t", t=128)
            db3 = dfm.rearrange("p (pr t) -> p pr t", t=128)
            XB = kb3[:, :, cc * 16:(cc + 1) * 16]
            XC = qb3[:, :, cc * 16:(cc + 1) * 16]
            DD = db3[:, :, cc * 16:(cc + 1) * 16]
            if cc == 0 or not AT1_PF:
                emit_atb(ch, XB, XC)
            S.add("sy", lambda e, ch=ch: e.dma_start(out=sTTsb, in_=W["sdT_ap"][ch * 16:(ch + 1) * 16, :]),
                  reads=[W["sdT_ap"][ch * 16:(ch + 1) * 16, :]], writes=[sTTsb],
                  dma=f"{uid}lsTT", tag=f"{uid}lsTT{ch}")
            S.add("sy", lambda e, ch=ch: e.dma_start(
                out=s2sb.rearrange("r (pr k) -> r pr k", k=16),
                in_=W["sd_ap"].rearrange("(pr r) (cg k) -> cg r pr k", r=2, k=16)[ch]),
                reads=[W["sd_ap"]], writes=[s2sb], dma=f"{uid}ls2", tag=f"{uid}ls2{ch}")
            if AT1_LEVEL >= 2:
                continue
            for pr in range(16 if "z" in AT1_SECTS else 0):
                S.add("pe", lambda e, pr=pr, XB=XB: e.matmul(
                    pZ[:, pr * 16:(pr + 1) * 16], w1s[:, pr * 128:(pr + 1) * 128],
                    XB[:, pr, :], start=True, stop=True),
                    reads=[w1s[:, pr * 128:(pr + 1) * 128], XB[:, pr, :]],
                    writes=[pZ], tag=f"{uid}z1.{ch}.{pr}")
            if pend[0] is not None:
                pend[0]()
                pend[0] = None
            S.add("ve", lambda e: e.tensor_tensor(z1e, pZ, b1s, op=ALU.add),
                  reads=[pZ, b1s], writes=[z1e], tag=f"{uid}z1b{ch}")
            S.add("ve", lambda e: e.tensor_tensor(s2sb, s2sb, a2c, op=ALU.mult),
                  reads=[s2sb, a2c], writes=[s2sb], tag=f"{uid}s2a{ch}")
            S.add("pe", lambda e: e.matmul(pCO, hind, s2sb, start=True, stop=True),
                  reads=[hind, s2sb], writes=[pCO], tag=f"{uid}cof{ch}")
            emit_ce(ch)
            emit_aet(ch)
            if cc < 7 and AT1_PF:
                XBn = kb3[:, :, (cc + 1) * 16:(cc + 2) * 16]
                XCn = qb3[:, :, (cc + 1) * 16:(cc + 2) * 16]
                emit_atb(ch + 1, XBn, XCn)
            if "s" in AT1_SECTS:
                for pr in range(16):
                    S.add("pe", lambda e, pr=pr, XC=XC, stp=(AT1_LEVEL >= 1 and pr == 15): e.matmul(
                        pZ[:, pr * 16:(pr + 1) * 16], w1s[:, pr * 128:(pr + 1) * 128],
                        XC[:, pr, :], start=(pr == 0), stop=stp, skip_group_check=True),
                        reads=[w1s[:, pr * 128:(pr + 1) * 128], XC[:, pr, :]],
                        writes=[pZ], tag=f"{uid}s1.{ch}.{pr}")
            S.add("ve", lambda e: e.tensor_scalar(coTM, sTTsb, aTMc, None, op0=ALU.mult),
                  reads=[sTTsb, aTMc], writes=[coTM], tag=f"{uid}coT{ch}")
            for quad in range(4):
                ptr = pTR[quad % 2]
                for q4 in range(4):
                    pr = quad * 4 + q4
                    S.add("pe", lambda e, pr=pr, q4=q4, ptr=ptr, XB=XB: e.transpose(
                        ptr[:, q4 * 128:(q4 + 1) * 128], XB[:, pr, :], ceye),
                        reads=[XB[:, pr, :], ceye], writes=[ptr],
                        tag=f"{uid}tx{ch}.{pr}")
                for q4 in range(4):
                    pr = quad * 4 + q4
                    for par in range(2):
                        h = pr * 2 + par
                        dst = cxb[:, pr * 128 + par * 64: pr * 128 + (par + 1) * 64]
                        srcp = ptr[:, q4 * 128 + par * 64: q4 * 128 + (par + 1) * 64]
                        S.add("ve", lambda e, dst=dst, srcp=srcp, h=h: e.tensor_scalar(
                            dst, srcp, coTM[:, h:h + 1], None, op0=ALU.mult),
                            reads=[srcp, coTM[:, h:h + 1]], writes=[dst],
                            tag=f"{uid}cxb{ch}.{h}")
            if pend[1] is not None:
                pend[1]()
                pend[1] = None
            bn_stats(z1e, f"{uid}bn{ch}")
            pmu = bcast(stat2[:, 0:256], f"{uid}bmu{ch}")
            prs = bcast(stat[:, 256:512], f"{uid}brs{ch}")
            S.add("ve", lambda e, pmu=pmu: e.tensor_tensor(t1s, z1e, pmu, op=ALU.subtract),
                  reads=[z1e, pmu], writes=[t1s], tag=f"{uid}xh1{ch}")
            S.add("ve", lambda e, prs=prs: e.tensor_tensor(xh, t1s, prs, op=ALU.mult),
                  reads=[t1s, prs], writes=[xh], tag=f"{uid}xh2{ch}")
            S.add("ve", lambda e: e.tensor_scalar(stat[:, 0:256], stat[:, 256:512], 1.0 / 64, None, op0=ALU.mult),
                  reads=[stat[:, 256:512]], writes=[stat[:, 0:256]], tag=f"{uid}r64{ch}")
            S.add("ve", lambda e: e.tensor_tensor(t1s, xh, gfm, op=ALU.mult),
                  reads=[xh, gfm], writes=[t1s], tag=f"{uid}y1{ch}")
            S.add("ve", lambda e: e.tensor_tensor(t2s, t1s, bfm, op=ALU.add),
                  reads=[t1s, bfm], writes=[t2s], tag=f"{uid}y2{ch}")
            S.add("ve", lambda e, DD=DD: e.tensor_tensor(
                t1s.rearrange("p (pr k) -> p pr k", k=16), t2s.rearrange("p (pr k) -> p pr k", k=16),
                DD, op=ALU.subtract),
                reads=[t2s, DD], writes=[t1s], tag=f"{uid}ymt{ch}")
            S.add("ve", lambda e: e.tensor_tensor(g0, t1s, gfm, op=ALU.mult),
                  reads=[t1s, gfm], writes=[g0], tag=f"{uid}g0{ch}")
            S.add("pe", lambda e: e.matmul(pR[:, 0:256], o2bd, g0, start=True, stop=True),
                  reads=[o2bd, g0], writes=[pR], tag=f"{uid}rsg{ch}")
            S.add("ve", lambda e: e.tensor_tensor(gxs, g0, xh, op=ALU.mult),
                  reads=[g0, xh], writes=[gxs], tag=f"{uid}gx{ch}")
            S.add("pe", lambda e: e.matmul(pR[:, 256:512], o2bd, gxs, start=True, stop=True),
                  reads=[o2bd, gxs], writes=[pR], tag=f"{uid}rgx{ch}")
            S.add("ve", lambda e: e.tensor_copy(stat2, pR), reads=[pR], writes=[stat2],
                  tag=f"{uid}sev2{ch}")
            psg = bcast(stat2[:, 0:256], f"{uid}bsg{ch}")
            psgx = bcast(stat2[:, 256:512], f"{uid}bsgx{ch}")
            S.add("ve", lambda e: e.tensor_scalar(t1s, g0, 64.0, None, op0=ALU.mult),
                  reads=[g0], writes=[t1s], tag=f"{uid}g64{ch}")
            S.add("ve", lambda e, psg=psg: e.tensor_tensor(t2s, t1s, psg, op=ALU.subtract),
                  reads=[t1s, psg], writes=[t2s], tag=f"{uid}gm1{ch}")
            S.add("ve", lambda e, psgx=psgx: e.tensor_tensor(t1s, xh, psgx, op=ALU.mult),
                  reads=[xh, psgx], writes=[t1s], tag=f"{uid}gm2{ch}")
            S.add("ve", lambda e: e.tensor_tensor(t2s, t2s, t1s, op=ALU.subtract),
                  reads=[t2s, t1s], writes=[t2s], tag=f"{uid}gm3{ch}")
            prs64 = bcast(stat[:, 0:256], f"{uid}brs64{ch}")
            S.add("ve", lambda e, prs64=prs64: e.tensor_tensor(grad, t2s, prs64, op=ALU.mult),
                  reads=[t2s, prs64], writes=[grad], tag=f"{uid}grad{ch}")
            S.add("ve", lambda e: e.tensor_tensor(cog, grad, pCO, op=ALU.mult),
                  reads=[grad, pCO], writes=[cog], tag=f"{uid}cgr{ch}")
            for half in range(2):
                ptr = pTR[half]
                for q4 in range(4):
                    pr = half * 4 + q4
                    S.add("pe", lambda e, pr=pr, q4=q4, ptr=ptr: e.transpose(
                        ptr[:, q4 * 128:(q4 + 1) * 128], grad[:, pr * 16:(pr + 1) * 16], ceye),
                        reads=[grad[:, pr * 16:(pr + 1) * 16], ceye], writes=[ptr],
                        tag=f"{uid}tg{ch}.{pr}")
                S.add("ve", lambda e, half=half, ptr=ptr: e.tensor_copy(
                    TMg[:, half * 512:(half + 1) * 512], ptr),
                    reads=[ptr], writes=[TMg[:, half * 512:(half + 1) * 512]],
                    tag=f"{uid}tgv{ch}.{half}")
            for half in range(2):
                ptr = pTR[half]
                for q4 in range(4):
                    pr = 8 + half * 4 + q4
                    S.add("pe", lambda e, pr=pr, q4=q4, ptr=ptr: e.transpose(
                        ptr[:, q4 * 128:(q4 + 1) * 128], grad[:, pr * 16:(pr + 1) * 16], ceye),
                        reads=[grad[:, pr * 16:(pr + 1) * 16], ceye], writes=[ptr],
                        tag=f"{uid}tg2{ch}.{pr}")
                S.add("ve", lambda e, half=half, ptr=ptr: e.tensor_copy(
                    TMg[:, 1024 + half * 512:1024 + (half + 1) * 512], ptr),
                    reads=[ptr], writes=[TMg[:, 1024 + half * 512:1024 + (half + 1) * 512]],
                    tag=f"{uid}tgv2{ch}.{half}")
            if "s" not in AT1_SECTS:
                continue
            if AT1_LEVEL < 1:
                for pr in range(16):
                    for par in range(2):
                        h = pr * 2 + par
                        tmslc = TMg[:, pr * 128 + par * 64: pr * 128 + (par + 1) * 64]
                        for ei, esrc in ((0, Etp[ch % 2]), (1, AEp[ch % 2])):
                            S.add("pe", lambda e, pr=pr, par=par, h=h, tmslc=tmslc,
                                  esrc=esrc, ei=ei: e.matmul(
                                pZ[par * 64:(par + 1) * 64, pr * 16:(pr + 1) * 16],
                                tmslc, esrc[:, h * 16:(h + 1) * 16],
                                start=False, stop=(pr == 15 and par == 1 and ei == 1),
                                skip_group_check=True),
                                reads=[tmslc, esrc[:, h * 16:(h + 1) * 16]],
                                writes=[pZ], tag=f"{uid}eg{ch}.{h}.{ei}")
            S.add("ve", lambda e: e.tensor_tensor(zbe, pZ, b1s, op=ALU.add),
                  reads=[pZ, b1s], writes=[zbe], tag=f"{uid}zbb{ch}")
            for pr in range(16 if AT1_LEVEL < 1 else 0):
                b = pr // 8
                for par in range(2):
                    h = pr * 2 + par
                    cxslc = cxb[:, pr * 128 + par * 64: pr * 128 + (par + 1) * 64]
                    tmslc = TMg[:, pr * 128 + par * 64: pr * 128 + (par + 1) * 64]
                    dst = pWF[b][par * 64:(par + 1) * 64, (pr % 8) * 64:(pr % 8 + 1) * 64]
                    S.add("pe", lambda e, cxslc=cxslc, tmslc=tmslc, dst=dst: e.matmul(
                        dst, cxslc, tmslc, start=True, stop=True),
                        reads=[cxslc, tmslc], writes=[pWF[b]], tag=f"{uid}wu{ch}.{h}")
            if AT1_LEVEL < 1:
                w1v = w1s.rearrange("p (pr o) -> p pr o", o=128)
                for b in range(2):
                    psv = pWF[b].rearrange("p (pr o) -> p pr o", o=64)
                    for par in range(2):
                        wslc = w1v[par * 64:(par + 1) * 64, b * 8:(b + 1) * 8,
                                   par * 64:(par + 1) * 64]
                        pslc = psv[par * 64:(par + 1) * 64, :, :]
                        S.add("ve", lambda e, wslc=wslc, pslc=pslc: e.tensor_tensor(
                            wslc, wslc, pslc, op=ALU.subtract),
                            reads=[wslc, pWF[b]], writes=[wslc],
                            tag=f"{uid}wus{ch}.{b}.{par}")
            if "b" not in AT1_SECTS:
                continue
            S.add("ve", lambda e: e.reduce_sum(
                red3, cog.rearrange("p (pr k) -> p pr k", k=16), axis=AX.X),
                reads=[cog], writes=[red3], tag=f"{uid}b1r{ch}")

            def tail_b1(ch=ch):
                for k in range(16):
                    b1v = b1s.rearrange("p (pr k) -> p k pr", k=16)[:, k, :]
                    S.add("ve", lambda e, b1v=b1v: e.tensor_tensor(b1v, b1v, red3, op=ALU.subtract),
                          reads=[b1v, red3], writes=[b1v], tag=f"{uid}b1u{ch}.{k}")
            pend[0] = tail_b1
            if "f" not in AT1_SECTS:
                continue

            def tail_fwd(ch=ch, cc=cc, XC=XC):
                bn_stats(zbe, f"{uid}bf{ch}")
                pmu2 = bcast(stat2[:, 0:256], f"{uid}bmu2{ch}")
                prs2 = bcast(stat[:, 256:512], f"{uid}brs2{ch}")
                S.add("ve", lambda e, pmu2=pmu2: e.tensor_tensor(t1s, zbe, pmu2, op=ALU.subtract),
                      reads=[zbe, pmu2], writes=[t1s], tag=f"{uid}f1{ch}")
                S.add("ve", lambda e, prs2=prs2: e.tensor_tensor(t2s, t1s, prs2, op=ALU.mult),
                      reads=[t1s, prs2], writes=[t2s], tag=f"{uid}f2{ch}")
                S.add("ve", lambda e: e.tensor_tensor(t1s, t2s, gfm, op=ALU.mult),
                      reads=[t2s, gfm], writes=[t1s], tag=f"{uid}f3{ch}")
                S.add("ve", lambda e: e.tensor_tensor(t2s, t1s, bfm, op=ALU.add),
                      reads=[t1s, bfm], writes=[t2s], tag=f"{uid}f4{ch}")
                S.add("ve", lambda e, XC=XC: e.tensor_tensor(
                    outf.rearrange("p (pr k) -> p pr k", k=16),
                    t2s.rearrange("p (pr k) -> p pr k", k=16), XC, op=ALU.add),
                    reads=[t2s, XC], writes=[outf], tag=f"{uid}fo{ch}")
                xw3 = xcwb.rearrange("p (pr t) -> p pr t", t=128)[:, :, cc * 16:(cc + 1) * 16]
                S.add("ve", lambda e, xw3=xw3: e.tensor_copy(
                    xw3, outf.rearrange("p (pr k) -> p pr k", k=16)),
                    reads=[outf], writes=[xw3], tag=f"{uid}xw{ch}")
            pend[1] = tail_fwd
        for pi in range(2):
            if pend[pi] is not None:
                pend[pi]()
                pend[pi] = None
        S.add("sc", lambda e, mg=mg: e.dma_start(
            out=sp3["x"][:, :, mg * 128:(mg + 1) * 128],
            in_=xcwb.rearrange("p (pr t) -> p pr t", t=128)),
            reads=[xcwb], writes=[sp3["x"][:, :, mg * 128:(mg + 1) * 128]],
            dma=f"{uid}xws", tag=f"{uid}xws{mg}")

    for g in range(NG):
        S.add("sy", lambda e, g=g: e.dma_start(
            out=xg[:, 0:16 * TG].rearrange("p (c t) -> p c t", t=TG),
            in_=x_in3[:, :, g * TG:(g + 1) * TG]),
            reads=[x_in3[:, :, g * TG:(g + 1) * TG]], writes=[xg[:, 0:16 * TG]],
            dma=f"{uid}xg", tag=f"{uid}ldx2{g}")
        S.add("sy", lambda e, g=g: e.dma_start(
            out=xn[:, 0:16 * TG].rearrange("p (c t) -> p c t", t=TG),
            in_=sp3["x"][:, :, g * TG:(g + 1) * TG]),
            reads=[sp3["x"][:, :, g * TG:(g + 1) * TG]], writes=[xn[:, 0:16 * TG]],
            dma=f"{uid}xcwl", tag=f"{uid}ldxc{g}")
        wm = W["wom1"]
        for o in range(16):
            b = o % 2
            wb = t[f"wgb{b}"].ap()
            S.add("sy", lambda e, wb=wb, o=o, wm=wm: e.dma_start(out=wb, in_=wm[o]),
                  reads=[wm[o]], writes=[wb], dma=f"{uid}w{b}", tag=f"{uid}ldwo{g}.{o}")
            for c in range(16):
                xnc = xn[:, c * TG:(c + 1) * TG]
                ws = wb[:, c * 128:(c + 1) * 128]
                S.add("pe", lambda e, ws=ws, xnc=xnc, c=c, b=b: e.matmul(
                    p_pr[b], ws, xnc, start=(c == 0), stop=(c == 15)),
                    reads=[ws, xnc], writes=[p_pr[b]], tag=f"{uid}om{g}.{o}.{c}")
            oslot = o % 2
            outr = t[f"outr{oslot}"].ap()[:, 0:TG]
            tsc = t["tscr"].ap()[:, 0:TG]
            S.add("ve", lambda e, b=b, tsc=tsc: e.tensor_copy(tsc, p_pr[b]),
                  reads=[p_pr[b]], writes=[tsc], tag=f"{uid}oev{g}.{o}")
            S.add("ve", lambda e, o=o, outr=outr, tsc=tsc: e.tensor_tensor(
                outr, tsc, xg[:, o * TG:(o + 1) * TG], op=ALU.add),
                reads=[tsc, xg[:, o * TG:(o + 1) * TG]], writes=[outr],
                tag=f"{uid}ores{g}.{o}")
            S.add("sc", lambda e, o=o, g=g, outr=outr, oslot=oslot: e.dma_start(
                out=x_out3[:, o, g * TG:(g + 1) * TG], in_=outr),
                reads=[outr], writes=[x_out3[:, o, g * TG:(g + 1) * TG]],
                dma=f"{uid}st{oslot}", tag=f"{uid}ostx{g}.{o}")


def pack_attn1_weights(Wq, Wk, Wv, Wo, lw, lb, ln_w, ln_b, W1i, b1i):
    d = {}
    d["wqm1"], d["wkm1"], d["wvm1"], d["wom1"] = (pack_proj(w) for w in (Wq, Wk, Wv, Wo))
    lwm = np.asarray(lw)[:, 0, :]
    d["lwt"] = np.ascontiguousarray(lwm.T.reshape(16, 128, 32).transpose(1, 0, 2)).reshape(128, 512)
    d["lbT"] = np.ascontiguousarray(np.asarray(lb).reshape(1, 32))
    gfm = np.zeros((128, 256), np.float32)
    bfm = np.zeros((128, 256), np.float32)
    b1fm = np.zeros((128, 256), np.float32)
    for par in range(2):
        for pr in range(16):
            h = 2 * pr + par
            gfm[par * 64:(par + 1) * 64, pr * 16:(pr + 1) * 16] = np.asarray(ln_w)[h][:, None]
            bfm[par * 64:(par + 1) * 64, pr * 16:(pr + 1) * 16] = np.asarray(ln_b)[h][:, None]
            b1fm[par * 64:(par + 1) * 64, pr * 16:(pr + 1) * 16] = np.asarray(b1i)[h, 0][:, None]
    d["gfm"], d["bfm"], d["b1fm"] = gfm, bfm, b1fm
    w1bd = np.zeros((16, 128, 128), np.float32)
    for pr in range(16):
        for par in range(2):
            w1bd[pr, par * 64:(par + 1) * 64, par * 64:(par + 1) * 64] = np.asarray(W1i)[2 * pr + par]
    d["w1bd"] = w1bd
    a = (1.0 / (64.0 * np.arange(1, 17, dtype=np.float64))).astype(np.float32)
    maskT = np.triu(np.ones((16, 16), np.float32))
    d["mkTnA"] = np.ascontiguousarray(np.tile(-(a[:, None]) * maskT, (1, 32)))
    d["csel"] = np.ascontiguousarray(np.kron(np.eye(32, dtype=np.float32), np.ones((1, 16), np.float32)))
    hind = np.zeros((2, 128), np.float32)
    hind[0, 0:64] = 1.0
    hind[1, 64:128] = 1.0
    d["hind"] = hind
    d["o2bd"] = np.ascontiguousarray(hind.T)
    d["a2c"] = np.ascontiguousarray(np.tile(a, (2, 16)).reshape(2, 256))
    d["aTMc"] = np.ascontiguousarray(a.reshape(16, 1))
    return d


def declare_attn1_weights(nc, prefix=""):
    d = {}
    shapes = dict(wqm1=[16, 128, 2048], wkm1=[16, 128, 2048], wvm1=[16, 128, 2048],
                  wom1=[16, 128, 2048], lwt=[128, 512], lbT=[1, 32],
                  gfm=[128, 256], bfm=[128, 256], b1fm=[128, 256],
                  w1bd=[16, 128, 128], mkTnA=[16, 512], csel=[32, 512],
                  hind=[2, 128], o2bd=[128, 2], a2c=[2, 256], aTMc=[16, 1])
    for n, s in shapes.items():
        d[n] = nc.dram_tensor(prefix + n + "_d", s, F32, kind="ExternalInput").ap()
    return d


def build_attn1_program():
    nc = bass.Bass("TRN2", target_bir_lowering=False)
    Lx = AT1_NG * AT1_TG
    x_in = nc.dram_tensor("x_in", [D, Lx], F32, kind="ExternalInput").ap()
    x_out = nc.dram_tensor("x_out", [D, Lx], F32, kind="ExternalOutput").ap()
    qs = nc.dram_tensor("q_s", [D, Lx], F32, kind="Internal").ap()
    ks = nc.dram_tensor("k_s", [D, Lx], F32, kind="Internal").ap()
    vs = nc.dram_tensor("v_s", [D, Lx], F32, kind="Internal").ap()
    xcws = nc.dram_tensor("xcw_s", [D, Lx], F32, kind="Internal").ap()
    sd = nc.dram_tensor("sd_s", [32, Lx], F32, kind="Internal").ap()
    sdT = nc.dram_tensor("sdT_s", [Lx, 32], F32, kind="Internal").ap()
    W = declare_attn1_weights(nc)
    W["sd_ap"] = sd
    W["sdT_ap"] = sdT
    if AT1_DBG:
        W["dbg"] = {}
        for dnm, shp in (("dz1e", [128, 256]), ("dgrad", [128, 256]), ("dcog", [128, 256]),
                         ("dzsb", [128, 256]), ("dzbe", [128, 256]), ("dxh", [128, 256]),
                         ("dg0", [128, 256]), ("dEt", [16, 512]), ("dAEt", [16, 512]),
                         ("dAT", [16, 512]), ("dTMg", [16, 2048]), ("dcxb", [16, 2048])):
            W["dbg"][dnm] = nc.dram_tensor(dnm, shp, F32, kind="Internal").ap()
    cones_d = nc.dram_tensor("cones_d", [128, 128], F32, kind="ExternalInput").ap()
    ceye_d = nc.dram_tensor("ceye_d", [128, 128], F32, kind="ExternalInput").ap()
    ctx = ExitStack()
    with ctx:
        A = Arena(nc, ctx)
        A.sb("cones", [128, 128])
        A.sb("ceye", [128, 128])
        alloc_shared(A)
        alloc_attn1(A)
        P = [A.ps(f"pb{i}", [128, 512]) for i in range(8)]
        S = Sched(nc, psum_names=[f"pb{i}" for i in range(8)])
        S.add("sy", lambda e: e.dma_start(out=A.t["cones"].ap(), in_=cones_d),
              writes=[A.t["cones"].ap()], dma="cones")
        S.add("sy", lambda e: e.dma_start(out=A.t["ceye"].ap(), in_=ceye_d),
              writes=[A.t["ceye"].ap()], dma="ceye")
        emit_consts(S, A)
        emit_attn1(S, A, P, W, x_in, x_out, qs, ks, vs, xcws, sd, sdT, uid="t")
        n = S.finalize(ctx)
    return nc, n


def host_pack_x(x_td):
    return np.ascontiguousarray(np.asarray(x_td).T)


def build_full_program():
    nc = bass.Bass("TRN2", target_bir_lowering=False)
    x_in = nc.dram_tensor("x_in", [D, L], F32, kind="ExternalInput").ap()
    x_out = nc.dram_tensor("x_out", [D, L], F32, kind="ExternalOutput").ap()
    xa = nc.dram_tensor("xa_s", [D, L], F32, kind="Internal").ap()
    xb = nc.dram_tensor("xb_s", [D, L], F32, kind="Internal").ap()
    qs = nc.dram_tensor("q_s", [D, L], F32, kind="Internal").ap()
    ks = nc.dram_tensor("k_s", [D, L], F32, kind="Internal").ap()
    vs = nc.dram_tensor("v_s", [D, L], F32, kind="Internal").ap()
    os_ = nc.dram_tensor("o_s", [D, L], F32, kind="Internal").ap()
    sd = nc.dram_tensor("sd_s", [32, L], F32, kind="Internal").ap()
    sdT = nc.dram_tensor("sdT_s", [L, 32], F32, kind="Internal").ap()
    Wf = declare_ffn0_weights(nc)
    Wa = declare_attn0_weights(nc)
    Wm = declare_moe1_weights(nc)
    Wt = declare_attn1_weights(nc)
    Wt["sd_ap"] = sd
    Wt["sdT_ap"] = sdT
    cones_d = nc.dram_tensor("cones_d", [128, 128], F32, kind="ExternalInput").ap()
    conesb_d = nc.dram_tensor("conesb_d", [128, 128], BF16, kind="ExternalInput").ap()
    ceye_d = nc.dram_tensor("ceye_d", [128, 128], F32, kind="ExternalInput").ap()
    ctx = ExitStack()
    with ctx:
        A = Arena(nc, ctx)
        A.sb("cones", [128, 128])
        A.sb("ceye", [128, 128])
        alloc_shared(A)
        alloc_ffn0(A)
        alloc_moe1(A)
        alloc_attn1(A)
        P = [A.ps(f"pb{i}", [128, 512]) for i in range(8)]
        S = Sched(nc, psum_names=[f"pb{i}" for i in range(8)])
        S.add("sy", lambda e: e.dma_start(out=A.t["cones"].ap(), in_=cones_d),
              writes=[A.t["cones"].ap()], dma="cones")
        S.add("sy", lambda e: e.dma_start(out=A.t["conesb"].ap(), in_=conesb_d),
              writes=[A.t["conesb"].ap()], dma="conesb")
        S.add("sy", lambda e: e.dma_start(out=A.t["ceye"].ap(), in_=ceye_d),
              writes=[A.t["ceye"].ap()], dma="ceye")
        emit_consts(S, A)
        emit_ffn0(S, A, P, Wf, x_in, xa, uid="f")
        emit_attn0(S, A, P, Wa, xa, xb, qs, ks, vs, os_, uid="a")
        emit_ffn0(S, A, P, Wf, xb, xa, uid="f")
        emit_moe1(S, A, P, Wm, xa, xb, uid="m")
        emit_attn1(S, A, P, Wt, xb, xa, qs, ks, vs, os_, sd, sdT, uid="t")
        emit_moe1(S, A, P, Wm, xa, x_out, uid="m")
        n = S.finalize(ctx)
    return nc, n


_PROGS = {}


def _get_progs():
    if not _PROGS:
        _PROGS["ffn0"] = build_ffn0_program()
        _PROGS["attn0"] = build_attn0_program()
        _PROGS["moe1"] = build_moe1_program()
        _PROGS["attn1"] = build_attn1_program()
    return _PROGS


def _pack_all(z):
    import ml_dtypes
    cones = np.ones((128, 128), np.float32)
    conesb = np.ones((128, 128), dtype=ml_dtypes.bfloat16)
    ceye = np.eye(128, dtype=np.float32)
    Wf = pack_ffn0_weights(z["L0_ffn_gate_w"], z["L0_ffn_gate_scale"],
                           z["L0_ffn_up_w"], z["L0_ffn_up_scale"],
                           z["L0_ffn_down_w"], z["L0_ffn_down_scale"])
    Wa = pack_attn0_weights(z["L0_attn_Wq"], z["L0_attn_bq"], z["L0_attn_Bq"], z["L0_attn_Aq"],
                            z["L0_attn_Wk"], z["L0_attn_bk"], z["L0_attn_Bk"], z["L0_attn_Ak"],
                            z["L0_attn_Wv"], z["L0_attn_bv"], z["L0_attn_Bv"], z["L0_attn_Av"],
                            z["L0_attn_Wo"], z["L0_attn_bo"], z["L0_attn_Bo"], z["L0_attn_Ao"])
    Wm = pack_moe1_weights(z["L1_ffn_Wr"], z["L1_ffn_We"], z["L1_ffn_be"])
    Wt = pack_attn1_weights(z["L1_attn_Wq"], z["L1_attn_Wk"], z["L1_attn_Wv"], z["L1_attn_Wo"],
                            z["L1_attn_linear_weight"], z["L1_attn_linear_bias"],
                            z["L1_attn_ln_weight"], z["L1_attn_ln_bias"],
                            z["L1_attn_W1_init"], z["L1_attn_b1_init"])
    base = dict(cones_d=cones, conesb_d=conesb, ceye_d=ceye, **Wf)
    base.update(wqm=Wa["wqm"], wkm=Wa["wkm"], wvm=Wa["wvm"], wom=Wa["wom"],
                lb4_d=Wa["lb4"], aqT4_d=Wa["aqT4"], bb_d=Wa["bb"])
    base.update(**Wm)
    base.update({k + "_d": v for k, v in Wt.items()})
    return base


def kernel(**inputs):
    if "full" not in _PROGS:
        _PROGS["full"] = build_full_program()
    nc, _ = _PROGS["full"]
    base = _pack_all(inputs)
    x = [host_pack_x(np.asarray(inputs["L0_attn_x"][b], np.float32)) for b in range(NCORES)]
    maps = []
    for b in range(NCORES):
        m = dict(base)
        m["x_in"] = x[b]
        maps.append(m)
    res = run_bass_kernel_spmd(nc, maps, core_ids=list(range(NCORES)))
    out = np.stack([np.ascontiguousarray(np.asarray(r["x_out"]).T) for r in res.results])
    return out.astype(np.float32)


def kernel_unfused(**inputs):
    progs = _get_progs()
    z = inputs
    cones = np.ones((128, 128), np.float32)
    ceye = np.eye(128, dtype=np.float32)
    Wf = pack_ffn0_weights(z["L0_ffn_gate_w"], z["L0_ffn_gate_scale"],
                           z["L0_ffn_up_w"], z["L0_ffn_up_scale"],
                           z["L0_ffn_down_w"], z["L0_ffn_down_scale"])
    Wa = pack_attn0_weights(z["L0_attn_Wq"], z["L0_attn_bq"], z["L0_attn_Bq"], z["L0_attn_Aq"],
                            z["L0_attn_Wk"], z["L0_attn_bk"], z["L0_attn_Bk"], z["L0_attn_Ak"],
                            z["L0_attn_Wv"], z["L0_attn_bv"], z["L0_attn_Bv"], z["L0_attn_Av"],
                            z["L0_attn_Wo"], z["L0_attn_bo"], z["L0_attn_Bo"], z["L0_attn_Ao"])
    Wm = pack_moe1_weights(z["L1_ffn_Wr"], z["L1_ffn_We"], z["L1_ffn_be"])
    Wt = pack_attn1_weights(z["L1_attn_Wq"], z["L1_attn_Wk"], z["L1_attn_Wv"], z["L1_attn_Wo"],
                            z["L1_attn_linear_weight"], z["L1_attn_linear_bias"],
                            z["L1_attn_ln_weight"], z["L1_attn_ln_bias"],
                            z["L1_attn_W1_init"], z["L1_attn_b1_init"])
    x = [host_pack_x(np.asarray(z["L0_attn_x"][b], np.float32)) for b in range(NCORES)]

    def launch(key, extra, xs):
        nc, _ = progs[key]
        maps = []
        for b in range(NCORES):
            m = dict(extra)
            m["x_in"] = xs[b]
            maps.append(m)
        res = run_bass_kernel_spmd(nc, maps, core_ids=list(range(NCORES)))
        return [np.asarray(r["x_out"]) for r in res.results]

    ef = dict(cones_d=cones, **Wf)
    ea = dict(cones_d=cones, ceye_d=ceye,
              wqm=Wa["wqm"], wkm=Wa["wkm"], wvm=Wa["wvm"], wom=Wa["wom"],
              lb4_d=Wa["lb4"], aqT4_d=Wa["aqT4"], bb_d=Wa["bb"])
    em = dict(cones_d=cones, ceye_d=ceye, **Wm)
    et = {k + "_d": v for k, v in Wt.items()}
    et.update(cones_d=cones, ceye_d=ceye)

    x = launch("ffn0", ef, x)
    x = launch("attn0", ea, x)
    x = launch("ffn0", ef, x)
    x = launch("moe1", em, x)
    x = launch("attn1", et, x)
    x = launch("moe1", em, x)
    out = np.stack([np.ascontiguousarray(xb.T) for xb in x])
    return out.astype(np.float32)
